# version 04
# baseline (speedup 1.0000x reference)
import contextlib
import numpy as np
import ml_dtypes
import concourse.bass as bass
import concourse.mybir as mybir
from concourse.bass_utils import run_bass_kernel_spmd

F32 = mybir.dt.float32
BF16 = mybir.dt.bfloat16
I8 = mybir.dt.int8
AF = mybir.ActivationFunctionType
ALU = mybir.AluOpType
AX = mybir.AxisListType.X

NCORES = 8
TLOC = 1024
E = 2048
NCH = 16
BF = ml_dtypes.bfloat16



def _prep(I):
    C = lambda a: np.ascontiguousarray(a)
    pr = {"shared": {}, "core": [dict() for _ in range(NCORES)]}
    sh = pr["shared"]
    x = np.asarray(I["L0_attn_x"])
    for c in range(NCORES):
        b, r = c // 4, c % 4
        pr["core"][c]["x0f"] = C(x[b, r::4, :].T)

    sh["Wattn"] = C(np.asarray(I["L0_attn_W_attn"]))
    sh["Wproj"] = C(np.asarray(I["L0_attn_W_proj"]))
    sh["w1T"] = C(np.asarray(I["L0_ffn_w1"]).T)
    sh["w2T"] = C(np.asarray(I["L0_ffn_w2"]).T)
    sh["b1row"] = C(np.asarray(I["L0_ffn_b1"])[None, :])
    sh["lngrow"] = C(np.asarray(I["L0_ffn_ln_g"])[None, :])
    sh["lnbrow"] = C(np.asarray(I["L0_ffn_ln_b"])[None, :])
    sh["b2row"] = C(np.asarray(I["L0_ffn_b2"])[None, :])
    for nm in ("lambda_q1", "lambda_k1", "lambda_q2", "lambda_k2"):
        sh["l0" + nm] = C(np.asarray(I["L0_attn_" + nm]).reshape(16, 128).T)

    sh["wqkvT"] = C(np.asarray(I["L1_attn_w_qkv"]).T)
    sh["woT"] = C(np.asarray(I["L1_attn_wo"]).T)
    for nm in ("lambda_q1", "lambda_q2", "lambda_k1", "lambda_k2"):
        sh["l1" + nm] = C(np.asarray(I["L1_attn_" + nm])[None, :])
    gam = np.asarray(I["L1_attn_gamma"]); bet = np.asarray(I["L1_attn_beta"])
    for c in range(NCORES):
        r = c % 4
        pr["core"][c]["gamrow"] = C(gam[(r // 2)::2][None, :1024])
        pr["core"][c]["betrow"] = C(bet[(r // 2)::2][None, :1024])

    sh["WqT"] = C(np.asarray(I["L1_ffn_Wq"]).T)
    sh["WkT"] = C(np.asarray(I["L1_ffn_Wk"]).T)
    sh["WvT"] = C(np.asarray(I["L1_ffn_Wv"]).T)
    sh["WoT2"] = C(np.asarray(I["L1_ffn_Wo"]).T)
    sh["Amat"] = C(np.asarray(I["L1_ffn_A"]))
    sh["Bmat"] = C(np.asarray(I["L1_ffn_B"]))
    sh["WinT"] = C(np.asarray(I["L1_ffn_W_in_base"]).T)
    sh["WupT"] = C(np.asarray(I["L1_ffn_W_up_base"]).T)
    sh["WdnT"] = C(np.asarray(I["L1_ffn_W_down"]).T)
    sh["ln1w"] = C(np.asarray(I["L1_ffn_ln1_w"]).reshape(16, 128).T)
    sh["ln2w"] = C(np.asarray(I["L1_ffn_ln2_w"]).reshape(16, 128).T)

    sh["identf"] = np.eye(128, dtype=np.float32)
    sh["identb"] = np.eye(128, dtype=BF)
    e16 = np.zeros((16, 1024), np.float32)
    for g in range(16):
        e16[g, g * 64:(g + 1) * 64] = 1.0
    sh["E16"] = e16

    hd = 128
    inv = 1.0 / (10000.0 ** (np.arange(0, hd, 2, dtype=np.float64) / hd))
    invfull = np.concatenate([inv, inv])
    for c in range(NCORES):
        r = c % 4
        pos = 4.0 * np.arange(1024, dtype=np.float64) + r
        ang = invfull[:, None] * pos[None, :]
        pr["core"][c]["cosfm"] = np.cos(ang).astype(np.float32)
        pr["core"][c]["sinfm"] = np.sin(ang).astype(np.float32)

    for c in range(NCORES):
        r = c % 4
        m_f = np.zeros((512, 128), np.float32)
        m_a = np.zeros((512, 128), np.float32)
        ki = np.arange(128)[None, :]
        qi = np.arange(128)[:, None]
        for rp in range(4):
            gt = (ki > qi) | ((ki == qi) & (rp > r))
            m_f[rp * 128:(rp + 1) * 128] = np.where(gt, -1e9, 0.0)
            m_a[rp * 128:(rp + 1) * 128] = np.where(~gt, -1.0, 0.0)
        pr["core"][c]["maskF"] = m_f
        pr["core"][c]["maskA"] = m_a
    return pr



class Prog:
    def __init__(self):
        self.nc = bass.Bass("TRN2", target_bir_lowering=False)
        self.ctx = contextlib.ExitStack()
        self._sems = {}
        self.cnt = {}
        self.name_map = {}
        self.base = {}
        self.next_slot = 0
        self.scopes = [self.ctx]

    NPOOL = 100

    def new_stage(self):
        self.name_map = {}
        self.base = {}

    def push_scope(self):
        s = contextlib.ExitStack()
        self.scopes.append(s)
        return s

    def pop_scope(self):
        self.scopes.pop().close()

    def _slot(self, name):
        if name not in self.name_map:
            sl = self.next_slot % self.NPOOL
            self.name_map[name] = sl
            self.next_slot += 1
            if sl not in self._sems:
                self._sems[sl] = self.ctx.enter_context(
                    self.nc.semaphore(f"S{sl}"))
                self.cnt[sl] = 0
            self.base[name] = self.cnt[sl]
        return self.name_map[name]

    def sem(self, name):
        return self._sems[self._slot(name)]

    def barrier(self, block):
        snap = [(self._sems[sl], self.cnt[sl]) for sl in sorted(self._sems)]

        def mk(eng):
            for s, v in snap:
                if v > 0:
                    eng.wait_ge(s, v)
        block.tensor(mk)
        block.vector(mk)
        block.scalar(mk)
        block.gpsimd(mk)
        block.sync(mk)

    def inc(self, inst, name, n=1):
        s = self.sem(name)
        sl = self.name_map[name]
        inst.then_inc(s, n)
        self.cnt[sl] += n
        return self.cnt[sl] - self.base[name]

    def wait(self, eng, name, val=None):
        s = self.sem(name)
        sl = self.name_map[name]
        eng.wait_ge(s, self.cnt[sl] if val is None
                    else self.base[name] + val)

    def wge(self, eng, name, val):
        s = self.sem(name)
        eng.wait_ge(s, self.base[name] + val)

    def c(self, name):
        self.sem(name)
        return self.cnt[self.name_map[name]] - self.base[name]

    _uid = [0]

    def un(self, name):
        Prog._uid[0] += 1
        return f"{name}_u{Prog._uid[0]}"

    def sb(self, name, shape, dt):
        Prog._uid[0] += 1
        return self.scopes[-1].enter_context(
            self.nc.sbuf_tensor(f"{name}_u{Prog._uid[0]}", list(shape), dt))

    def din(self, name, shape, dt=F32):
        return self.nc.dram_tensor(name, list(shape), dt, kind="ExternalInput").ap()

    def dout(self, name, shape, dt=F32):
        return self.nc.dram_tensor(name, list(shape), dt, kind="ExternalOutput").ap()

    def dram(self, name, shape, dt=F32):
        return self.nc.dram_tensor(name, list(shape), dt)


def emit_weight_stream(p, block, name, src_ap, blocks, dst_tiles, tile_free_sem=None,
                       tile_depth=None, pre_wait=None):
    ld_s = name + "_ld"

    def ld(g):
        if pre_wait is not None:
            pre_wait(g)
        for k, (a, b, c, d) in enumerate(blocks):
            if tile_free_sem is not None and k >= tile_depth:
                p.wge(g, tile_free_sem, k - tile_depth + 1)
            i = g.dma_start(out=dst_tiles(k), in_=src_ap[a:a + b, c:c + d])
            p.inc(i, ld_s, 16)
    block.gpsimd(ld)
    return ld_s



def emit_l0pre(p, block, io):
    nc = p.nc
    x0f = io["x0f"]
    Wattn = io["Wattn"]
    lam_in = [io["l0lambda_" + nm] for nm in ("q1", "k1", "q2", "k2")]
    identf = io["identf"]
    o_xnt = io["xnt"]
    o_qp = io["qp"]
    o_kp = io["kp"]
    o_vt = io["vt"]

    xw = p.sb("xw", [128, NCH * TLOC], F32)
    xbf = p.sb("xbf", [128, NCH * TLOC], BF16)
    sqs = [p.sb(f"sqs{i}", [128, TLOC], BF16) for i in range(2)]
    sid = p.sb("sid", [128, 128], F32)
    ones_cb = p.sb("ones_cb", [128, 1], BF16)
    ones_rf = p.sb("ones_rf", [1, 128], F32)
    onesf_c = p.sb("onesf_c", [128, 1], F32)
    rows = p.sb("rows", [1, 2 * TLOC], F32)
    mB = p.sb("mB", [128, TLOC], F32)
    rB = p.sb("rB", [128, TLOC], F32)
    lv = [p.sb(f"lv{i}", [128, 16], F32) for i in range(4)]
    ltmp = p.sb("ltmp", [128, 16], F32)
    lred = p.sb("lred", [128, 2], F32)
    lsc = p.sb("lsc", [1, 4], F32)
    neglam_c = p.sb("neglam_c", [128, 1], F32)
    xnt_sb = [p.sb(f"xnt{i}", [128, E], F32) for i in range(2)]
    wtil = [p.sb(f"wqk{i}", [128, 256], BF16) for i in range(2)]
    stage = [p.sb(f"qkstg{i}", [128, TLOC], F32) for i in range(2)]
    wvs = [p.sb(f"wvs{i}", [128, NCH * 512], BF16) for i in range(2)]
    vstage = [p.sb(f"vstg{i}", [128, 512], F32) for i in range(2)]

    SM = slice(0, TLOC)
    SA = slice(TLOC, 2 * TLOC)

    with \
         nc.psum_tensor(p.un("psA"), [128, TLOC], F32) as psA, \
         nc.psum_tensor(p.un("psB"), [128, TLOC], F32) as psB, \
         nc.psum_tensor(p.un("psC"), [128, TLOC], F32) as psC, \
         nc.psum_tensor(p.un("psD"), [128, TLOC], F32) as psD:

        sum_ps = psA[0:1, :]
        sq_ps = psB[0:1, :]
        dot_ps = psC[0:1, 0:2]
        nl_ps = psC[:, 512:513]
        bc_ps = psC
        tp_ps = [psD[:, 0:128], psD[:, 512:640]]
        qk_ps = [psA, psB]
        vv_ps = [psD[:, 0:512], psD[:, 512:1024]]

        def ld0(sync):
            i = sync.dma_start(out=xw[:].rearrange("p (j t) -> p j t", j=NCH),
                               in_=x0f.rearrange("(j p) t -> p j t", p=128))
            p.inc(i, "ld", 16)
            p.inc(sync.dma_start(out=sid[:], in_=identf), "ld", 16)
            for i_, t_ in enumerate(lam_in):
                p.inc(sync.dma_start(out=lv[i_][:], in_=t_), "ld", 16)
        block.sync(ld0)
        ld_done = p.c("ld")

        def v0(v):
            v.memset(ones_cb[:], 1.0)
            v.memset(ones_rf[:], 1.0)
            v.memset(onesf_c[:], 1.0)
            p.inc(v.drain(), "cst")
            p.wait(v, "ld", ld_done)
            p.inc(v.tensor_copy(out=xbf[:], in_=xw[:]), "xbf")
        block.vector(v0)

        def vlam(v):
            p.wait(v, "ld", ld_done)
            v.tensor_tensor(out=ltmp[:], in0=lv[0][:], in1=lv[1][:], op=ALU.mult)
            v.drain()
            v.reduce_sum(out=lred[:, 0:1], in_=ltmp[:], axis=AX)
            v.drain()
            v.tensor_tensor(out=ltmp[:], in0=lv[2][:], in1=lv[3][:], op=ALU.mult)
            v.drain()
            p.inc(v.reduce_sum(out=lred[:, 1:2], in_=ltmp[:], axis=AX), "lred")
        block.vector(vlam)

        def tlam(t):
            p.wait(t, "cst")
            p.wait(t, "lred")
            t.matmul(out=dot_ps[0:1, 0:1], lhsT=lred[:, 0:1], rhs=onesf_c[:],
                     start=True, stop=True)
            p.inc(t.matmul(out=dot_ps[0:1, 1:2], lhsT=lred[:, 1:2], rhs=onesf_c[:],
                           start=True, stop=True), "ldot")
        block.tensor(tlam)

        def slam(s):
            p.wait(s, "ldot")
            p.inc(s.activation(out=lsc[0:1, 0:2], in_=dot_ps[0:1, 0:2], func=AF.Exp),
                  "lexp")
        block.scalar(slam)

        def vlam2(v):
            p.wait(v, "lexp")
            v.tensor_tensor(out=lsc[0:1, 2:3], in0=lsc[0:1, 0:1], in1=lsc[0:1, 1:2],
                            op=ALU.subtract)
            v.drain()
            v.tensor_scalar(out=lsc[0:1, 2:3], in0=lsc[0:1, 2:3], scalar1=0.2,
                            scalar2=None, op0=ALU.add)
            v.drain()
            p.inc(v.tensor_scalar(out=lsc[0:1, 3:4], in0=lsc[0:1, 2:3], scalar1=-1.0,
                                  scalar2=None, op0=ALU.mult), "lam")
        block.vector(vlam2)

        def tnl(t):
            p.wait(t, "lam")
            p.inc(t.matmul(out=nl_ps, lhsT=ones_rf[:], rhs=lsc[0:1, 3:4],
                           start=True, stop=True), "nldot")
        block.tensor(tnl)

        def snl(s):
            p.wait(s, "nldot")
            p.inc(s.copy(out=neglam_c[:], in_=nl_ps), "nlc")
        block.scalar(snl)

        def ssq(s):
            p.wait(s, "xbf")
            for j in range(NCH):
                if j >= 2:
                    p.wge(s, "squse", j - 1)
                i = s.activation(out=sqs[j % 2][:],
                                 in_=xbf[:, j * TLOC:(j + 1) * TLOC], func=AF.Square)
                p.inc(i, "x2")
        block.scalar(ssq)

        def tstats(t):
            p.wait(t, "cst")
            p.wait(t, "xbf")
            for tc in range(2):
                i = None
                for j in range(NCH):
                    i = t.matmul(
                        out=sum_ps[0:1, tc * 512:(tc + 1) * 512], lhsT=ones_cb[:],
                        rhs=xbf[:, j * TLOC + tc * 512:j * TLOC + (tc + 1) * 512],
                        start=(j == 0), stop=(j == NCH - 1))
                p.inc(i, "stat")
            for j in range(NCH):
                p.wge(t, "x2", j + 1)
                i = None
                for tc in range(2):
                    i = t.matmul(
                        out=sq_ps[0:1, tc * 512:(tc + 1) * 512], lhsT=ones_cb[:],
                        rhs=sqs[j % 2][:, tc * 512:(tc + 1) * 512],
                        start=(j == 0), stop=(j == NCH - 1), skip_group_check=True)
                p.inc(i, "squse")
            p.inc(t.matmul(out=sq_ps[0:1, 0:1], lhsT=ones_cb[0:1, :],
                           rhs=sqs[(NCH - 1) % 2][0:1, 0:1], start=False, stop=True,
                           skip_group_check=True), "stat") if False else None
        block.tensor(tstats)

        def srows(s):
            p.wait(s, "stat")
            p.wge(s, "squse", NCH)
            s.activation(out=rows[0:1, SM], in_=sum_ps, func=AF.Identity,
                         scale=1.0 / E)
            p.inc(s.activation(out=rows[0:1, SA], in_=sq_ps, func=AF.Identity,
                               scale=1.0 / E), "mrow")
        block.scalar(srows)

        def vvar(v):
            p.wait(v, "mrow")
            v.tensor_tensor(out=rows[0:1, SA], in0=rows[0:1, SA], in1=rows[0:1, SM],
                            op=ALU.subtract) if False else None
            v.tensor_tensor(out=ltmp[0:1, 0:16], in0=ltmp[0:1, 0:16],
                            in1=ltmp[0:1, 0:16], op=ALU.mult) if False else None
            v.tensor_tensor(out=mB[0:1, :], in0=rows[0:1, SM], in1=rows[0:1, SM],
                            op=ALU.mult)
            v.drain()
            v.tensor_tensor(out=rows[0:1, SA], in0=rows[0:1, SA],
                            in1=mB[0:1, :], op=ALU.subtract)
            v.drain()
            p.inc(v.tensor_scalar(out=rows[0:1, SA], in0=rows[0:1, SA],
                                  scalar1=1e-5, scalar2=None, op0=ALU.add), "var")
        block.vector(vvar)

        def sstd(s):
            p.wait(s, "var")
            p.inc(s.activation(out=rows[0:1, SA], in_=rows[0:1, SA], func=AF.Sqrt),
                  "std")
        block.scalar(sstd)

        def vrstd(v):
            p.wait(v, "std")
            p.inc(v.reciprocal(out=rows[0:1, SA], in_=rows[0:1, SA]), "rstd")
        block.vector(vrstd)

        def tbcm(t):
            p.wait(t, "rstd")
            p.wait(t, "nlc")
            i = None
            for tc in range(2):
                i = t.matmul(out=bc_ps[:, tc * 512:(tc + 1) * 512], lhsT=ones_rf[:],
                             rhs=rows[0:1, tc * 512:(tc + 1) * 512],
                             start=True, stop=True)
            p.inc(i, "bcm")
        block.tensor(tbcm)

        def sevm(s):
            p.wait(s, "bcm")
            p.inc(s.copy(out=mB[:], in_=bc_ps[:]), "mB")
        block.scalar(sevm)

        def tbcr(t):
            p.wait(t, "mB")
            i = None
            for tc in range(2):
                i = t.matmul(out=bc_ps[:, tc * 512:(tc + 1) * 512], lhsT=ones_rf[:],
                             rhs=rows[0:1, TLOC + tc * 512:TLOC + (tc + 1) * 512],
                             start=True, stop=True)
            p.inc(i, "bcr")
        block.tensor(tbcr)

        def sevr(s):
            p.wait(s, "bcr")
            p.inc(s.copy(out=rB[:], in_=bc_ps[:]), "rB")
        block.scalar(sevr)

        def vxn(v):
            p.wait(v, "rB")
            for j in range(NCH):
                sl = slice(j * TLOC, (j + 1) * TLOC)
                v.tensor_tensor(out=xw[:, sl], in0=xw[:, sl], in1=mB[:],
                                op=ALU.subtract)
                v.drain()
                v.tensor_tensor(out=xw[:, sl], in0=xw[:, sl], in1=rB[:], op=ALU.mult)
                v.drain()
            p.inc(v.tensor_copy(out=xbf[:], in_=xw[:]), "xnbf")
        block.vector(vxn)
        xn_done = p.c("xnbf")

        def ttp(t):
            p.wait(t, "xnbf", xn_done)
            for tt in range(8):
                for j in range(NCH):
                    k = tt * NCH + j
                    if k >= 2:
                        p.wge(t, "xte", k - 1)
                    i = t.transpose(
                        out=tp_ps[k % 2],
                        in_=xw[:, j * TLOC + tt * 128:j * TLOC + (tt + 1) * 128],
                        identity=sid[:])
                    p.inc(i, "xtp")
        block.tensor(ttp)

        def stp(s):
            for tt in range(8):
                if tt >= 2:
                    p.wge(s, "xts", (tt - 1) * 16)
                for j in range(NCH):
                    k = tt * NCH + j
                    p.wge(s, "xtp", k + 1)
                    i = s.copy(out=xnt_sb[tt % 2][:, j * 128:(j + 1) * 128],
                               in_=tp_ps[k % 2])
                    p.inc(i, "xte")
        block.scalar(stp)

        def sttp(sync):
            for tt in range(8):
                p.wge(sync, "xte", (tt + 1) * NCH)
                i = sync.dma_start(out=o_xnt[tt * 128:(tt + 1) * 128, :],
                                   in_=xnt_sb[tt % 2][:])
                p.inc(i, "xts", 16)
        block.sync(sttp)

        wblocks = [(j * 128, 128, ocg * 256, 256)
                   for ocg in range(16) for j in range(NCH)]
        ld_qk = emit_weight_stream(
            p, block, "wqk", Wattn, wblocks, lambda k: wtil[k % 2][:],
            tile_free_sem="wqkuse", tile_depth=2)

        def tqk(t):
            p.wait(t, "mrow")
            p.wait(t, "xnbf", xn_done)
            for ocg in range(16):
                if ocg >= 1:
                    p.wge(t, "qkevh", ocg * 2)
                for j in range(NCH):
                    k = ocg * NCH + j
                    p.wge(t, ld_qk, (k + 1) * 16)
                    last = None
                    for oo in range(2):
                        for tc in range(2):
                            last = t.matmul(
                                out=qk_ps[oo][:, tc * 512:(tc + 1) * 512],
                                lhsT=wtil[k % 2][:, oo * 128:(oo + 1) * 128],
                                rhs=xbf[:, j * TLOC + tc * 512:j * TLOC + (tc + 1) * 512],
                                start=(j == 0), stop=(j == NCH - 1),
                                skip_group_check=True)
                    p.inc(last, "wqkuse")
        block.tensor(tqk)

        def sqk(s):
            p.wait(s, "nlc")
            for ocg in range(16):
                p.wge(s, "wqkuse", (ocg + 1) * NCH)
                for oo in range(2):
                    oc = ocg * 2 + oo
                    if oc >= 2:
                        p.wge(s, "qkst", (oc - 1) * 32)
                    buf = stage[oc % 2]
                    if oc < 8 or (16 <= oc < 24):
                        s.activation(out=buf[0:64, :], in_=qk_ps[oo][0:64, :],
                                     func=AF.Identity, scale=0.125)
                        i = s.activation(out=buf[64:128, :],
                                         in_=qk_ps[oo][64:128, :],
                                         func=AF.Identity, scale=0.125)
                    elif oc < 16:
                        s.copy(out=buf[0:64, :], in_=qk_ps[oo][0:64, :])
                        i = s.copy(out=buf[64:128, :], in_=qk_ps[oo][64:128, :])
                    else:
                        s.activation(out=buf[0:64, :], in_=qk_ps[oo][0:64, :],
                                     func=AF.Identity, scale=neglam_c[0:64, :])
                        i = s.activation(out=buf[64:128, :],
                                         in_=qk_ps[oo][64:128, :],
                                         func=AF.Identity,
                                         scale=neglam_c[64:128, :])
                    p.inc(i, "qkevh")
        block.scalar(sqk)

        def stqk(sync):
            for oc in range(32):
                p.wge(sync, "qkevh", oc + 1)
                buf = stage[oc % 2]
                if oc < 8:
                    dst, h0, off = o_qp, 2 * oc, 0
                elif oc < 16:
                    dst, h0, off = o_kp, 2 * (oc - 8), 0
                elif oc < 24:
                    dst, h0, off = o_qp, 2 * (oc - 16), 64
                else:
                    dst, h0, off = o_kp, 2 * (oc - 24), 64
                i = sync.dma_start(out=dst[h0 * 128 + off:h0 * 128 + off + 64, :],
                                   in_=buf[0:64, :])
                p.inc(i, "qkst", 16)
                i = sync.dma_start(
                    out=dst[(h0 + 1) * 128 + off:(h0 + 1) * 128 + off + 64, :],
                    in_=buf[64:128, :])
                p.inc(i, "qkst", 16)
        block.sync(stqk)

        vblocks = [(j * 128, 128, 4096 + vg * 512, 512)
                   for vg in range(4) for j in range(NCH)]
        ld_wv = emit_weight_stream(
            p, block, "wv", Wattn, vblocks,
            lambda k: wvs[(k // NCH) % 2][:, (k % NCH) * 512:((k % NCH) + 1) * 512],
            tile_free_sem="wvuse", tile_depth=2 * NCH)

        def tv(t):
            p.wait(t, "xnbf", xn_done)
            p.wge(t, "xte", 8 * NCH)
            for vg in range(4):
                p.wge(t, ld_wv, (vg + 1) * NCH * 16)
                for tt in range(8):
                    k = vg * 8 + tt
                    if k >= 2:
                        p.wge(t, "vev", k - 1)
                    last = None
                    for j in range(NCH):
                        last = t.matmul(
                            out=vv_ps[k % 2],
                            lhsT=xbf[:, j * TLOC + tt * 128:j * TLOC + (tt + 1) * 128],
                            rhs=wvs[vg % 2][:, j * 512:(j + 1) * 512],
                            start=(j == 0), stop=(j == NCH - 1))
                    if tt == 7:
                        for _ in range(NCH):
                            pass
                        p.inc(last, "wvuse", NCH)
                    p.inc(last, "vev0") if False else None
                    p.inc(last, "vmm") if tt != 7 else None
        block.tensor(tv)

        def sv(s):
            for vg in range(4):
                for tt in range(8):
                    k = vg * 8 + tt
                    if tt != 7:
                        p.wge(s, "vmm", vg * 7 + tt + 1)
                    else:
                        p.wge(s, "wvuse", (vg + 1) * NCH)
                    if k >= 2:
                        p.wge(s, "vts", (k - 1) * 16)
                    i = s.copy(out=vstage[k % 2][:], in_=vv_ps[k % 2])
                    p.inc(i, "vev")
        block.scalar(sv)

        def stv(sync):
            for vg in range(4):
                for tt in range(8):
                    k = vg * 8 + tt
                    p.wge(sync, "vev", k + 1)
                    i = sync.dma_start(
                        out=o_vt[tt * 128:(tt + 1) * 128, vg * 512:(vg + 1) * 512],
                        in_=vstage[k % 2][:])
                    p.inc(i, "vts", 16)
        block.sync(stv)

        def fin(sync):
            p.wait(sync, "xts")
            p.wait(sync, "qkst")
            p.wait(sync, "vts")
        block.sync(fin)





def emit_l0attn(p, block, io):
    nc = p.nc
    qp = io["qp"]
    kfull = io["kfull"]
    vfull = io["vfull"]
    Wproj = io["Wproj"]
    identb = io["identb"]
    o_s0 = io["s0"]

    NH = 16
    KT = 4096

    kb = [p.sb(f"kb{i}", [128, KT], BF16) for i in range(2)]
    vb = [p.sb(f"vb{i}", [128, KT], BF16) for i in range(2)]
    qb = [p.sb(f"qb{i}", [128, TLOC], BF16) for i in range(2)]
    ptile = [p.sb(f"ptile{i}", [128, KT], BF16) for i in range(4)]
    pts = [p.sb(f"pts{i}", [128, 512], BF16) for i in range(2)]
    fm = p.sb("fm", [128, 16], F32)
    ao = p.sb("ao", [128, NH * TLOC], BF16)
    sidb = p.sb("sidb", [128, 128], BF16)
    wpjs = [p.sb(f"wpjs{i}", [128, 16 * 512], BF16) for i in range(2)]
    stage = [p.sb(f"pstg{i}", [128, 512], F32) for i in range(2)]

    NM = slice(0, 4)
    SS = slice(4, 8)
    FF = slice(8, 12)
    CMN = slice(12, 13)
    CZ = slice(13, 14)
    CR = slice(14, 15)

    with \
         nc.psum_tensor(p.un("psA"), [128, TLOC], F32) as psA, \
         nc.psum_tensor(p.un("psB"), [128, TLOC], F32) as psB, \
         nc.psum_tensor(p.un("psC"), [128, TLOC], F32) as psC, \
         nc.psum_tensor(p.un("ptb0"), [128, 512], BF16) as ptb0, \
         nc.psum_tensor(p.un("ptb1"), [128, 512], BF16) as ptb1:

        sc_ps = [psA, psB]
        pv_ps = psC[:, 0:512]
        ptb = [ptb0, ptb1]

        def ld0(sync):
            p.inc(sync.dma_start(out=sidb[:], in_=identb), "ld", 16)
        block.sync(ld0)

        for h in range(NH):
            hb = h % 2

            def g_ld(g, h=h, hb=hb):
                if h >= 2:
                    p.wge(g, "hdone", h - 1)
                for rp in range(4):
                    i = g.dma_start(
                        out=kb[hb][:, rp * 1024:(rp + 1) * 1024],
                        in_=kfull[rp * 2048 + h * 128:rp * 2048 + (h + 1) * 128, :])
                    p.inc(i, "kld", 16)
                i = g.dma_start(
                    out=vb[hb][:].rearrange("pp (kt d) -> pp kt d", d=128),
                    in_=vfull.rearrange("(kt pp) (hh d) -> hh pp kt d",
                                        pp=128, d=128)[h])
                p.inc(i, "vld", 16)
                i = g.dma_start(out=qb[hb][:], in_=qp[h * 128:(h + 1) * 128, :])
                p.inc(i, "qld", 16)
            block.gpsimd(g_ld)

            for g in range(2):
                for l in range(4):
                    ql = g * 4 + l
                    for kc in range(4):
                        def t_sc(t, h=h, hb=hb, ql=ql, kc=kc):
                            if ql == 0 and kc == 0:
                                p.wge(t, "kld", (h + 1) * 64)
                                p.wge(t, "qld", (h + 1) * 16)
                                if h >= 1:
                                    p.wait(t, "aoev")
                            if kc == 0:
                                p.wait(t, "tpmm")
                            p.wait(t, "expd", max(0, p.c("expd") - 1))
                            last = None
                            for n2 in range(2):
                                last = t.matmul(
                                    out=sc_ps[kc % 2][:, n2 * 512:(n2 + 1) * 512],
                                    lhsT=qb[hb][:, ql * 128:(ql + 1) * 128],
                                    rhs=kb[hb][:, kc * 1024 + n2 * 512:kc * 1024 + (n2 + 1) * 512],
                                    start=True, stop=(n2 == 1),
                                    skip_group_check=True)
                            p.inc(last, "scmm")
                        block.tensor(t_sc)

                        def v_mx(v, h=h, ql=ql, kc=kc):
                            p.wait(v, "scmm")
                            if kc == 0:
                                p.wait(v, "resc")
                            i = v.reduce_max(out=fm[:, kc:kc + 1], in_=sc_ps[kc % 2][:],
                                             axis=AX, negate=True)
                            p.inc(i, "mx")
                        block.vector(v_mx)

                        def s_exp(s, h=h, l=l, kc=kc):
                            p.wait(s, "mx")
                            i = s.activation(
                                out=ptile[l][:, kc * 1024:(kc + 1) * 1024],
                                in_=sc_ps[kc % 2][:], func=AF.Exp,
                                bias=fm[:, kc:kc + 1],
                                accum_out=fm[:, 4 + kc:5 + kc])
                            p.inc(i, "expd")
                        block.scalar(s_exp)

                    def v_row(v, h=h, l=l):
                        p.wait(v, "expd")
                        i = v.tensor_reduce(out=fm[:, CMN], in_=fm[:, NM], axis=AX,
                                            op=ALU.min)
                        p.inc(i, "mn")
                    block.vector(v_row)

                    def s_f(s, h=h, l=l):
                        p.wait(s, "mn")
                        i = s.activation(out=fm[:, FF], in_=fm[:, NM], func=AF.Exp,
                                         scale=-1.0, bias=fm[:, CMN])
                        p.inc(i, "fe")
                    block.scalar(s_f)

                    def v_z(v, h=h, l=l):
                        p.wait(v, "fe")
                        v.tensor_tensor(out=fm[:, SS], in0=fm[:, SS], in1=fm[:, FF],
                                        op=ALU.mult)
                        v.drain()
                        v.reduce_sum(out=fm[:, CZ], in_=fm[:, SS], axis=AX)
                        v.drain()
                        v.reciprocal(out=fm[:, CR], in_=fm[:, CZ])
                        v.drain()
                        v.tensor_scalar(out=fm[:, FF], in0=fm[:, FF],
                                        scalar1=fm[:, CR], scalar2=0.8,
                                        op0=ALU.mult, op1=ALU.mult)
                        v.drain()
                        for kc in range(4):
                            i = v.tensor_scalar(
                                out=ptile[l][:, kc * 1024:(kc + 1) * 1024],
                                in0=ptile[l][:, kc * 1024:(kc + 1) * 1024],
                                scalar1=fm[:, 8 + kc:9 + kc], scalar2=None,
                                op0=ALU.mult)
                        p.inc(i, "resc")
                    block.vector(v_z)

                evval = {}
                for kt in range(33):
                    if kt < 32:
                        def t_tp(t, kt=kt, g=g, h=h):
                            if kt == 0:
                                p.wait(t, "resc")
                                if not (h == 0 and g == 0):
                                    p.wait(t, "aoev")
                            if kt >= 2:
                                sem = "ptevV" if kt % 2 == 0 else "ptevS"
                                p.wge(t, sem, evval[kt - 2])
                            last = None
                            for l in range(4):
                                last = t.transpose(
                                    out=ptb[kt % 2][:, l * 128:(l + 1) * 128],
                                    in_=ptile[l][:, kt * 128:(kt + 1) * 128],
                                    identity=sidb[:])
                            p.inc(last, "tpmm")
                        block.tensor(t_tp)

                        if kt % 2 == 0:
                            def v_ptev(v, kt=kt):
                                p.wait(v, "tpmm")
                                evval[kt] = p.inc(v.tensor_copy(
                                    out=pts[kt % 2][:], in_=ptb[kt % 2][:]), "ptevV")
                            block.vector(v_ptev)
                        else:
                            def s_ptev(s, kt=kt):
                                p.wait(s, "tpmm")
                                evval[kt] = p.inc(s.copy(
                                    out=pts[kt % 2][:], in_=ptb[kt % 2][:]), "ptevS")
                            block.scalar(s_ptev)

                    if kt >= 1:
                        def t_pv(t, kt=kt, h=h, hb=hb):
                            kk = kt - 1
                            sem = "ptevV" if kk % 2 == 0 else "ptevS"
                            p.wge(t, sem, evval[kk])
                            i = t.matmul(out=pv_ps,
                                         lhsT=vb[hb][:, kk * 128:(kk + 1) * 128],
                                         rhs=pts[kk % 2][:],
                                         start=(kk == 0), stop=(kk == 31),
                                         skip_group_check=True)
                            if kk == 31:
                                p.inc(i, "pvdn")
                        block.tensor(t_pv)

                def v_ao(v, h=h, g=g):
                    p.wait(v, "pvdn")
                    i = v.tensor_copy(
                        out=ao[:, h * TLOC + g * 512:h * TLOC + (g + 1) * 512],
                        in_=pv_ps)
                    p.inc(i, "aoev")
                block.vector(v_ao)

            def g_hdone(g_, h=h):
                p.wge(g_, "aoev", (h + 1) * 2)
                p.inc(g_.engine_nop(), "hdone")
            block.gpsimd(g_hdone)

        for ocg in range(4):
            def ldpj(g_, ocg=ocg):
                if ocg >= 2:
                    p.wge(g_, "pjmm", (ocg - 1) * 8)
                for j in range(NH):
                    i = g_.dma_start(
                        out=wpjs[ocg % 2][:, j * 512:(j + 1) * 512],
                        in_=Wproj[j * 128:(j + 1) * 128, ocg * 512:(ocg + 1) * 512])
                    p.inc(i, "pjld", 16)
            block.gpsimd(ldpj)

            def tpj(t, ocg=ocg):
                if ocg == 0:
                    p.wait(t, "aoev")
                p.wge(t, "pjld", (ocg + 1) * NH * 16)
                for tt in range(8):
                    k = ocg * 8 + tt
                    if k >= 2:
                        p.wge(t, "pjev", k - 1)
                    last = None
                    for j in range(NH):
                        last = t.matmul(
                            out=(psA if k % 2 == 0 else psB)[:, 0:512],
                            lhsT=ao[:, j * TLOC + tt * 128:j * TLOC + (tt + 1) * 128],
                            rhs=wpjs[ocg % 2][:, j * 512:(j + 1) * 512],
                            start=(j == 0), stop=(j == NH - 1),
                            skip_group_check=True)
                    p.inc(last, "pjmm")
            block.tensor(tpj)

            def spj(s, ocg=ocg):
                for tt in range(8):
                    k = ocg * 8 + tt
                    p.wge(s, "pjmm", k + 1)
                    if k >= 2:
                        p.wge(s, "pjst", (k - 1) * 16)
                    i = s.copy(out=stage[k % 2][:, 0:512],
                               in_=(psA if k % 2 == 0 else psB)[:, 0:512])
                    p.inc(i, "pjev")
            block.scalar(spj)

            def stpj(sync, ocg=ocg):
                for tt in range(8):
                    k = ocg * 8 + tt
                    p.wge(sync, "pjev", k + 1)
                    i = sync.dma_start(
                        out=o_s0[tt * 128:(tt + 1) * 128, ocg * 512:(ocg + 1) * 512],
                        in_=stage[k % 2][:, 0:512])
                    p.inc(i, "pjst", 16)
            block.sync(stpj)

        def fin(sync):
            p.wait(sync, "pjst")
        block.sync(fin)




def emit_meanabs(p, block, tag, W, R, Ccols, strip, wacc, srow_out, onesf_c):
    cb = min(strip[0].shape[1], Ccols)
    blks = [(r, c) for r in range(R // 128) for c in range(Ccols // cb)]

    def ld(sync):
        for k, (r, c) in enumerate(blks):
            if k >= 2:
                p.wge(sync, tag + "red", k - 1)
            i = sync.dma_start(out=strip[k % 2][:, 0:cb],
                               in_=W[r * 128:(r + 1) * 128, c * cb:(c + 1) * cb])
            p.inc(i, tag + "ld", 16)
    block.sync(ld)

    def vred(v):
        for k in range(len(blks)):
            p.wge(v, tag + "ld", (k + 1) * 16)
            i = v.reduce_sum(out=wacc[:, k:k + 1], in_=strip[k % 2][:, 0:cb], axis=AX,
                             apply_absolute_value=True)
            p.inc(i, tag + "red")
        v.drain()
        p.inc(v.reduce_sum(out=wacc[:, len(blks):len(blks) + 1],
                           in_=wacc[:, 0:len(blks)], axis=AX), tag + "racc")
    block.vector(vred)

    def tdot(t):
        p.wait(t, tag + "racc")
        p.inc(t.matmul(out=p.meanps[0:1, 0:1], lhsT=wacc[:, len(blks):len(blks) + 1],
                       rhs=onesf_c[:], start=True, stop=True,
                       skip_group_check=True), tag + "dot")
    block.tensor(tdot)

    def sev(s):
        p.wait(s, tag + "dot")
        p.inc(s.activation(out=srow_out, in_=p.meanps[0:1, 0:1], func=AF.Identity,
                           scale=1.0 / (R * Ccols)), tag + "mean")
    block.scalar(sev)

    def vclip(v):
        p.wait(v, tag + "mean")
        p.inc(v.tensor_scalar(out=srow_out, in0=srow_out, scalar1=1e-5, scalar2=None,
                              op0=ALU.max), tag + "f")
    block.vector(vclip)
    return tag + "f"



def emit_l0ffn_a(p, block, io):
    nc = p.nc
    xnt = io["xnt"]
    w1T = io["w1T"]
    b1row = io["b1row"]
    identb = io["identb"]
    o_h = io["hpre"]

    strip = [p.sb(f"wstrip{i}", [128, 4096], F32) for i in range(2)]
    wacc = p.sb("wacc", [128, 40], F32)
    srow = p.sb("srow", [1, 8], F32)
    onesf_c = p.sb("onesf_c", [128, 1], F32)
    ones_rf = p.sb("ones_rf", [1, 128], F32)
    ntauc = p.sb("ntauc", [128, 1], F32)
    ptauc = p.sb("ptauc", [128, 1], F32)
    f1c = p.sb("f1c", [128, 1], F32)
    xtile = [p.sb(f"xtile{i}", [128, E], F32) for i in range(2)]
    qcol = p.sb("qcol", [128, 6], F32)
    xr = p.sb("xr", [128, E], F32)
    xq8 = p.sb("xq8", [128, E], I8)
    xqb = p.sb("xqb", [128, E], BF16)
    xqf = p.sb("xqf", [128, NCH * TLOC], BF16)
    sidb = p.sb("sidb", [128, 128], BF16)
    sgn = [p.sb(f"sgn{i}", [128, 1024], BF16) for i in range(2)]
    wst = p.sb("wst", [128, NCH * 1024], BF16)
    b1B = p.sb("b1B", [128, 8192], F32)
    b1s = p.sb("b1s", [1, 512], F32)
    ystg = [p.sb(f"ystg{i}", [128, 512], F32) for i in range(2)]

    with \
         nc.psum_tensor(p.un("psA"), [128, TLOC], F32) as psA, \
         nc.psum_tensor(p.un("psB"), [128, TLOC], F32) as psB, \
         nc.psum_tensor(p.un("psC"), [128, TLOC], F32) as psC, \
         nc.psum_tensor(p.un("ptb0"), [128, 128], BF16) as ptb0, \
         nc.psum_tensor(p.un("ptb1"), [128, 128], BF16) as ptb1:

        p.meanps = psC[0:1, 512:1024]
        ptb = [ptb0, ptb1]

        def ld0(sync):
            p.inc(sync.dma_start(out=sidb[:], in_=identb), "ld", 16)
        block.sync(ld0)

        def v0(v):
            v.memset(onesf_c[:], 1.0)
            v.memset(ones_rf[:], 1.0)
            p.inc(v.drain(), "cst")
        block.vector(v0)

        f1sem = emit_meanabs(p, block, "w1m", w1T, E, 8192, strip, wacc,
                             srow[0:1, 0:1], onesf_c)

        def vtau(v):
            p.wait(v, f1sem)
            v.tensor_scalar(out=srow[0:1, 1:2], in0=srow[0:1, 0:1], scalar1=-0.5,
                            scalar2=None, op0=ALU.mult)
            v.drain()
            p.inc(v.tensor_scalar(out=srow[0:1, 2:3], in0=srow[0:1, 0:1], scalar1=0.5,
                                  scalar2=None, op0=ALU.mult), "tau")
        block.vector(vtau)

        def tbc(t):
            p.wait(t, "tau")
            p.wait(t, "cst")
            t.matmul(out=psC[:, 0:1], lhsT=ones_rf[:], rhs=srow[0:1, 1:2],
                     start=True, stop=True, skip_group_check=True)
            t.matmul(out=psC[:, 1:2], lhsT=ones_rf[:], rhs=srow[0:1, 2:3],
                     start=True, stop=True, skip_group_check=True)
            p.inc(t.matmul(out=psC[:, 2:3], lhsT=ones_rf[:], rhs=srow[0:1, 0:1],
                           start=True, stop=True, skip_group_check=True), "tbc")
        block.tensor(tbc)

        def sbc(s):
            p.wait(s, "tbc")
            s.copy(out=ntauc[:], in_=psC[:, 0:1])
            s.copy(out=ptauc[:], in_=psC[:, 1:2])
            p.inc(s.copy(out=f1c[:], in_=psC[:, 2:3]), "bcc")
        block.scalar(sbc)

        for tt in range(8):
            def ldx(sync, tt=tt):
                if tt >= 2:
                    p.wge(sync, "xqd", tt - 1)
                i = sync.dma_start(out=xtile[tt % 2][:],
                                   in_=xnt[tt * 128:(tt + 1) * 128, :])
                p.inc(i, "xld", 16)
            block.sync(ldx)

            def vq(v, tt=tt):
                p.wge(v, "xld", (tt + 1) * 16)
                xt = xtile[tt % 2]
                v.tensor_tensor(out=xr[:], in0=xt[:], in1=xt[:], op=ALU.mult)
                v.drain()
                v.reduce_sum(out=qcol[:, 0:1], in_=xr[:], axis=AX)
                v.drain()
                v.tensor_scalar(out=qcol[:, 0:1], in0=qcol[:, 0:1],
                                scalar1=1.0 / E, scalar2=1e-5, op0=ALU.mult,
                                op1=ALU.add)
                p.inc(v.drain(), "ms")
            block.vector(vq)

            def sq(s, tt=tt):
                p.wait(s, "ms")
                p.inc(s.activation(out=qcol[:, 1:2], in_=qcol[:, 0:1], func=AF.Sqrt),
                      "msq")
            block.scalar(sq)

            def vq2(v, tt=tt):
                p.wait(v, "msq")
                v.reciprocal(out=qcol[:, 1:2], in_=qcol[:, 1:2])
                v.drain()
                v.tensor_scalar(out=xr[:], in0=xtile[tt % 2][:], scalar1=qcol[:, 1:2],
                                scalar2=None, op0=ALU.mult)
                v.drain()
                v.reduce_max(out=qcol[:, 2:3], in_=xr[:], axis=AX,
                             apply_absolute_value=True)
                v.drain()
                v.tensor_scalar(out=qcol[:, 2:3], in0=qcol[:, 2:3], scalar1=1e-5,
                                scalar2=None, op0=ALU.max)
                v.drain()
                v.reciprocal(out=qcol[:, 3:4], in_=qcol[:, 2:3])
                v.drain()
                v.tensor_scalar(out=qcol[:, 3:4], in0=qcol[:, 3:4], scalar1=127.0,
                                scalar2=None, op0=ALU.mult)
                v.tensor_scalar(out=qcol[:, 4:5], in0=qcol[:, 2:3],
                                scalar1=1.0 / 127.0, scalar2=None, op0=ALU.mult)
                v.drain()
                v.tensor_scalar(out=xr[:], in0=xr[:], scalar1=qcol[:, 3:4],
                                scalar2=None, op0=ALU.mult)
                v.drain()
                v.tensor_copy(out=xq8[:], in_=xr[:])
                v.drain()
                p.inc(v.tensor_scalar(out=xqb[:], in0=xq8[:], scalar1=qcol[:, 4:5],
                                      scalar2=None, op0=ALU.mult), "xqb")
            block.vector(vq2)

            for j in range(NCH):
                def ttp(t, tt=tt, j=j):
                    k = tt * NCH + j
                    p.wait(t, "xqb")
                    p.wait(t, "xqe", max(0, p.c("xqe") - 1))
                    i = t.transpose(out=ptb[k % 2][:],
                                    in_=xqb[:, j * 128:(j + 1) * 128],
                                    identity=sidb[:])
                    p.inc(i, "xqt")
                block.tensor(ttp)

                def vev(v, tt=tt, j=j):
                    p.wait(v, "xqt")
                    i = v.tensor_copy(
                        out=xqf[:, j * TLOC + tt * 128:j * TLOC + (tt + 1) * 128],
                        in_=ptb[(tt * NCH + j) % 2][:])
                    p.inc(i, "xqe")
                block.vector(vev)

            def vdone(v, tt=tt):
                p.wait(v, "xqe")
                p.inc(v.engine_nop(), "xqd")
            block.vector(vdone)

        for oc in range(16):
            def ldb(sync, oc=oc):
                if oc >= 1:
                    p.wge(sync, "b1bc", oc)
                i = sync.dma_start(out=b1s[:], in_=b1row[0:1, oc * 512:(oc + 1) * 512])
                p.inc(i, "b1ld", 16)
            block.sync(ldb)

            def tb1(t, oc=oc):
                p.wge(t, "b1ld", (oc + 1) * 16)
                p.wait(t, "bcc")
                if oc >= 1:
                    p.wge(t, "b1ev", oc)
                i = t.matmul(out=psC[:, 0:512], lhsT=ones_rf[:], rhs=b1s[:],
                             start=True, stop=True, skip_group_check=True)
                p.inc(i, "b1bc")
            block.tensor(tb1)

            def sb1(s, oc=oc):
                p.wge(s, "b1bc", oc + 1)
                i = s.copy(out=b1B[:, oc * 512:(oc + 1) * 512], in_=psC[:, 0:512])
                p.inc(i, "b1ev")
            block.scalar(sb1)

        for og in range(8):
            for j in range(NCH):
                def ldw(sync, og=og, j=j):
                    k = og * NCH + j
                    if k < 2:
                        p.wait(sync, "w1mracc")
                    else:
                        p.wge(sync, "sgn", k - 1)
                    i = sync.dma_start(
                        out=strip[k % 2][:, 0:1024],
                        in_=w1T[j * 128:(j + 1) * 128, og * 1024:(og + 1) * 1024])
                    p.inc(i, "wqld", 16)
                block.sync(ldw)

                def ssgn(s, og=og, j=j):
                    k = og * NCH + j
                    p.wge(s, "wqld", (k + 1) * 16)
                    p.wait(s, "bcc")
                    p.wait(s, "wsg")
                    s.activation(out=sgn[0][:], in_=strip[k % 2][:, 0:1024],
                                 func=AF.Sign, bias=ntauc[:])
                    i = s.activation(out=sgn[1][:], in_=strip[k % 2][:, 0:1024],
                                     func=AF.Sign, bias=ptauc[:])
                    p.inc(i, "sgn")
                block.scalar(ssgn)

                def vws(v, og=og, j=j):
                    p.wait(v, "sgn")
                    if og >= 1:
                        p.wge(v, "ymm", og * 16)
                    wsl = wst[:, j * 1024:(j + 1) * 1024]
                    v.tensor_tensor(out=wsl, in0=sgn[0][:], in1=sgn[1][:], op=ALU.add)
                    v.drain()
                    i = v.tensor_scalar(out=wsl, in0=wsl, scalar1=0.5, scalar2=None,
                                        op0=ALU.mult)
                    p.inc(i, "wsg")
                block.vector(vws)

            for tt in range(8):
                for oc in range(2):
                    def tmm(t, og=og, tt=tt, oc=oc):
                        k = og * 16 + tt * 2 + oc
                        p.wait(t, "wsg")
                        p.wait(t, "xqe")
                        p.wait(t, "b1ev")
                        p.wait(t, "yf", max(0, p.c("yf") - 1))
                        last = None
                        for j in range(NCH):
                            last = t.matmul(
                                out=(psA if k % 2 == 0 else psB)[:, 0:512],
                                lhsT=xqf[:, j * TLOC + tt * 128:j * TLOC + (tt + 1) * 128],
                                rhs=wst[:, j * 1024 + oc * 512:j * 1024 + (oc + 1) * 512],
                                start=(j == 0), stop=(j == NCH - 1),
                                skip_group_check=True)
                        p.inc(last, "ymm")
                    block.tensor(tmm)

                    def sev(s, og=og, tt=tt, oc=oc):
                        k = og * 16 + tt * 2 + oc
                        p.wait(s, "ymm")
                        if k >= 2:
                            p.wge(s, "hst", (k - 1) * 16)
                        i = s.activation(out=ystg[k % 2][:],
                                         in_=(psA if k % 2 == 0 else psB)[:, 0:512],
                                         func=AF.Identity, scale=f1c[:])
                        p.inc(i, "yf")
                    block.scalar(sev)

                    def vb1(v, og=og, tt=tt, oc=oc):
                        k = og * 16 + tt * 2 + oc
                        p.wait(v, "yf")
                        i = v.tensor_tensor(
                            out=ystg[k % 2][:], in0=ystg[k % 2][:],
                            in1=b1B[:, og * 1024 + oc * 512:og * 1024 + (oc + 1) * 512],
                            op=ALU.add)
                        p.inc(i, "yb")
                    block.vector(vb1)

                    def ssl(s, og=og, tt=tt, oc=oc):
                        k = og * 16 + tt * 2 + oc
                        p.wait(s, "yb")
                        i = s.activation(out=ystg[k % 2][:], in_=ystg[k % 2][:],
                                         func=AF.Silu)
                        p.inc(i, "yev")
                    block.scalar(ssl)

                    def sth(sync, og=og, tt=tt, oc=oc):
                        p.wait(sync, "yev")
                        i = sync.dma_start(
                            out=o_h[tt * 128:(tt + 1) * 128,
                                    og * 1024 + oc * 512:og * 1024 + (oc + 1) * 512],
                            in_=ystg[(og * 16 + tt * 2 + oc) % 2][:])
                        p.inc(i, "hst", 16)
                    block.sync(sth)

        def fin(sync):
            p.wait(sync, "hst")
        block.sync(fin)





def emit_l0ffn_b(p, block, io):
    nc = p.nc
    hpre = io["hpre"]
    lngrow = io["lngrow"]
    lnbrow = io["lnbrow"]
    o_xq2 = io["xq2"]
    o_inv = io["inv2"]

    ht = p.sb("ht", [128, 8192], F32)
    hsq = p.sb("hsq", [128, 8192], F32)
    gB = p.sb("gB", [128, 8192], F32)
    bB = p.sb("bB", [128, 8192], F32)
    rowslc = p.sb("rowslc", [1, 512], F32)
    ones_rf = p.sb("ones_rf", [1, 128], F32)
    qcol = p.sb("qcol", [128, 8], F32)
    xq8 = p.sb("xq8", [128, 8192], I8)
    invst = p.sb("invst", [128, 8], F32)

    with \
         nc.psum_tensor(p.un("psA"), [128, TLOC], F32) as psA:

        def v0(v):
            v.memset(ones_rf[:], 1.0)
            p.inc(v.drain(), "cst")
        block.vector(v0)

        for i2, (row_in, dstB) in enumerate(((lngrow, gB), (lnbrow, bB))):
            for oc in range(16):
                def ldb(sync, row_in=row_in, i2=i2, oc=oc):
                    k = i2 * 16 + oc
                    if k >= 1:
                        p.wge(sync, "gbc", k)
                    i = sync.dma_start(out=rowslc[:],
                                       in_=row_in[0:1, oc * 512:(oc + 1) * 512])
                    p.inc(i, "gld", 16)
                block.sync(ldb)

                def tb(t, i2=i2, oc=oc):
                    k = i2 * 16 + oc
                    p.wge(t, "gld", (k + 1) * 16)
                    p.wait(t, "cst")
                    if k >= 1:
                        p.wge(t, "gev", k)
                    i = t.matmul(out=psA[:, 0:512], lhsT=ones_rf[:], rhs=rowslc[:],
                                 start=True, stop=True, skip_group_check=True)
                    p.inc(i, "gbc")
                block.tensor(tb)

                def sb_(s, dstB=dstB, i2=i2, oc=oc):
                    k = i2 * 16 + oc
                    p.wge(s, "gbc", k + 1)
                    i = s.copy(out=dstB[:, oc * 512:(oc + 1) * 512], in_=psA[:, 0:512])
                    p.inc(i, "gev")
                block.scalar(sb_)

        for tt in range(8):
            def ldh(sync, tt=tt):
                if tt >= 1:
                    p.wge(sync, "q2", tt)
                i = sync.dma_start(out=ht[:], in_=hpre[tt * 128:(tt + 1) * 128, :])
                p.inc(i, "hld", 16)
            block.sync(ldh)

            def sgel(s, tt=tt):
                p.wge(s, "hld", (tt + 1) * 16)
                p.wait(s, "gev")
                i = s.activation(out=ht[:], in_=ht[:], func=AF.Gelu)
                p.inc(i, "gelu")
            block.scalar(sgel)

            def vln(v, tt=tt):
                p.wait(v, "gelu")
                v.reduce_sum(out=qcol[:, 0:1], in_=ht[:], axis=AX)
                v.drain()
                v.tensor_scalar(out=qcol[:, 0:1], in0=qcol[:, 0:1], scalar1=1.0 / 8192,
                                scalar2=None, op0=ALU.mult)
                v.drain()
                v.tensor_scalar(out=ht[:], in0=ht[:], scalar1=qcol[:, 0:1],
                                scalar2=None, op0=ALU.subtract)
                v.drain()
                v.tensor_tensor(out=hsq[:], in0=ht[:], in1=ht[:], op=ALU.mult)
                v.drain()
                v.reduce_sum(out=qcol[:, 1:2], in_=hsq[:], axis=AX)
                v.drain()
                v.tensor_scalar(out=qcol[:, 1:2], in0=qcol[:, 1:2], scalar1=1.0 / 8192,
                                scalar2=1e-5, op0=ALU.mult, op1=ALU.add)
                p.inc(v.drain(), "var2")
            block.vector(vln)

            def sstd(s, tt=tt):
                p.wait(s, "var2")
                p.inc(s.activation(out=qcol[:, 2:3], in_=qcol[:, 1:2], func=AF.Sqrt),
                      "std2")
            block.scalar(sstd)

            def vq(v, tt=tt):
                p.wait(v, "std2")
                v.reciprocal(out=qcol[:, 2:3], in_=qcol[:, 2:3])
                v.drain()
                v.tensor_scalar(out=ht[:], in0=ht[:], scalar1=qcol[:, 2:3],
                                scalar2=None, op0=ALU.mult)
                v.drain()
                v.tensor_tensor(out=ht[:], in0=ht[:], in1=gB[:], op=ALU.mult)
                v.drain()
                v.tensor_tensor(out=ht[:], in0=ht[:], in1=bB[:], op=ALU.add)
                v.drain()
                v.reduce_max(out=qcol[:, 3:4], in_=ht[:], axis=AX,
                             apply_absolute_value=True)
                v.drain()
                v.tensor_scalar(out=qcol[:, 3:4], in0=qcol[:, 3:4], scalar1=1e-5,
                                scalar2=None, op0=ALU.max)
                v.drain()
                v.reciprocal(out=qcol[:, 4:5], in_=qcol[:, 3:4])
                v.drain()
                v.tensor_scalar(out=qcol[:, 4:5], in0=qcol[:, 4:5], scalar1=127.0,
                                scalar2=None, op0=ALU.mult)
                v.tensor_scalar(out=invst[:, 0:1], in0=qcol[:, 3:4],
                                scalar1=1.0 / 127.0, scalar2=None, op0=ALU.mult)
                v.drain()
                v.tensor_scalar(out=ht[:], in0=ht[:], scalar1=qcol[:, 4:5],
                                scalar2=None, op0=ALU.mult)
                v.drain()
                p.wge(v, "q2st", tt * 32)
                p.inc(v.tensor_copy(out=xq8[:], in_=ht[:]), "q2")
            block.vector(vq)

            def stq(sync, tt=tt):
                p.wait(sync, "q2")
                p.inc(sync.dma_start(out=o_xq2[tt * 128:(tt + 1) * 128, :],
                                     in_=xq8[:]), "q2st", 16)
                p.inc(sync.dma_start(out=o_inv[tt * 128:(tt + 1) * 128, :],
                                     in_=invst[:, 0:1]), "q2st", 16)
            block.sync(stq)

        def fin(sync):
            p.wait(sync, "q2st")
        block.sync(fin)




def emit_l0ffn_c(p, block, io):
    nc = p.nc
    xq2 = io["xq2"]
    inv2 = io["inv2"]
    w2T = io["w2T"]
    b2row = io["b2row"]
    x0t = io["x0t"]
    s0 = io["s0"]
    identb = io["identb"]
    o_x1 = io["x1t"]

    xq2f = p.sb("xq2f", [128, 64 * TLOC], BF16)
    strip = [p.sb(f"w2s{i}", [128, 2048], F32) for i in range(2)]
    wacc = p.sb("wacc", [128, 80], F32)
    srow = p.sb("srow", [1, 8], F32)
    onesf_c = p.sb("onesf_c", [128, 1], F32)
    ones_rf = p.sb("ones_rf", [1, 128], F32)
    ntauc = p.sb("ntauc", [128, 1], F32)
    ptauc = p.sb("ptauc", [128, 1], F32)
    f2c = p.sb("f2c", [128, 1], F32)
    sgn = [p.sb(f"sgn{i}", [128, 512], BF16) for i in range(2)]
    wsb = [p.sb(f"wsb{i}", [128, 512], BF16) for i in range(2)]
    xq8t = p.sb("xq8t", [128, 8192], I8)
    invc = p.sb("invc", [128, 8], F32)
    xqb2 = p.sb("xqb2", [128, 8192], BF16)
    rowslc = p.sb("rowslc", [1, 512], F32)
    b2B = p.sb("b2B", [128, E], F32)
    ystg = [p.sb(f"ystg{i}", [128, 512], F32) for i in range(2)]
    xblk = [p.sb(f"xblk{i}", [128, 512], F32) for i in range(2)]
    sblk = [p.sb(f"sblk{i}", [128, 512], F32) for i in range(2)]
    sidb = p.sb("sidb", [128, 128], BF16)

    if True:
        with nc.psum_tensor(p.un("mps"), [128, 512], F32) as mps, \
             nc.psum_tensor(p.un("ptb0"), [128, 128], BF16) as ptb0, \
             nc.psum_tensor(p.un("ptb1"), [128, 128], BF16) as ptb1:
            ptb = [ptb0, ptb1]
            p.meanps = mps[0:1, :]

            def ld0(sync):
                p.inc(sync.dma_start(out=sidb[:], in_=identb), "ld", 16)
            block.sync(ld0)

            def v0(v):
                v.memset(onesf_c[:], 1.0)
                v.memset(ones_rf[:], 1.0)
                p.inc(v.drain(), "cst")
            block.vector(v0)

            f2sem = emit_meanabs(p, block, "w2m", w2T, 8192, E, strip, wacc,
                                 srow[0:1, 0:1], onesf_c)

            def vtau(v):
                p.wait(v, f2sem)
                v.tensor_scalar(out=srow[0:1, 1:2], in0=srow[0:1, 0:1], scalar1=-0.5,
                                scalar2=None, op0=ALU.mult)
                v.drain()
                p.inc(v.tensor_scalar(out=srow[0:1, 2:3], in0=srow[0:1, 0:1],
                                      scalar1=0.5, scalar2=None, op0=ALU.mult), "tau")
            block.vector(vtau)

            def tbc(t):
                p.wait(t, "tau")
                p.wait(t, "cst")
                t.matmul(out=mps[:, 0:1], lhsT=ones_rf[:], rhs=srow[0:1, 1:2],
                         start=True, stop=True, skip_group_check=True)
                t.matmul(out=mps[:, 1:2], lhsT=ones_rf[:], rhs=srow[0:1, 2:3],
                         start=True, stop=True, skip_group_check=True)
                p.inc(t.matmul(out=mps[:, 2:3], lhsT=ones_rf[:], rhs=srow[0:1, 0:1],
                               start=True, stop=True, skip_group_check=True), "tbc")
            block.tensor(tbc)

            def sbc(s):
                p.wait(s, "tbc")
                s.copy(out=ntauc[:], in_=mps[:, 0:1])
                s.copy(out=ptauc[:], in_=mps[:, 1:2])
                p.inc(s.copy(out=f2c[:], in_=mps[:, 2:3]), "bcc")
            block.scalar(sbc)

            for oc in range(4):
                def ldb(sync, oc=oc):
                    if oc >= 1:
                        p.wge(sync, "b2bc", oc)
                    i = sync.dma_start(out=rowslc[:],
                                       in_=b2row[0:1, oc * 512:(oc + 1) * 512])
                    p.inc(i, "b2ld", 16)
                block.sync(ldb)

                def tb2(t, oc=oc):
                    p.wge(t, "b2ld", (oc + 1) * 16)
                    p.wait(t, "bcc")
                    if oc >= 1:
                        p.wge(t, "b2ev", oc)
                    i = t.matmul(out=mps[:], lhsT=ones_rf[:], rhs=rowslc[:],
                                 start=True, stop=True, skip_group_check=True)
                    p.inc(i, "b2bc")
                block.tensor(tb2)

                def sb2(s, oc=oc):
                    p.wge(s, "b2bc", oc + 1)
                    i = s.copy(out=b2B[:, oc * 512:(oc + 1) * 512], in_=mps[:])
                    p.inc(i, "b2ev")
                block.scalar(sb2)

            for tt in range(8):
                def ldq(sync, tt=tt):
                    if tt >= 1:
                        p.wge(sync, "xqrec", tt)
                    i = sync.dma_start(out=xq8t[:],
                                       in_=xq2[tt * 128:(tt + 1) * 128, :])
                    p.inc(i, "qld", 16)
                    i = sync.dma_start(out=invc[:, 0:1],
                                       in_=inv2[tt * 128:(tt + 1) * 128, :])
                    p.inc(i, "qld", 16)
                block.sync(ldq)

                def vrec(v, tt=tt):
                    p.wge(v, "qld", (tt + 1) * 32)
                    if tt >= 1:
                        p.wge(v, "xq2e", tt * 64)
                    i = v.tensor_scalar(out=xqb2[:], in0=xq8t[:], scalar1=invc[:, 0:1],
                                        scalar2=None, op0=ALU.mult)
                    p.inc(i, "xqrec")
                block.vector(vrec)

                for c2 in range(64):
                    def ttp(t, tt=tt, c2=c2):
                        k = tt * 64 + c2
                        p.wge(t, "xqrec", tt + 1)
                        p.wge(t, "xq2e", max(0, k - 1))
                        i = t.transpose(out=ptb[k % 2][:],
                                        in_=xqb2[:, c2 * 128:(c2 + 1) * 128],
                                        identity=sidb[:])
                        p.inc(i, "xqt")
                    block.tensor(ttp)

                    def vev(v, tt=tt, c2=c2):
                        k = tt * 64 + c2
                        p.wge(v, "xqt", k + 1)
                        i = v.tensor_copy(
                            out=xq2f[:, c2 * TLOC + tt * 128:c2 * TLOC + (tt + 1) * 128],
                            in_=ptb[k % 2][:])
                        p.inc(i, "xq2e")
                    block.vector(vev)

        with nc.psum_tensor(p.un("psA"), [128, TLOC], F32) as psA, \
             nc.psum_tensor(p.un("psB"), [128, TLOC], F32) as psB, \
             nc.psum_tensor(p.un("psC"), [128, TLOC], F32) as psC, \
             nc.psum_tensor(p.un("psD"), [128, TLOC], F32) as psD:
            pool = [psA, psB, psC, psD]

            def ps8(tt):
                return pool[tt // 2][:, (tt % 2) * 512:(tt % 2 + 1) * 512]

            for oc in range(4):
                for c2 in range(64):
                    def ldw2(sync, oc=oc, c2=c2):
                        k = oc * 64 + c2
                        if k < 2:
                            p.wait(sync, "w2mracc")
                        else:
                            p.wge(sync, "sgn2", k - 1)
                        i = sync.dma_start(
                            out=strip[k % 2][:, 0:512],
                            in_=w2T[c2 * 128:(c2 + 1) * 128, oc * 512:(oc + 1) * 512])
                        p.inc(i, "w2ld", 16)
                    block.sync(ldw2)

                    def ssgn(s, oc=oc, c2=c2):
                        k = oc * 64 + c2
                        p.wge(s, "w2ld", (k + 1) * 16)
                        p.wait(s, "bcc")
                        p.wait(s, "wsg2")
                        s.activation(out=sgn[0][:], in_=strip[k % 2][:, 0:512],
                                     func=AF.Sign, bias=ntauc[:])
                        i = s.activation(out=sgn[1][:], in_=strip[k % 2][:, 0:512],
                                         func=AF.Sign, bias=ptauc[:])
                        p.inc(i, "sgn2")
                    block.scalar(ssgn)

                    def vws(v, oc=oc, c2=c2):
                        k = oc * 64 + c2
                        p.wait(v, "sgn2")
                        if k >= 2:
                            p.wge(v, "w2use", k - 1)
                        wsl = wsb[k % 2][:]
                        v.tensor_tensor(out=wsl, in0=sgn[0][:], in1=sgn[1][:],
                                        op=ALU.add)
                        v.drain()
                        i = v.tensor_scalar(out=wsl, in0=wsl, scalar1=0.5,
                                            scalar2=None, op0=ALU.mult)
                        p.inc(i, "wsg2")
                    block.vector(vws)

                    def tmm(t, oc=oc, c2=c2):
                        k = oc * 64 + c2
                        p.wait(t, "wsg2")
                        if c2 == 0:
                            p.wait(t, "xq2e")
                            if oc >= 1:
                                p.wait(t, "gev")
                        last = None
                        for tt in range(8):
                            last = t.matmul(
                                out=ps8(tt),
                                lhsT=xq2f[:, c2 * TLOC + tt * 128:c2 * TLOC + (tt + 1) * 128],
                                rhs=wsb[k % 2][:],
                                start=(c2 == 0), stop=(c2 == 63),
                                skip_group_check=True)
                        p.inc(last, "w2use")
                    block.tensor(tmm)

                for tt in range(8):
                    k = oc * 8 + tt

                    def sev(s, oc=oc, tt=tt, k=k):
                        p.wait(s, "w2use")
                        if k >= 2:
                            p.wge(s, "x1st", (k - 1) * 16)
                        i = s.activation(out=ystg[k % 2][:], in_=ps8(tt),
                                         func=AF.Identity, scale=f2c[:])
                        p.inc(i, "gev")
                    block.scalar(sev)

                    def ldx(sync, oc=oc, tt=tt, k=k):
                        if k >= 2:
                            p.wge(sync, "upd", k - 1)
                        i = sync.dma_start(
                            out=xblk[k % 2][:],
                            in_=x0t[tt * 128:(tt + 1) * 128, oc * 512:(oc + 1) * 512])
                        p.inc(i, "xld", 16)
                        i = sync.dma_start(
                            out=sblk[k % 2][:],
                            in_=s0[tt * 128:(tt + 1) * 128, oc * 512:(oc + 1) * 512])
                        p.inc(i, "xld", 16)
                    block.sync(ldx)

                    def vb2(v, oc=oc, tt=tt, k=k):
                        p.wait(v, "gev")
                        i = v.tensor_tensor(out=ystg[k % 2][:], in0=ystg[k % 2][:],
                                            in1=b2B[:, oc * 512:(oc + 1) * 512],
                                            op=ALU.add)
                        p.inc(i, "gb2")
                    block.vector(vb2)

                    def ssil(s, oc=oc, tt=tt, k=k):
                        p.wait(s, "gb2")
                        i = s.activation(out=ystg[k % 2][:], in_=ystg[k % 2][:],
                                         func=AF.Silu)
                        p.inc(i, "gsil")
                    block.scalar(ssil)

                    def vupd(v, oc=oc, tt=tt, k=k):
                        p.wait(v, "gsil")
                        p.wge(v, "xld", (k + 1) * 32)
                        v.tensor_tensor(out=ystg[k % 2][:], in0=ystg[k % 2][:],
                                        in1=sblk[k % 2][:], op=ALU.mult)
                        v.drain()
                        i = v.tensor_tensor(out=ystg[k % 2][:], in0=ystg[k % 2][:],
                                            in1=xblk[k % 2][:], op=ALU.add)
                        p.inc(i, "upd")
                    block.vector(vupd)

                    def stx(sync, oc=oc, tt=tt, k=k):
                        p.wait(sync, "upd")
                        i = sync.dma_start(
                            out=o_x1[tt * 128:(tt + 1) * 128, oc * 512:(oc + 1) * 512],
                            in_=ystg[k % 2][:])
                        p.inc(i, "x1st", 16)
                    block.sync(stx)

            def fin(sync):
                p.wait(sync, "x1st")
            block.sync(fin)




def emit_l1pre(p, block, io):
    nc = p.nc
    x1t = io["x1t"]
    wqkvT = io["wqkvT"]
    cosfm = io["cosfm"]
    sinfm = io["sinfm"]
    identb = io["identb"]
    o_xnt = io["xnt1"]
    o_qr = io["qr"]
    o_kr = io["kr"]
    o_vt = io["vt1"]

    xtile = [p.sb(f"xtile{i}", [128, E], F32) for i in range(2)]
    xbt = p.sb("xbt", [128, E], BF16)
    qcol = p.sb("qcol", [128, 4], F32)
    xnf_bf = p.sb("xnf_bf", [128, NCH * TLOC], BF16)
    cosb = p.sb("cosb", [128, TLOC], F32)
    sinb = p.sb("sinb", [128, TLOC], F32)
    sidb = p.sb("sidb", [128, 128], BF16)
    wtil = [p.sb(f"wqk{i}", [128, 256], BF16) for i in range(2)]
    qtile = p.sb("qtile", [128, TLOC], F32)
    rot = p.sb("rot", [128, TLOC], F32)
    ra = p.sb("ra", [128, TLOC], F32)
    lnsq = p.sb("lnsq", [128, E], F32)
    stage = [p.sb(f"rstg{i}", [128, TLOC], F32) for i in range(2)]
    wvs = [p.sb(f"wvs{i}", [128, NCH * 512], BF16) for i in range(2)]
    vstage = [p.sb(f"vstg{i}", [128, 512], F32) for i in range(2)]

    SCALER = float(128.0 ** (-0.5))

    with \
         nc.psum_tensor(p.un("psA"), [128, TLOC], F32) as psA, \
         nc.psum_tensor(p.un("psB"), [128, TLOC], F32) as psB, \
         nc.psum_tensor(p.un("psD"), [128, TLOC], F32) as psD, \
         nc.psum_tensor(p.un("ptb0"), [128, 128], BF16) as ptb0, \
         nc.psum_tensor(p.un("ptb1"), [128, 128], BF16) as ptb1:

        qk_ps = [psA, psB]
        vv_ps = [psD[:, 0:512], psD[:, 512:1024]]
        ptb = [ptb0, ptb1]

        def ld0(sync):
            p.inc(sync.dma_start(out=sidb[:], in_=identb), "ld", 16)
            p.inc(sync.dma_start(out=cosb[:], in_=cosfm), "ld", 16)
            p.inc(sync.dma_start(out=sinb[:], in_=sinfm), "ld", 16)
        block.sync(ld0)
        ld_done = p.c("ld")

        for tt in range(8):
            def ldx(sync, tt=tt):
                if tt >= 2:
                    p.wge(sync, "xnst", (tt - 1) * 16)
                i = sync.dma_start(out=xtile[tt % 2][:],
                                   in_=x1t[tt * 128:(tt + 1) * 128, :])
                p.inc(i, "xld", 16)
            block.sync(ldx)

            def vln(v, tt=tt):
                p.wge(v, "xld", (tt + 1) * 16)
                xt = xtile[tt % 2]
                v.reduce_sum(out=qcol[:, 0:1], in_=xt[:], axis=AX)
                v.drain()
                v.tensor_scalar(out=qcol[:, 0:1], in0=qcol[:, 0:1], scalar1=1.0 / E,
                                scalar2=None, op0=ALU.mult)
                v.drain()
                v.tensor_scalar(out=xt[:], in0=xt[:], scalar1=qcol[:, 0:1],
                                scalar2=None, op0=ALU.subtract)
                v.drain()
                v.tensor_tensor(out=lnsq[:], in0=xt[:], in1=xt[:], op=ALU.mult)
                v.drain()
                v.reduce_sum(out=qcol[:, 1:2], in_=lnsq[:], axis=AX)
                v.drain()
                v.tensor_scalar(out=qcol[:, 1:2], in0=qcol[:, 1:2], scalar1=1.0 / E,
                                scalar2=1e-5, op0=ALU.mult, op1=ALU.add)
                p.inc(v.drain(), "vr2")
            block.vector(vln)

            def sstd(s, tt=tt):
                p.wait(s, "vr2")
                p.inc(s.activation(out=qcol[:, 2:3], in_=qcol[:, 1:2], func=AF.Sqrt),
                      "std")
            block.scalar(sstd)

            def vxn(v, tt=tt):
                p.wait(v, "std")
                v.reciprocal(out=qcol[:, 2:3], in_=qcol[:, 2:3])
                v.drain()
                v.tensor_scalar(out=xtile[tt % 2][:], in0=xtile[tt % 2][:],
                                scalar1=qcol[:, 2:3], scalar2=None, op0=ALU.mult)
                v.drain()
                p.inc(v.tensor_copy(out=xbt[:], in_=xtile[tt % 2][:]), "xnb")
            block.vector(vxn)

            def stxn(sync, tt=tt):
                p.wait(sync, "xnb")
                i = sync.dma_start(out=o_xnt[tt * 128:(tt + 1) * 128, :],
                                   in_=xtile[tt % 2][:])
                p.inc(i, "xnst", 16)
            block.sync(stxn)

            for j in range(NCH):
                def ttp(t, tt=tt, j=j):
                    k = tt * NCH + j
                    p.wait(t, "xnb")
                    p.wge(t, "xfe", max(0, p.c("xfe") - 1))
                    i = t.transpose(out=ptb[k % 2][:],
                                    in_=xbt[:, j * 128:(j + 1) * 128],
                                    identity=sidb[:])
                    p.inc(i, "xft")
                block.tensor(ttp)

                def vev(v, tt=tt, j=j):
                    k = tt * NCH + j
                    p.wge(v, "xft", k + 1)
                    i = v.tensor_copy(
                        out=xnf_bf[:, j * TLOC + tt * 128:j * TLOC + (tt + 1) * 128],
                        in_=ptb[k % 2][:])
                    p.inc(i, "xfe")
                block.vector(vev)

            def vnb(v, tt=tt):
                p.wait(v, "xfe")
                p.inc(v.engine_nop(), "xnd")
            block.vector(vnb)

        wblocks = [(j * 128, 128, ocg * 256, 256)
                   for ocg in range(16) for j in range(NCH)]
        ld_qk = emit_weight_stream(
            p, block, "wqk", wqkvT, wblocks, lambda k: wtil[k % 2][:],
            tile_free_sem="wqkuse", tile_depth=2)

        def tqk(t):
            p.wait(t, "xfe")
            for ocg in range(16):
                if ocg >= 1:
                    p.wge(t, "qkev", ocg * 2)
                for j in range(NCH):
                    k = ocg * NCH + j
                    p.wge(t, ld_qk, (k + 1) * 16)
                    last = None
                    for oo in range(2):
                        for tc in range(2):
                            last = t.matmul(
                                out=qk_ps[oo][:, tc * 512:(tc + 1) * 512],
                                lhsT=wtil[k % 2][:, oo * 128:(oo + 1) * 128],
                                rhs=xnf_bf[:, j * TLOC + tc * 512:j * TLOC + (tc + 1) * 512],
                                start=(j == 0), stop=(j == NCH - 1),
                                skip_group_check=True)
                    p.inc(last, "wqkuse")
        block.tensor(tqk)

        for ocg in range(16):
            for oo in range(2):
                oc = ocg * 2 + oo

                def sev(s, ocg=ocg, oo=oo, oc=oc):
                    p.wge(s, "wqkuse", (ocg + 1) * NCH)
                    p.wait(s, "rpd", max(0, p.c("rpd")))
                    if oc < 16:
                        i = s.activation(out=qtile[:], in_=qk_ps[oo][:],
                                         func=AF.Identity, scale=SCALER)
                    else:
                        i = s.copy(out=qtile[:], in_=qk_ps[oo][:])
                    p.inc(i, "qkev")
                block.scalar(sev)

                def srot(s, oc=oc):
                    p.wait(s, "qkev")
                    s.activation(out=rot[0:64, :], in_=qtile[64:128, :],
                                 func=AF.Identity, scale=-1.0)
                    i = s.copy(out=rot[64:128, :], in_=qtile[0:64, :])
                    p.inc(i, "rotd")
                block.scalar(srot)

                def vrope(v, oc=oc):
                    p.wait(v, "rotd")
                    p.wge(v, "rpst", max(0, (oc - 1) * 16))
                    v.tensor_tensor(out=ra[:], in0=qtile[:], in1=cosb[:], op=ALU.mult)
                    v.drain()
                    v.tensor_tensor(out=stage[oc % 2][:], in0=rot[:], in1=sinb[:],
                                    op=ALU.mult)
                    v.drain()
                    i = v.tensor_tensor(out=stage[oc % 2][:], in0=stage[oc % 2][:],
                                        in1=ra[:], op=ALU.add)
                    p.inc(i, "rpd")
                block.vector(vrope)

                def strp(sync, oc=oc):
                    p.wait(sync, "rpd")
                    dst = o_qr if oc < 16 else o_kr
                    h = oc if oc < 16 else oc - 16
                    i = sync.dma_start(out=dst[h * 128:(h + 1) * 128, :],
                                       in_=stage[oc % 2][:])
                    p.inc(i, "rpst", 16)
                block.sync(strp)

        vblocks = [(j * 128, 128, 4096 + vg * 512, 512)
                   for vg in range(4) for j in range(NCH)]
        ld_wv = emit_weight_stream(
            p, block, "wv", wqkvT, vblocks,
            lambda k: wvs[(k // NCH) % 2][:, (k % NCH) * 512:((k % NCH) + 1) * 512],
            tile_free_sem="wvuse", tile_depth=2 * NCH)

        def tv(t):
            p.wait(t, "wqkuse")
            for vg in range(4):
                p.wge(t, ld_wv, (vg + 1) * NCH * 16)
                for tt in range(8):
                    k = vg * 8 + tt
                    if k >= 2:
                        p.wge(t, "vev", k - 1)
                    last = None
                    for j in range(NCH):
                        last = t.matmul(
                            out=vv_ps[k % 2],
                            lhsT=xnf_bf[:, j * TLOC + tt * 128:j * TLOC + (tt + 1) * 128],
                            rhs=wvs[vg % 2][:, j * 512:(j + 1) * 512],
                            start=(j == 0), stop=(j == NCH - 1))
                    if tt == 7:
                        p.inc(last, "wvuse", NCH)
                    else:
                        p.inc(last, "vmm")
        block.tensor(tv)

        def sv(s):
            for vg in range(4):
                for tt in range(8):
                    k = vg * 8 + tt
                    if tt != 7:
                        p.wge(s, "vmm", vg * 7 + tt + 1)
                    else:
                        p.wge(s, "wvuse", (vg + 1) * NCH)
                    if k >= 2:
                        p.wge(s, "vts", (k - 1) * 16)
                    i = s.copy(out=vstage[k % 2][:], in_=vv_ps[k % 2])
                    p.inc(i, "vev")
        block.scalar(sv)

        def stv(sync):
            for vg in range(4):
                for tt in range(8):
                    k = vg * 8 + tt
                    p.wge(sync, "vev", k + 1)
                    i = sync.dma_start(
                        out=o_vt[tt * 128:(tt + 1) * 128, vg * 512:(vg + 1) * 512],
                        in_=vstage[k % 2][:])
                    p.inc(i, "vts", 16)
        block.sync(stv)

        def fin(sync):
            p.wait(sync, "xnst")
            p.wait(sync, "rpst")
            p.wait(sync, "vts")
        block.sync(fin)





def emit_l1attn_a(p, block, io):
    nc = p.nc
    qr = io["qr"]
    kfull = io["kfull1"]
    vfull = io["vfull1"]
    lamin = [io["l1lambda_" + nm] for nm in ("q1", "k1", "q2", "k2")]
    maskA = io["maskA"]
    identb = io["identb"]
    o_y = io["y"]
    o_part = io["part"]

    NH = 16
    KT = 4096

    kb = [p.sb(f"kb{i}", [128, KT], BF16) for i in range(2)]
    vb = [p.sb(f"vb{i}", [128, KT], BF16) for i in range(2)]
    qb = [p.sb(f"qb{i}", [128, TLOC], BF16) for i in range(2)]
    lrow = [p.sb(f"lrow{i}", [1, 128], F32) for i in range(4)]
    lsc = p.sb("lsc", [1, 8], F32)
    nlB = p.sb("nlB", [128, 1], F32)
    maskb = p.sb("maskb", [128, 512], F32)
    ptile = [p.sb(f"ptile{i}", [128, KT], BF16) for i in range(4)]
    p2t = p.sb("p2t", [128, KT], BF16)
    pts = [p.sb(f"pts{i}", [128, 512], BF16) for i in range(2)]
    fmb = [p.sb(f"fmb{i}", [128, 52], F32) for i in range(2)]
    ao = p.sb("ao", [128, NH * TLOC], BF16)
    ystage = [p.sb(f"ystage{i}", [128, 512], F32) for i in range(2)]
    sidb = p.sb("sidb", [128, 128], BF16)
    ones_rf = p.sb("ones_rf", [1, 128], F32)
    ones_cb = p.sb("ones_cb", [128, 1], BF16)
    prow = p.sb("prow", [1, TLOC], F32)
    part = p.sb("part_sb", [1, 32], F32)
    sq_s = p.sb("sq_s", [128, TLOC], BF16)

    def ranges_for(ql):
        rr = []
        if ql > 0:
            rr.append((0, ql * 128, 1))
        rr.append((ql * 128, (ql + 1) * 128, 0))
        if ql < 7:
            rr.append(((ql + 1) * 128, 1024, 0))
        return rr

    with \
         nc.psum_tensor(p.un("psA"), [128, TLOC], F32) as psA, \
         nc.psum_tensor(p.un("psB"), [128, TLOC], F32) as psB, \
         nc.psum_tensor(p.un("psC"), [128, TLOC], F32) as psC, \
         nc.psum_tensor(p.un("ptb0"), [128, 512], BF16) as ptb0, \
         nc.psum_tensor(p.un("ptb1"), [128, 512], BF16) as ptb1:

        sc_ps = [psA, psB]
        pv_ps = psC[:, 0:512]
        aux_ps = psC[:, 512:1024]
        ptb = [ptb0, ptb1]

        def ld0(sync):
            p.inc(sync.dma_start(out=sidb[:], in_=identb), "ld", 16)
            p.inc(sync.dma_start(
                out=maskb[:].rearrange("pp (r c) -> pp r c", c=128),
                in_=maskA.rearrange("(r pp) c -> pp r c", pp=128)), "ld", 16)
            for i_, t_ in enumerate(lamin):
                p.inc(sync.dma_start(out=lrow[i_][:], in_=t_), "ld", 16)
        block.sync(ld0)
        ld_done = p.c("ld")

        def v0(v):
            v.memset(ones_rf[:], 1.0)
            v.memset(ones_cb[:], 1.0)
            p.inc(v.drain(), "cst")
        block.vector(v0)

        def vlam(v):
            p.wait(v, "ld", ld_done)
            v.tensor_tensor(out=lrow[0][:], in0=lrow[0][:], in1=lrow[1][:],
                            op=ALU.mult)
            v.drain()
            v.reduce_sum(out=lsc[0:1, 0:1], in_=lrow[0][:], axis=AX)
            v.drain()
            v.tensor_tensor(out=lrow[2][:], in0=lrow[2][:], in1=lrow[3][:],
                            op=ALU.mult)
            v.drain()
            v.reduce_sum(out=lsc[0:1, 1:2], in_=lrow[2][:], axis=AX)
            p.inc(v.drain(), "ldot")
        block.vector(vlam)

        def slam(s):
            p.wait(s, "ldot")
            p.inc(s.activation(out=lsc[0:1, 2:4], in_=lsc[0:1, 0:2], func=AF.Exp),
                  "lexp")
        block.scalar(slam)

        def vlam2(v):
            p.wait(v, "lexp")
            v.tensor_tensor(out=lsc[0:1, 4:5], in0=lsc[0:1, 2:3], in1=lsc[0:1, 3:4],
                            op=ALU.subtract)
            v.drain()
            v.tensor_scalar(out=lsc[0:1, 4:5], in0=lsc[0:1, 4:5], scalar1=0.8,
                            scalar2=None, op0=ALU.add)
            v.drain()
            p.inc(v.tensor_scalar(out=lsc[0:1, 5:6], in0=lsc[0:1, 4:5], scalar1=-1.0,
                                  scalar2=None, op0=ALU.mult), "lam")
        block.vector(vlam2)

        def tnl(t):
            p.wait(t, "lam")
            p.wait(t, "cst")
            p.inc(t.matmul(out=aux_ps[:, 0:1], lhsT=ones_rf[:], rhs=lsc[0:1, 5:6],
                           start=True, stop=True, skip_group_check=True), "nldot")
        block.tensor(tnl)

        def snl(s):
            p.wait(s, "nldot")
            p.inc(s.copy(out=nlB[:], in_=aux_ps[:, 0:1]), "nlc")
        block.scalar(snl)

        for h in range(NH):
            hb = h % 2

            def g_ld(g, h=h, hb=hb):
                if h >= 2:
                    p.wge(g, "hdone", h - 1)
                for rp in range(4):
                    i = g.dma_start(
                        out=kb[hb][:, rp * 1024:(rp + 1) * 1024],
                        in_=kfull[rp * 2048 + h * 128:rp * 2048 + (h + 1) * 128, :])
                    p.inc(i, "kld", 16)
                i = g.dma_start(
                    out=vb[hb][:].rearrange("pp (kt d) -> pp kt d", d=128),
                    in_=vfull.rearrange("(kt pp) (hh d) -> hh pp kt d",
                                        pp=128, d=128)[h])
                p.inc(i, "vld", 16)
                i = g.dma_start(out=qb[hb][:], in_=qr[h * 128:(h + 1) * 128, :])
                p.inc(i, "qld", 16)
            block.gpsimd(g_ld)

            for g in range(2):
                for l in range(4):
                    ql = g * 4 + l
                    rrs = ranges_for(ql)
                    for si in range(2):
                        pdst = ptile[l] if si == 0 else p2t
                        prange = slice(0, 64) if si == 0 else slice(64, 128)
                        fm = fmb[si]

                        def v_fz(v, si=si, fm=fm):
                            if si == 0:
                                p.wait(v, "resc")
                            v.memset(fm[:, 0:24], 1e30)
                            v.memset(fm[:, 24:36], 0.0)
                            p.inc(v.drain(), "fz")
                        block.vector(v_fz)

                        for rp in range(4):
                            def t_sc(t, h=h, hb=hb, ql=ql, si=si, rp=rp,
                                     prange=prange, g=g, l=l):
                                if ql == 0 and si == 0 and rp == 0:
                                    p.wge(t, "kld", (h + 1) * 64)
                                    p.wge(t, "qld", (h + 1) * 16)
                                    p.wait(t, "nlc")
                                    if h >= 1:
                                        p.wait(t, "aoev")
                                if si == 0 and rp == 0:
                                    p.wait(t, "tpmm")
                                if si == 1 and rp == 0:
                                    p.wait(t, "resc", max(0, p.c("resc") - 0))
                                p.wait(t, "expd", max(0, p.c("expd") - 1))
                                last = None
                                for n2 in range(2):
                                    last = t.matmul(
                                        out=sc_ps[rp % 2][:, n2 * 512:(n2 + 1) * 512],
                                        lhsT=qb[hb][prange, ql * 128:(ql + 1) * 128],
                                        rhs=kb[hb][prange,
                                                   rp * 1024 + n2 * 512:rp * 1024 + (n2 + 1) * 512],
                                        start=True, stop=(n2 == 1),
                                        skip_group_check=True)
                                p.inc(last, "scmm")
                            block.tensor(t_sc)

                            def v_dg(v, ql=ql, si=si, rp=rp):
                                p.wait(v, "scmm")
                                i = v.tensor_tensor(
                                    out=sc_ps[rp % 2][:, ql * 128:(ql + 1) * 128],
                                    in0=sc_ps[rp % 2][:, ql * 128:(ql + 1) * 128],
                                    in1=maskb[:, rp * 128:(rp + 1) * 128],
                                    op=ALU.add)
                                p.inc(i, "dmask")
                            block.vector(v_dg)

                            for ri, (lo, hi, shift) in enumerate(rrs):
                                ci = rp * 3 + ri

                                def v_mx(v, si=si, rp=rp, ri=ri, lo=lo, hi=hi,
                                         ci=ci, fm=fm):
                                    p.wait(v, "dmask")
                                    i = v.reduce_max(out=fm[:, ci:ci + 1],
                                                     in_=sc_ps[rp % 2][:, lo:hi],
                                                     axis=AX, negate=True)
                                    p.inc(i, "mx")
                                block.vector(v_mx)

                                def s_exp(s, si=si, rp=rp, ri=ri, lo=lo, hi=hi,
                                          ci=ci, fm=fm, pdst=pdst):
                                    p.wait(s, "mx")
                                    i = s.activation(
                                        out=pdst[:, rp * 1024 + lo:rp * 1024 + hi],
                                        in_=sc_ps[rp % 2][:, lo:hi], func=AF.Exp,
                                        bias=fm[:, ci:ci + 1],
                                        accum_out=fm[:, 24 + ci:25 + ci])
                                    p.inc(i, "expd")
                                block.scalar(s_exp)

                        def v_nmc(v, si=si, fm=fm, rrs=rrs, ql=ql):
                            p.wait(v, "expd")
                            for rp in range(4):
                                for ri, (lo, hi, shift) in enumerate(rrs):
                                    ci = rp * 3 + ri
                                    v.tensor_scalar(out=fm[:, 12 + ci:13 + ci],
                                                    in0=fm[:, ci:ci + 1],
                                                    scalar1=float(shift),
                                                    scalar2=None, op0=ALU.add)
                            v.drain()
                            i = v.tensor_reduce(out=fm[:, 48:49],
                                                in_=fm[:, 12:12 + 12], axis=AX,
                                                op=ALU.min)
                            p.inc(i, "mn")
                        block.vector(v_nmc)

                    for si in range(2):
                        fm = fmb[si]

                        def s_f(s, si=si, fm=fm):
                            p.wait(s, "mn")
                            i = s.activation(out=fm[:, 36:48], in_=fm[:, 12:24],
                                             func=AF.Exp, scale=-1.0,
                                             bias=fm[:, 48:49])
                            p.inc(i, "fe")
                        block.scalar(s_f)

                        def v_z(v, si=si, fm=fm):
                            p.wait(v, "fe")
                            v.tensor_tensor(out=fm[:, 24:36], in0=fm[:, 24:36],
                                            in1=fm[:, 36:48], op=ALU.mult)
                            v.drain()
                            v.reduce_sum(out=fm[:, 49:50], in_=fm[:, 24:36], axis=AX)
                            v.drain()
                            i = v.reciprocal(out=fm[:, 50:51], in_=fm[:, 49:50])
                            p.inc(i, "zr")
                        block.vector(v_z)

                    def v_resc(v, h=h, g=g, l=l, ql=ql, rrs=rrs):
                        p.wait(v, "zr")
                        v.tensor_scalar(out=fmb[0][:, 36:48], in0=fmb[0][:, 36:48],
                                        scalar1=fmb[0][:, 50:51], scalar2=None,
                                        op0=ALU.mult)
                        v.tensor_scalar(out=fmb[1][:, 36:48], in0=fmb[1][:, 36:48],
                                        scalar1=fmb[1][:, 50:51], scalar2=None,
                                        op0=ALU.mult)
                        v.drain()
                        v.tensor_scalar(out=fmb[1][:, 36:48], in0=fmb[1][:, 36:48],
                                        scalar1=nlB[:], scalar2=None, op0=ALU.mult)
                        v.drain()
                        i = None
                        for rp in range(4):
                            for ri, (lo, hi, shift) in enumerate(rrs):
                                ci = rp * 3 + ri
                                sl = slice(rp * 1024 + lo, rp * 1024 + hi)
                                v.tensor_scalar(out=ptile[l][:, sl],
                                                in0=ptile[l][:, sl],
                                                scalar1=fmb[0][:, 36 + ci:37 + ci],
                                                scalar2=None, op0=ALU.mult)
                                i = v.scalar_tensor_tensor(
                                    out=ptile[l][:, sl], in0=p2t[:, sl],
                                    scalar=fmb[1][:, 36 + ci:37 + ci],
                                    in1=ptile[l][:, sl], op0=ALU.mult, op1=ALU.add)
                        v.drain()
                        p.inc(i, "resc")
                    block.vector(v_resc)

                evval = {}
                for kt in range(33):
                    if kt < 32:
                        def t_tp(t, kt=kt, g=g, h=h):
                            if kt == 0:
                                p.wait(t, "resc")
                                if not (h == 0 and g == 0):
                                    p.wait(t, "aoev")
                            if kt >= 2:
                                p.wge(t, "ptevV" if kt % 2 == 0 else "ptevS", evval[kt - 2])
                            last = None
                            for l in range(4):
                                last = t.transpose(
                                    out=ptb[kt % 2][:, l * 128:(l + 1) * 128],
                                    in_=ptile[l][:, kt * 128:(kt + 1) * 128],
                                    identity=sidb[:])
                            p.inc(last, "tpmm")
                        block.tensor(t_tp)

                        if kt % 2 == 0:
                            def v_ptev(v, kt=kt):
                                p.wait(v, "tpmm")
                                evval[kt] = p.inc(v.tensor_copy(
                                    out=pts[kt % 2][:], in_=ptb[kt % 2][:]), "ptevV")
                            block.vector(v_ptev)
                        else:
                            def s_ptev(s, kt=kt):
                                p.wait(s, "tpmm")
                                evval[kt] = p.inc(s.copy(
                                    out=pts[kt % 2][:], in_=ptb[kt % 2][:]), "ptevS")
                            block.scalar(s_ptev)

                    if kt >= 1:
                        def t_pv(t, kt=kt, h=h, hb=hb):
                            kk = kt - 1
                            p.wge(t, "ptevV" if kk % 2 == 0 else "ptevS", evval[kk])
                            i = t.matmul(out=pv_ps,
                                         lhsT=vb[hb][:, kk * 128:(kk + 1) * 128],
                                         rhs=pts[kk % 2][:],
                                         start=(kk == 0), stop=(kk == 31),
                                         skip_group_check=True)
                            if kk == 31:
                                p.inc(i, "pvdn")
                        block.tensor(t_pv)

                def s_yev(s, h=h, g=g):
                    p.wait(s, "pvdn")
                    p.wge(s, "yst", max(0, (h * 2 + g - 1) * 16))
                    i = s.copy(out=ystage[g % 2][:], in_=pv_ps)
                    p.inc(i, "ysev")
                block.scalar(s_yev)

                def v_ao(v, h=h, g=g):
                    p.wait(v, "ysev")
                    i = v.tensor_copy(
                        out=ao[:, h * TLOC + g * 512:h * TLOC + (g + 1) * 512],
                        in_=pv_ps)
                    p.inc(i, "aoev")
                block.vector(v_ao)

                def st_y(sync, h=h, g=g):
                    p.wait(sync, "ysev")
                    i = sync.dma_start(
                        out=o_y[h * 128:(h + 1) * 128, g * 512:(g + 1) * 512],
                        in_=ystage[g % 2][:])
                    p.inc(i, "yst", 16)
                block.sync(st_y)

            def g_hdone(g_, h=h):
                p.wge(g_, "aoev", (h + 1) * 2)
                p.inc(g_.engine_nop(), "hdone")
            block.gpsimd(g_hdone)

        def t_gn1(t):
            p.wait(t, "cst")
            p.wait(t, "aoev")
            p.wait(t, "expd")
            for tc in range(2):
                i = None
                for j in range(NH):
                    i = t.matmul(out=psA[0:1, tc * 512:(tc + 1) * 512],
                                 lhsT=ones_cb[:],
                                 rhs=ao[:, j * TLOC + tc * 512:j * TLOC + (tc + 1) * 512],
                                 start=(j == 0), stop=(j == NH - 1),
                                 skip_group_check=True)
                p.inc(i, "gn1")
        block.tensor(t_gn1)

        def s_gn1(s):
            p.wait(s, "gn1")
            p.inc(s.copy(out=prow[:], in_=psA[0:1, :]), "gn1e")
        block.scalar(s_gn1)

        def v_seg1(v):
            p.wait(v, "gn1e")
            for gq in range(16):
                v.reduce_sum(out=part[0:1, gq:gq + 1],
                             in_=prow[0:1, gq * 64:(gq + 1) * 64], axis=AX)
            p.inc(v.drain(), "seg1")
        block.vector(v_seg1)

        for j in range(NH):
            def s_sqj(s, j=j):
                p.wait(s, "aoev")
                if j >= 1:
                    p.wait(s, "gn2")
                i = s.activation(out=sq_s[:],
                                 in_=ao[:, j * TLOC:(j + 1) * TLOC], func=AF.Square)
                p.inc(i, "aosqj")
            block.scalar(s_sqj)

            def t_gn2(t, j=j):
                p.wait(t, "aosqj")
                p.wait(t, "seg1")
                i = None
                for tc in range(2):
                    i = t.matmul(out=psB[0:1, tc * 512:(tc + 1) * 512],
                                 lhsT=ones_cb[:],
                                 rhs=sq_s[:, tc * 512:(tc + 1) * 512],
                                 start=(j == 0), stop=(j == NH - 1),
                                 skip_group_check=True)
                p.inc(i, "gn2")
            block.tensor(t_gn2)

        def s_gn2(s):
            p.wait(s, "gn2")
            p.inc(s.copy(out=prow[:], in_=psB[0:1, :]), "gn2e")
        block.scalar(s_gn2)

        def v_seg2(v):
            p.wait(v, "gn2e")
            for gq in range(16):
                v.reduce_sum(out=part[0:1, 16 + gq:16 + gq + 1],
                             in_=prow[0:1, gq * 64:(gq + 1) * 64], axis=AX)
            p.inc(v.drain(), "seg2")
        block.vector(v_seg2)

        def st_part(sync):
            p.wait(sync, "seg2")
            p.inc(sync.dma_start(out=o_part, in_=part[:]), "pst", 16)
        block.sync(st_part)

        def fin(sync):
            p.wait(sync, "yst")
            p.wait(sync, "pst")
        block.sync(fin)





def emit_l1attn_b(p, block, io):
    nc = p.nc
    y_in = io["y"]
    part_all = io["part_all"]
    gamrow = io["gamrow"]
    betrow = io["betrow"]
    woT = io["woT"]
    E16 = io["E16"]
    identf = io["identf"]
    o_s1 = io["s1"]

    NH = 16
    NTOK = 2 * E * TLOC * 4 // 16 // 1
    NCHK = 524288.0

    yf = p.sb("yf", [128, NH * TLOC], F32)
    ybf = p.sb("ybf", [128, NH * TLOC], BF16)
    pa = p.sb("pa", [4, 32], F32)
    e16 = p.sb("e16", [16, TLOC], F32)
    grow = p.sb("grow", [1, TLOC], F32)
    brow = p.sb("brow", [1, TLOC], F32)
    rows = p.sb("rows2", [1, 6 * 64], F32)
    mcol = p.sb("mcol", [16, 1], F32)
    rcol = p.sb("rcol2", [16, 1], F32)
    arow = p.sb("arow", [1, TLOC], F32)
    crow = p.sb("crow", [1, TLOC], F32)
    AB = p.sb("AB", [128, TLOC], F32)
    CB = p.sb("CB", [128, TLOC], F32)
    onesf4 = p.sb("onesf4", [4, 1], F32)
    ones_rf = p.sb("ones_rf", [1, 128], F32)
    oneb = p.sb("oneb", [1, 1], F32)
    sid = p.sb("sid", [128, 128], F32)
    wos = [p.sb(f"wos{i}", [128, NH * 512], BF16) for i in range(2)]
    stage = [p.sb(f"sstg{i}", [128, 512], F32) for i in range(2)]

    SM16 = slice(0, 16)
    SV16 = slice(64, 80)
    SR16 = slice(128, 144)
    SME = slice(192, 192 + 0)

    with \
         nc.psum_tensor(p.un("psA"), [128, TLOC], F32) as psA, \
         nc.psum_tensor(p.un("psB"), [128, TLOC], F32) as psB:

        def ld0(sync):
            i = sync.dma_start(out=yf[:].rearrange("pp (j t) -> pp j t", j=NH),
                               in_=y_in.rearrange("(j pp) t -> pp j t", pp=128))
            p.inc(i, "ld", 16)
            p.inc(sync.dma_start(out=pa[:], in_=part_all), "ld", 16)
            p.inc(sync.dma_start(out=e16[:], in_=E16), "ld", 16)
            p.inc(sync.dma_start(out=grow[:], in_=gamrow), "ld", 16)
            p.inc(sync.dma_start(out=brow[:], in_=betrow), "ld", 16)
            p.inc(sync.dma_start(out=sid[:], in_=identf), "ld", 16)
        block.sync(ld0)
        ld_done = p.c("ld")

        def v0(v):
            v.memset(onesf4[:], 1.0)
            v.memset(ones_rf[:], 1.0)
            v.memset(oneb[:], 1.0)
            p.inc(v.drain(), "cst")
        block.vector(v0)

        def t_pc(t):
            p.wait(t, "cst")
            p.wait(t, "ld", ld_done)
            p.inc(t.matmul(out=psA[0:1, 0:32], lhsT=onesf4[:], rhs=pa[:],
                           start=True, stop=True, skip_group_check=True), "pcm")
        block.tensor(t_pc)

        def s_pc(s):
            p.wait(s, "pcm")
            s.activation(out=rows[0:1, 0:16], in_=psA[0:1, 0:16], func=AF.Identity,
                         scale=1.0 / NCHK)
            p.inc(s.activation(out=rows[0:1, 64:80], in_=psA[0:1, 16:32],
                               func=AF.Identity, scale=1.0 / NCHK), "pce")
        block.scalar(s_pc)

        def v_var(v):
            p.wait(v, "pce")
            v.tensor_tensor(out=rows[0:1, 128:144], in0=rows[0:1, 0:16],
                            in1=rows[0:1, 0:16], op=ALU.mult)
            v.drain()
            v.tensor_tensor(out=rows[0:1, 64:80], in0=rows[0:1, 64:80],
                            in1=rows[0:1, 128:144], op=ALU.subtract)
            v.drain()
            p.inc(v.tensor_scalar(out=rows[0:1, 64:80], in0=rows[0:1, 64:80],
                                  scalar1=1e-5, scalar2=None, op0=ALU.add), "var")
        block.vector(v_var)

        def s_std(s):
            p.wait(s, "var")
            p.inc(s.activation(out=rows[0:1, 64:80], in_=rows[0:1, 64:80],
                               func=AF.Sqrt), "std")
        block.scalar(s_std)

        def v_rstd(v):
            p.wait(v, "std")
            p.inc(v.reciprocal(out=rows[0:1, 128:144], in_=rows[0:1, 64:80]), "rstd")
        block.vector(v_rstd)

        def t_mc(t):
            p.wait(t, "rstd")
            t.matmul(out=psA[0:16, 32:33], lhsT=rows[0:1, 0:16], rhs=oneb[:],
                     start=True, stop=True, skip_group_check=True)
            p.inc(t.matmul(out=psA[0:16, 33:34], lhsT=rows[0:1, 128:144], rhs=oneb[:],
                           start=True, stop=True, skip_group_check=True), "mcm")
        block.tensor(t_mc)

        def s_mc(s):
            p.wait(s, "mcm")
            s.copy(out=mcol[:], in_=psA[0:16, 32:33])
            p.inc(s.copy(out=rcol[:], in_=psA[0:16, 33:34]), "mce")
        block.scalar(s_mc)

        def t_me(t):
            p.wait(t, "mce")
            i = None
            for tc in range(2):
                t.matmul(out=psA[0:1, tc * 512:(tc + 1) * 512], lhsT=mcol[:],
                         rhs=e16[:, tc * 512:(tc + 1) * 512], start=True, stop=True,
                         skip_group_check=True)
                i = t.matmul(out=psB[0:1, tc * 512:(tc + 1) * 512], lhsT=rcol[:],
                             rhs=e16[:, tc * 512:(tc + 1) * 512], start=True,
                             stop=True, skip_group_check=True)
            p.inc(i, "mem")
        block.tensor(t_me)

        def s_me(s):
            p.wait(s, "mem")
            s.copy(out=arow[:], in_=psB[0:1, :])
            p.inc(s.copy(out=crow[:], in_=psA[0:1, :]), "mee")
        block.scalar(s_me)

        def v_ac(v):
            p.wait(v, "mee")
            v.tensor_tensor(out=arow[:], in0=arow[:], in1=grow[:], op=ALU.mult)
            v.drain()
            v.tensor_tensor(out=crow[:], in0=crow[:], in1=arow[:], op=ALU.mult)
            v.drain()
            v.tensor_tensor(out=crow[:], in0=brow[:], in1=crow[:], op=ALU.subtract)
            v.drain()
            v.tensor_scalar(out=arow[:], in0=arow[:], scalar1=0.2, scalar2=None,
                            op0=ALU.mult)
            v.tensor_scalar(out=crow[:], in0=crow[:], scalar1=0.2, scalar2=None,
                            op0=ALU.mult)
            p.inc(v.drain(), "ac")
        block.vector(v_ac)

        def t_acb(t):
            p.wait(t, "ac")
            i = None
            for tc in range(2):
                t.matmul(out=psA[:, tc * 512:(tc + 1) * 512], lhsT=ones_rf[:],
                         rhs=arow[0:1, tc * 512:(tc + 1) * 512], start=True,
                         stop=True, skip_group_check=True)
                i = t.matmul(out=psB[:, tc * 512:(tc + 1) * 512], lhsT=ones_rf[:],
                             rhs=crow[0:1, tc * 512:(tc + 1) * 512], start=True,
                             stop=True, skip_group_check=True)
            p.inc(i, "acb")
        block.tensor(t_acb)

        def s_acb(s):
            p.wait(s, "acb")
            s.copy(out=AB[:], in_=psA[:])
            p.inc(s.copy(out=CB[:], in_=psB[:]), "abe")
        block.scalar(s_acb)

        def v_yb(v):
            p.wait(v, "abe")
            for j in range(NH):
                sl = slice(j * TLOC, (j + 1) * TLOC)
                v.tensor_tensor(out=yf[:, sl], in0=yf[:, sl], in1=AB[:], op=ALU.mult)
                v.drain()
                v.tensor_tensor(out=ybf[:, sl], in0=yf[:, sl], in1=CB[:], op=ALU.add)
                v.drain()
            p.inc(v.engine_nop(), "ybd")
        block.vector(v_yb)

        for ocg in range(4):
            def ldwo(g_, ocg=ocg):
                if ocg >= 2:
                    p.wge(g_, "womm", (ocg - 1) * 8)
                for j in range(NH):
                    i = g_.dma_start(
                        out=wos[ocg % 2][:, j * 512:(j + 1) * 512],
                        in_=woT[j * 128:(j + 1) * 128, ocg * 512:(ocg + 1) * 512])
                    p.inc(i, "wold", 16)
            block.gpsimd(ldwo)

            def t_wo(t, ocg=ocg):
                if ocg == 0:
                    p.wait(t, "ybd")
                p.wge(t, "wold", (ocg + 1) * NH * 16)
                for tt in range(8):
                    k = ocg * 8 + tt
                    if k >= 2:
                        p.wge(t, "woev", k - 1)
                    last = None
                    for j in range(NH):
                        last = t.matmul(
                            out=(psA if k % 2 == 0 else psB)[:, 0:512],
                            lhsT=ybf[:, j * TLOC + tt * 128:j * TLOC + (tt + 1) * 128],
                            rhs=wos[ocg % 2][:, j * 512:(j + 1) * 512],
                            start=(j == 0), stop=(j == NH - 1),
                            skip_group_check=True)
                    p.inc(last, "womm")
            block.tensor(t_wo)

            def s_wo(s, ocg=ocg):
                for tt in range(8):
                    k = ocg * 8 + tt
                    p.wge(s, "womm", k + 1)
                    if k >= 2:
                        p.wge(s, "s1st", (k - 1) * 16)
                    i = s.copy(out=stage[k % 2][:, 0:512],
                               in_=(psA if k % 2 == 0 else psB)[:, 0:512])
                    p.inc(i, "woev")
            block.scalar(s_wo)

            def st_wo(sync, ocg=ocg):
                for tt in range(8):
                    k = ocg * 8 + tt
                    p.wge(sync, "woev", k + 1)
                    i = sync.dma_start(
                        out=o_s1[tt * 128:(tt + 1) * 128, ocg * 512:(ocg + 1) * 512],
                        in_=stage[k % 2][:, 0:512])
                    p.inc(i, "s1st", 16)
            block.sync(st_wo)

        def fin(sync):
            p.wait(sync, "s1st")
        block.sync(fin)





def emit_rms_fm(p, block, tag, xnt_ap, wcol, dst, ptb, sidb, xtile, lnsq, qcol, xbt,
                pre_wait=None, wcol_sem=None, wcol_val=0):
    for tt in range(8):
        def ldx(sync, tt=tt):
            if tt == 0 and pre_wait is not None:
                pre_wait(sync)
            if tt >= 2:
                p.wge(sync, tag + "xd", tt - 1)
            i = sync.dma_start(out=xtile[tt % 2][:],
                               in_=xnt_ap[tt * 128:(tt + 1) * 128, :])
            p.inc(i, tag + "xld", 16)
        block.sync(ldx)

        def vrms(v, tt=tt):
            p.wge(v, tag + "xld", (tt + 1) * 16)
            xt = xtile[tt % 2]
            v.tensor_tensor(out=lnsq[:], in0=xt[:], in1=xt[:], op=ALU.mult)
            v.drain()
            v.reduce_sum(out=qcol[:, 0:1], in_=lnsq[:], axis=AX)
            v.drain()
            v.tensor_scalar(out=qcol[:, 0:1], in0=qcol[:, 0:1], scalar1=1.0 / E,
                            scalar2=1e-6, op0=ALU.mult, op1=ALU.add)
            p.inc(v.drain(), tag + "ms")
        block.vector(vrms)

        def sstd(s, tt=tt):
            p.wait(s, tag + "ms")
            p.inc(s.activation(out=qcol[:, 1:2], in_=qcol[:, 0:1], func=AF.Sqrt),
                  tag + "sd")
        block.scalar(sstd)

        def vxx(v, tt=tt):
            p.wait(v, tag + "sd")
            v.reciprocal(out=qcol[:, 1:2], in_=qcol[:, 1:2])
            v.drain()
            v.tensor_scalar(out=xtile[tt % 2][:], in0=xtile[tt % 2][:],
                            scalar1=qcol[:, 1:2], scalar2=None, op0=ALU.mult)
            v.drain()
            p.inc(v.tensor_copy(out=xbt[:], in_=xtile[tt % 2][:]), tag + "xb")
        block.vector(vxx)

        for j in range(NCH):
            def ttp(t, tt=tt, j=j):
                k = tt * NCH + j
                p.wait(t, tag + "xb")
                p.wge(t, tag + "fe", max(0, p.c(tag + "fe") - 1))
                i = t.transpose(out=ptb[k % 2][:],
                                in_=xbt[:, j * 128:(j + 1) * 128],
                                identity=sidb[:])
                p.inc(i, tag + "tp")
            block.tensor(ttp)

            def vev(v, tt=tt, j=j):
                k = tt * NCH + j
                if k == 0 and wcol_sem is not None:
                    p.wge(v, wcol_sem, wcol_val)
                p.wge(v, tag + "tp", k + 1)
                v.tensor_copy(out=dst[:, j * TLOC + tt * 128:j * TLOC + (tt + 1) * 128],
                              in_=ptb[k % 2][:])
                v.drain()
                i = v.tensor_scalar(
                    out=dst[:, j * TLOC + tt * 128:j * TLOC + (tt + 1) * 128],
                    in0=dst[:, j * TLOC + tt * 128:j * TLOC + (tt + 1) * 128],
                    scalar1=wcol[:, j:j + 1], scalar2=None, op0=ALU.mult)
                p.inc(i, tag + "fe")
            block.vector(vev)

        def vdn(v, tt=tt):
            p.wait(v, tag + "fe")
            p.inc(v.engine_nop(), tag + "xd")
        block.vector(vdn)




def emit_l1fpre(p, block, io):
    nc = p.nc
    xnt1 = io["xnt1"]
    ln1w = io["ln1w"]
    WqT = io["WqT"]
    WkT = io["WkT"]
    WvT = io["WvT"]
    identb = io["identb"]
    o_q = io["qf2"]
    o_k = io["kf2"]
    o_v = io["vt2"]

    SCALER = float(128.0 ** (-0.5))

    xtile = [p.sb(f"xtile{i}", [128, E], F32) for i in range(2)]
    lnsq = p.sb("lnsq", [128, E], F32)
    xbt = p.sb("xbt", [128, E], BF16)
    qcol = p.sb("qcol", [128, 4], F32)
    wcol = p.sb("wcol", [128, 16], F32)
    xxf = p.sb("xxf", [128, NCH * TLOC], BF16)
    sidb = p.sb("sidb", [128, 128], BF16)
    wtil2 = [[p.sb(f"wt{w}_{i}", [128, 256], BF16) for i in range(2)]
             for w in range(2)]
    stage = [p.sb(f"qstg{i}", [128, TLOC], F32) for i in range(2)]
    wvs = [p.sb(f"wvs{i}", [128, NCH * 512], BF16) for i in range(2)]
    vstage = [p.sb(f"vstg{i}", [128, 512], F32) for i in range(2)]

    with \
         nc.psum_tensor(p.un("psA"), [128, TLOC], F32) as psA, \
         nc.psum_tensor(p.un("psB"), [128, TLOC], F32) as psB, \
         nc.psum_tensor(p.un("psD"), [128, TLOC], F32) as psD, \
         nc.psum_tensor(p.un("ptb0"), [128, 128], BF16) as ptb0, \
         nc.psum_tensor(p.un("ptb1"), [128, 128], BF16) as ptb1:

        qk_ps = [psA, psB]
        vv_ps = [psD[:, 0:512], psD[:, 512:1024]]
        ptb = [ptb0, ptb1]

        def ld0(sync):
            p.inc(sync.dma_start(out=sidb[:], in_=identb), "ld", 16)
            p.inc(sync.dma_start(out=wcol[:], in_=ln1w), "ld", 16)
        block.sync(ld0)
        ld_done = p.c("ld")

        def vw(v):
            p.wait(v, "ld", ld_done)
            p.inc(v.drain(), "wc")
        block.vector(vw)

        emit_rms_fm(p, block, "r1", xnt1, wcol, xxf, ptb, sidb, xtile, lnsq, qcol,
                    xbt, wcol_sem="ld", wcol_val=ld_done)

        for wi, (Wap, dsto, scale) in enumerate(((WqT, o_q, SCALER), (WkT, o_k, None))):
            wblocks = [(j * 128, 128, ocg * 256, 256)
                       for ocg in range(8) for j in range(NCH)]
            wtil = wtil2[wi]
            ld_s = emit_weight_stream(
                p, block, f"w{wi}", Wap, wblocks,
                lambda k, wtil=wtil: wtil[k % 2][:],
                tile_free_sem=f"w{wi}use", tile_depth=2)

            def tqk(t, wi=wi, ld_s=ld_s, wtil=wtil2[wi]):
                if wi == 0:
                    p.wait(t, "r1fe")
                for ocg in range(8):
                    if ocg >= 1 or wi == 1:
                        p.wge(t, "qkev", wi * 16 + ocg * 2)
                    for j in range(NCH):
                        k = ocg * NCH + j
                        p.wge(t, ld_s, (k + 1) * 16)
                        last = None
                        for oo in range(2):
                            for tc in range(2):
                                last = t.matmul(
                                    out=qk_ps[oo][:, tc * 512:(tc + 1) * 512],
                                    lhsT=wtil[k % 2][:, oo * 128:(oo + 1) * 128],
                                    rhs=xxf[:, j * TLOC + tc * 512:j * TLOC + (tc + 1) * 512],
                                    start=(j == 0), stop=(j == NCH - 1),
                                    skip_group_check=True)
                        p.inc(last, f"w{wi}use")
            block.tensor(tqk)

            def sev(s, wi=wi, dsto=dsto, scale=scale):
                for ocg in range(8):
                    p.wge(s, f"w{wi}use", (ocg + 1) * NCH)
                    for oo in range(2):
                        oc = ocg * 2 + oo
                        kk = wi * 16 + oc
                        if kk >= 2:
                            p.wge(s, "qkst", (kk - 1) * 16)
                        if scale is not None:
                            i = s.activation(out=stage[oc % 2][:], in_=qk_ps[oo][:],
                                             func=AF.Identity, scale=scale)
                        else:
                            i = s.copy(out=stage[oc % 2][:], in_=qk_ps[oo][:])
                        p.inc(i, "qkev")
                block.scalar if False else None
            block.scalar(sev)

            def stq(sync, wi=wi, dsto=dsto):
                for oc in range(16):
                    kk = wi * 16 + oc
                    p.wge(sync, "qkev", kk + 1)
                    i = sync.dma_start(out=dsto[oc * 128:(oc + 1) * 128, :],
                                       in_=stage[oc % 2][:])
                    p.inc(i, "qkst", 16)
            block.sync(stq)

        vblocks = [(j * 128, 128, vg * 512, 512)
                   for vg in range(4) for j in range(NCH)]
        ld_wv = emit_weight_stream(
            p, block, "wv2", WvT, vblocks,
            lambda k: wvs[(k // NCH) % 2][:, (k % NCH) * 512:((k % NCH) + 1) * 512],
            tile_free_sem="wv2use", tile_depth=2 * NCH)

        def tv(t):
            p.wait(t, "w1use")
            for vg in range(4):
                p.wge(t, ld_wv, (vg + 1) * NCH * 16)
                for tt in range(8):
                    k = vg * 8 + tt
                    if k >= 2:
                        p.wge(t, "vev", k - 1)
                    last = None
                    for j in range(NCH):
                        last = t.matmul(
                            out=vv_ps[k % 2],
                            lhsT=xxf[:, j * TLOC + tt * 128:j * TLOC + (tt + 1) * 128],
                            rhs=wvs[vg % 2][:, j * 512:(j + 1) * 512],
                            start=(j == 0), stop=(j == NCH - 1))
                    if tt == 7:
                        p.inc(last, "wv2use", NCH)
                    else:
                        p.inc(last, "vmm")
        block.tensor(tv)

        def sv(s):
            for vg in range(4):
                for tt in range(8):
                    k = vg * 8 + tt
                    if tt != 7:
                        p.wge(s, "vmm", vg * 7 + tt + 1)
                    else:
                        p.wge(s, "wv2use", (vg + 1) * NCH)
                    if k >= 2:
                        p.wge(s, "vts", (k - 1) * 16)
                    i = s.copy(out=vstage[k % 2][:], in_=vv_ps[k % 2])
                    p.inc(i, "vev")
        block.scalar(sv)

        def stv(sync):
            for vg in range(4):
                for tt in range(8):
                    k = vg * 8 + tt
                    p.wge(sync, "vev", k + 1)
                    i = sync.dma_start(
                        out=o_v[tt * 128:(tt + 1) * 128, vg * 512:(vg + 1) * 512],
                        in_=vstage[k % 2][:])
                    p.inc(i, "vts", 16)
        block.sync(stv)

        def fin(sync):
            p.wait(sync, "qkst")
            p.wait(sync, "vts")
        block.sync(fin)





def emit_l1ffn(p, block, io):
    nc = p.nc
    qf2 = io["qf2"]
    kfull = io["kfull2"]
    vfull = io["vfull2"]
    maskF = io["maskF"]
    WoT2 = io["WoT2"]
    Amat = io["Amat"]
    Bmat = io["Bmat"]
    WinT = io["WinT"]
    WupT = io["WupT"]
    WdnT = io["WdnT"]
    ln2w = io["ln2w"]
    xnt1 = io["xnt1"]
    x1t = io["x1t"]
    s1 = io["s1"]
    identb = io["identb"]
    o_x2 = io["x2t"]
    guF = io["guF"]

    NH = 16

    sidb = p.sb("sidb", [128, 128], BF16)
    ao2 = p.sb("ao2", [128, NH * TLOC], BF16)

    attn_ctx = contextlib.ExitStack()

    def asb(name, shape, dt):
        return attn_ctx.enter_context(nc.sbuf_tensor(name, list(shape), dt))

    kb = [asb(f"kb{i}", [128, 4096], BF16) for i in range(2)]
    vb = [asb(f"vb{i}", [128, 4096], BF16) for i in range(2)]
    qb = [asb(f"qb{i}", [128, TLOC], BF16) for i in range(2)]
    ptile = [asb(f"ptile{i}", [128, 4096], BF16) for i in range(2)]
    pts = [asb(f"pts{i}", [128, 128], BF16) for i in range(2)]
    fm = asb("fm", [128, 28], F32)
    maskb = asb("maskb", [128, 512], F32)

    with \
         nc.psum_tensor(p.un("psA"), [128, TLOC], F32) as psA, \
         nc.psum_tensor(p.un("psB"), [128, TLOC], F32) as psB, \
         nc.psum_tensor(p.un("psC"), [128, TLOC], F32) as psC:

        ptb_ctx = contextlib.ExitStack()
        ptb = [ptb_ctx.enter_context(nc.psum_tensor(f"ptb{i}", [128, 128], BF16))
               for i in range(2)]

        sc_ps = [psA, psB]
        pv_ps = psC[:, 0:128]

        def ld0(sync):
            p.inc(sync.dma_start(out=sidb[:], in_=identb), "ld", 16)
            p.inc(sync.dma_start(
                out=maskb[:].rearrange("pp (r c) -> pp r c", c=128),
                in_=maskF.rearrange("(r pp) c -> pp r c", pp=128)), "ld", 16)
        block.sync(ld0)
        ld_done = p.c("ld")

        def v0(v):
            p.wait(v, "ld", ld_done)
            p.inc(v.drain(), "cst")
        block.vector(v0)

        for h in range(NH):
            hb = h % 2

            def g_ld(g, h=h, hb=hb):
                if h >= 2:
                    p.wge(g, "hdone", h - 1)
                for rp in range(4):
                    i = g.dma_start(
                        out=kb[hb][:, rp * 1024:(rp + 1) * 1024],
                        in_=kfull[rp * 2048 + h * 128:rp * 2048 + (h + 1) * 128, :])
                    p.inc(i, "kld", 16)
                i = g.dma_start(
                    out=vb[hb][:].rearrange("pp (kt d) -> pp kt d", d=128),
                    in_=vfull.rearrange("(kt pp) (hh d) -> hh pp kt d",
                                        pp=128, d=128)[h])
                p.inc(i, "vld", 16)
                i = g.dma_start(out=qb[hb][:], in_=qf2[h * 128:(h + 1) * 128, :])
                p.inc(i, "qld", 16)
            block.gpsimd(g_ld)

            for ql in range(8):
                lb = ql % 2
                rrs = ([(0, ql * 128, 0)] if ql > 0 else []) + \
                      [(ql * 128, (ql + 1) * 128, 1)]

                def v_fz(v, h=h, ql=ql):
                    p.wait(v, "resc")
                    v.memset(fm[:, 0:8], 1e30)
                    v.memset(fm[:, 8:16], 0.0)
                    p.inc(v.drain(), "fz")
                block.vector(v_fz)

                for rp in range(4):
                    def t_sc(t, h=h, hb=hb, ql=ql, rp=rp):
                        if ql == 0 and rp == 0:
                            p.wge(t, "kld", (h + 1) * 64)
                            p.wge(t, "qld", (h + 1) * 16)
                            p.wait(t, "cst")
                            if h >= 1:
                                p.wait(t, "aoev")
                        if rp == 0:
                            p.wait(t, "tpmm")
                        p.wait(t, "expd", max(0, p.c("expd") - 1))
                        hi = (ql + 1) * 128
                        last = None
                        for n2 in range((hi + 511) // 512):
                            w = min(512, hi - n2 * 512)
                            last = t.matmul(
                                out=sc_ps[rp % 2][:, n2 * 512:n2 * 512 + w],
                                lhsT=qb[hb][:, ql * 128:(ql + 1) * 128],
                                rhs=kb[hb][:, rp * 1024 + n2 * 512:rp * 1024 + n2 * 512 + w],
                                start=True, stop=True, skip_group_check=True)
                        p.inc(last, "scmm")
                    block.tensor(t_sc)

                    def v_dg(v, ql=ql, rp=rp):
                        p.wait(v, "scmm")
                        p.wait(v, "fz")
                        i = v.tensor_tensor(
                            out=sc_ps[rp % 2][:, ql * 128:(ql + 1) * 128],
                            in0=sc_ps[rp % 2][:, ql * 128:(ql + 1) * 128],
                            in1=maskb[:, rp * 128:(rp + 1) * 128], op=ALU.add)
                        p.inc(i, "dmask")
                    block.vector(v_dg)

                    for ri, (lo, hi, isd) in enumerate(rrs):
                        ci = rp * 2 + ri

                        def v_mx(v, rp=rp, ri=ri, lo=lo, hi=hi, ci=ci):
                            p.wait(v, "dmask")
                            i = v.reduce_max(out=fm[:, ci:ci + 1],
                                             in_=sc_ps[rp % 2][:, lo:hi],
                                             axis=AX, negate=True)
                            p.inc(i, "mx")
                        block.vector(v_mx)

                        def s_exp(s, ql=ql, rp=rp, ri=ri, lo=lo, hi=hi, ci=ci):
                            p.wait(s, "mx")
                            i = s.activation(
                                out=ptile[ql % 2][:, rp * 1024 + lo:rp * 1024 + hi],
                                in_=sc_ps[rp % 2][:, lo:hi], func=AF.Exp,
                                bias=fm[:, ci:ci + 1],
                                accum_out=fm[:, 8 + ci:9 + ci])
                            p.inc(i, "expd")
                        block.scalar(s_exp)

                def v_row(v, h=h, ql=ql):
                    p.wait(v, "expd")
                    i = v.tensor_reduce(out=fm[:, 24:25], in_=fm[:, 0:8], axis=AX,
                                        op=ALU.min)
                    p.inc(i, "mn")
                block.vector(v_row)

                def s_f(s, h=h, ql=ql):
                    p.wait(s, "mn")
                    i = s.activation(out=fm[:, 16:24], in_=fm[:, 0:8], func=AF.Exp,
                                     scale=-1.0, bias=fm[:, 24:25])
                    p.inc(i, "fe")
                block.scalar(s_f)

                def v_z(v, h=h, ql=ql, rrs=rrs):
                    p.wait(v, "fe")
                    v.tensor_tensor(out=fm[:, 8:16], in0=fm[:, 8:16], in1=fm[:, 16:24],
                                    op=ALU.mult)
                    v.drain()
                    v.reduce_sum(out=fm[:, 25:26], in_=fm[:, 8:16], axis=AX)
                    v.drain()
                    v.reciprocal(out=fm[:, 26:27], in_=fm[:, 25:26])
                    v.drain()
                    v.tensor_scalar(out=fm[:, 16:24], in0=fm[:, 16:24],
                                    scalar1=fm[:, 26:27], scalar2=None, op0=ALU.mult)
                    v.drain()
                    i = None
                    for rp in range(4):
                        for ri, (lo, hi, isd) in enumerate(rrs):
                            ci = rp * 2 + ri
                            sl = slice(rp * 1024 + lo, rp * 1024 + hi)
                            i = v.tensor_scalar(out=ptile[ql % 2][:, sl],
                                                in0=ptile[ql % 2][:, sl],
                                                scalar1=fm[:, 16 + ci:17 + ci],
                                                scalar2=None, op0=ALU.mult)
                    v.drain()
                    p.inc(i, "resc")
                block.vector(v_z)

                kts = [rp * 8 + kk for rp in range(4) for kk in range(ql + 1)]
                evval = {}
                for idx in range(len(kts) + 1):
                    if idx < len(kts):
                        kt = kts[idx]

                        def t_tp(t, idx=idx, kt=kt, h=h, ql=ql):
                            if idx == 0:
                                p.wait(t, "resc")
                                if not (h == 0 and ql == 0):
                                    p.wait(t, "aoev")
                            if idx >= 2:
                                p.wge(t, "ptevV" if idx % 2 == 0 else "ptevS", evval[idx - 2])
                            i = t.transpose(out=ptb[idx % 2][:],
                                            in_=ptile[ql % 2][:, kt * 128:(kt + 1) * 128],
                                            identity=sidb[:])
                            p.inc(i, "tpmm")
                        block.tensor(t_tp)

                        if idx % 2 == 0:
                            def v_pe(v, idx=idx):
                                p.wait(v, "tpmm")
                                evval[idx] = p.inc(v.tensor_copy(
                                    out=pts[idx % 2][:], in_=ptb[idx % 2][:]),
                                    "ptevV")
                            block.vector(v_pe)
                        else:
                            def s_pe(s, idx=idx):
                                p.wait(s, "tpmm")
                                evval[idx] = p.inc(s.copy(
                                    out=pts[idx % 2][:], in_=ptb[idx % 2][:]),
                                    "ptevS")
                            block.scalar(s_pe)

                    if idx >= 1:
                        kk = idx - 1
                        kt = kts[kk]

                        def t_pv(t, idx=idx, kk=kk, kt=kt, h=h, hb=hb, last=(idx == len(kts))):
                            p.wge(t, "ptevV" if kk % 2 == 0 else "ptevS", evval[kk])
                            i = t.matmul(out=pv_ps,
                                         lhsT=vb[hb][:, kt * 128:(kt + 1) * 128],
                                         rhs=pts[kk % 2][:],
                                         start=(kk == 0), stop=last,
                                         skip_group_check=True)
                            if last:
                                p.inc(i, "pvdn")
                        block.tensor(t_pv)

                def v_ao(v, h=h, ql=ql):
                    p.wait(v, "pvdn")
                    i = v.tensor_copy(
                        out=ao2[:, h * TLOC + ql * 128:h * TLOC + (ql + 1) * 128],
                        in_=pv_ps)
                    p.inc(i, "aoev")
                block.vector(v_ao)

            def g_hdone(g_, h=h):
                p.wge(g_, "aoev", (h + 1) * 8)
                p.inc(g_.engine_nop(), "hdone")
            block.gpsimd(g_hdone)

        attn_ctx.close()
        attnb = p.sb("attnb", [128, NH * TLOC], BF16)
        wot = [p.sb(f"wot{i}", [128, 256], BF16) for i in range(2)]
        wo_blocks = [(j * 128, 128, ocg * 256, 256)
                     for ocg in range(8) for j in range(NCH)]
        ld_wo = emit_weight_stream(
            p, block, "wo2", WoT2, wo_blocks, lambda k: wot[k % 2][:],
            tile_free_sem="wo2use", tile_depth=2)

        def t_wo(t):
            p.wait(t, "aoev")
            for ocg in range(8):
                if ocg >= 1:
                    p.wge(t, "woev", ocg * 2)
                for j in range(NCH):
                    k = ocg * NCH + j
                    p.wge(t, ld_wo, (k + 1) * 16)
                    last = None
                    for oo in range(2):
                        for tc in range(2):
                            last = t.matmul(
                                out=sc_ps[oo][:, tc * 512:(tc + 1) * 512],
                                lhsT=wot[k % 2][:, oo * 128:(oo + 1) * 128],
                                rhs=ao2[:, j * TLOC + tc * 512:j * TLOC + (tc + 1) * 512],
                                start=(j == 0), stop=(j == NCH - 1),
                                skip_group_check=True)
                    p.inc(last, "wo2use")
        block.tensor(t_wo)

        def s_wo(s):
            for ocg in range(8):
                p.wge(s, "wo2use", (ocg + 1) * NCH)
                for oo in range(2):
                    oc = ocg * 2 + oo
                    i = s.copy(out=attnb[:, oc * TLOC:(oc + 1) * TLOC],
                               in_=sc_ps[oo][:])
                    p.inc(i, "woev")
        block.scalar(s_wo)

        abf = p.sb("abf", [128, 16 * 16], BF16)
        bbf = [p.sb(f"bbf{i}", [16, 512], BF16) for i in range(2)]
        mod = p.sb("mod", [16, TLOC], F32)
        hA = p.sb("hA", [16, TLOC], F32)
        hmodb = p.sb("hmodb", [16, TLOC], BF16)

        def g_ab(g):
            i = g.dma_start(out=abf[:].rearrange("pp (j c) -> pp j c", c=16),
                            in_=Amat.rearrange("(j pp) c -> pp j c", pp=128))
            p.inc(i, "abld", 16)
            p.inc(g.engine_nop(), "abld", 16)
        block.gpsimd(g_ab)

        def t_mod(t):
            p.wait(t, "woev")
            p.wait(t, "abld", 32)
            for tc in range(2):
                i = None
                for j in range(NCH):
                    i = t.matmul(out=psC[0:16, tc * 512:(tc + 1) * 512],
                                 lhsT=abf[:, j * 16:(j + 1) * 16],
                                 rhs=attnb[:, j * TLOC + tc * 512:j * TLOC + (tc + 1) * 512],
                                 start=(j == 0), stop=(j == NCH - 1),
                                 skip_group_check=True)
                p.inc(i, "modm")
        block.tensor(t_mod)

        def s_mod(s):
            p.wait(s, "modm")
            p.inc(s.activation(out=mod[:], in_=psC[0:16, 0:TLOC], func=AF.Tanh),
                  "modt")
        block.scalar(s_mod)

        def v_mod(v):
            p.wait(v, "modt")
            p.inc(v.tensor_scalar(out=mod[:], in0=mod[:], scalar1=0.1, scalar2=None,
                                  op0=ALU.mult), "modev")
        block.vector(v_mod)

        h2bf = p.sb("h2bf", [128, NCH * TLOC], BF16)
        xtile = [p.sb(f"xtile{i}", [128, E], F32) for i in range(2)]
        lnsq = p.sb("lnsq", [128, E], F32)
        xbt = p.sb("xbt", [128, E], BF16)
        qcol = p.sb("qcol", [128, 4], F32)
        wcol = p.sb("wcol", [128, 16], F32)

        def ldw2(sync):
            p.inc(sync.dma_start(out=wcol[:], in_=ln2w), "w2ld", 16)
        block.sync(ldw2)

        emit_rms_fm(p, block, "r2", xnt1, wcol, h2bf, ptb, sidb, xtile, lnsq, qcol,
                    xbt, pre_wait=lambda e: p.wge(e, "woev", 16),
                    wcol_sem="w2ld", wcol_val=16)

        def t_hA(t):
            p.wait(t, "r2fe")
            p.wait(t, "modt")
            for tc in range(2):
                i = None
                for j in range(NCH):
                    i = t.matmul(out=psC[0:16, tc * 512:(tc + 1) * 512],
                                 lhsT=abf[:, j * 16:(j + 1) * 16],
                                 rhs=h2bf[:, j * TLOC + tc * 512:j * TLOC + (tc + 1) * 512],
                                 start=(j == 0), stop=(j == NCH - 1),
                                 skip_group_check=True)
                p.inc(i, "hAm")
        block.tensor(t_hA)

        def s_hA(s):
            p.wait(s, "hAm")
            p.inc(s.copy(out=hA[:], in_=psC[0:16, 0:TLOC]), "hAe")
        block.scalar(s_hA)

        def v_hm(v):
            p.wait(v, "hAe")
            p.wait(v, "modev")
            v.tensor_tensor(out=hA[:], in0=hA[:], in1=mod[:], op=ALU.mult)
            v.drain()
            p.inc(v.tensor_copy(out=hmodb[:], in_=hA[:]), "hme")
        block.vector(v_hm)

        wis = p.sb("wis", [128, NCH * 512], BF16)
        gstg = [p.sb(f"gstg{i}", [128, 512], F32) for i in range(8)]
        ustg = [p.sb(f"ustg{i}", [128, 512], F32) for i in range(2)]
        gub = p.sb("gub", [128, 512], BF16)
        guf_og = p.sb("guf_og", [128, 4 * TLOC], BF16)

        for og in range(16):
            def g_win(g, og=og):
                p.wait(g, "hme")
                if og >= 1:
                    p.wge(g, "gmm", og * 8)
                    p.wge(g, "umm", og * 8)
                for j in range(NCH):
                    i = g.dma_start(out=wis[:, j * 512:(j + 1) * 512],
                                    in_=WinT[j * 128:(j + 1) * 128,
                                             og * 512:(og + 1) * 512])
                    p.inc(i, "wgld", 16)
                i = g.dma_start(out=bbf[og % 2][:],
                                in_=Bmat[0:16, og * 512:(og + 1) * 512])
                p.inc(i, "bgld", 16)
            block.gpsimd(g_win)

            for tt in range(8):
                def t_g(t, og=og, tt=tt):
                    k = og * 8 + tt
                    p.wge(t, "wgld", (og * 2) * NCH * 16 + NCH * 16)
                    if tt == 0:
                        p.wait(t, "hAe")
                        p.wait(t, "hme")
                        if og == 0:
                            p.wait(t, "woev")
                    p.wait(t, "gev", max(0, p.c("gev") - 1))
                    last = None
                    for j in range(NCH):
                        last = t.matmul(
                            out=psA[:, (k % 2) * 512:(k % 2 + 1) * 512],
                            lhsT=h2bf[:, j * TLOC + tt * 128:j * TLOC + (tt + 1) * 128],
                            rhs=wis[:, j * 512:(j + 1) * 512],
                            start=(j == 0), stop=False, skip_group_check=True)
                    if tt == 0:
                        p.wge(t, "bgld", (og + 1) * 16)
                    last = t.matmul(
                        out=psA[:, (k % 2) * 512:(k % 2 + 1) * 512],
                        lhsT=hmodb[0:16, tt * 128:(tt + 1) * 128],
                        rhs=bbf[og % 2][0:16, :],
                        start=False, stop=True, skip_group_check=True)
                    p.inc(last, "gmm")
                block.tensor(t_g)

                def s_g(s, og=og, tt=tt):
                    k = og * 8 + tt
                    p.wait(s, "gmm")
                    if og >= 1:
                        p.wge(s, "gu", (og - 1) * 8 + tt + 1)
                    i = s.activation(out=gstg[tt][:],
                                     in_=psA[:, (k % 2) * 512:(k % 2 + 1) * 512],
                                     func=AF.Silu)
                    p.inc(i, "gev")
                block.scalar(s_g)

            def g_wup(g, og=og):
                p.wge(g, "gmm", (og + 1) * 8)
                for j in range(NCH):
                    i = g.dma_start(out=wis[:, j * 512:(j + 1) * 512],
                                    in_=WupT[j * 128:(j + 1) * 128,
                                             og * 512:(og + 1) * 512])
                    p.inc(i, "wgld", 16)
            block.gpsimd(g_wup)

            for tt in range(8):
                def t_u(t, og=og, tt=tt):
                    k = og * 8 + tt
                    p.wge(t, "wgld", (og * 2 + 2) * NCH * 16)
                    p.wait(t, "uev", max(0, p.c("uev") - 1))
                    last = None
                    for j in range(NCH):
                        last = t.matmul(
                            out=psB[:, (k % 2) * 512:(k % 2 + 1) * 512],
                            lhsT=h2bf[:, j * TLOC + tt * 128:j * TLOC + (tt + 1) * 128],
                            rhs=wis[:, j * 512:(j + 1) * 512],
                            start=(j == 0), stop=(j == NCH - 1),
                            skip_group_check=True)
                    p.inc(last, "umm")
                block.tensor(t_u)

                def s_u(s, og=og, tt=tt):
                    k = og * 8 + tt
                    p.wait(s, "umm")
                    if k >= 2:
                        p.wait(s, "gu", max(0, p.c("gu") - 1))
                    i = s.copy(out=ustg[k % 2][:],
                               in_=psB[:, (k % 2) * 512:(k % 2 + 1) * 512])
                    p.inc(i, "uev")
                block.scalar(s_u)

                def v_gu(v, og=og, tt=tt):
                    k = og * 8 + tt
                    p.wait(v, "gev", k + 1)
                    p.wait(v, "uev", k + 1)
                    if k >= 1:
                        p.wait(v, "gut")
                    i = v.tensor_tensor(out=gub[:], in0=gstg[tt][:],
                                        in1=ustg[k % 2][:], op=ALU.mult)
                    p.inc(i, "gu")
                block.vector(v_gu)

                for cc in range(4):
                    def t_gt(t, og=og, tt=tt, cc=cc):
                        k = (og * 8 + tt) * 4 + cc
                        p.wait(t, "gu")
                        if tt == 0 and cc == 0:
                            p.wge(t, "gust", og * 16)
                        p.wge(t, "gue", max(0, p.c("gue") - 1))
                        i = t.transpose(out=ptb[k % 2][:],
                                        in_=gub[:, cc * 128:(cc + 1) * 128],
                                        identity=sidb[:])
                        p.inc(i, "gut")
                    block.tensor(t_gt)

                    def v_gte(v, og=og, tt=tt, cc=cc):
                        p.wait(v, "gut")
                        i = v.tensor_copy(
                            out=guf_og[:, cc * TLOC + tt * 128:cc * TLOC + (tt + 1) * 128],
                            in_=ptb[((og * 8 + tt) * 4 + cc) % 2][:])
                        p.inc(i, "gue")
                    block.vector(v_gte)

            def st_gu(sync, og=og):
                p.wait(sync, "gue")
                i = sync.dma_start(
                    out=guF.ap().rearrange("(oo cc pp) t -> oo pp cc t",
                                           pp=128, cc=4)[og],
                    in_=guf_og[:].rearrange("pp (cc t) -> pp cc t", cc=4))
                p.inc(i, "gust", 16)
            block.sync(st_gu)
            def v_gow(v, og=og):
                p.wge(v, "gust", (og + 1) * 16)
                p.inc(v.engine_nop(), "gow")
            block.vector(v_gow)

        ptb_ctx.close()

        with nc.psum_tensor(p.un("psD"), [128, TLOC], F32) as psD:
            pool = [psA, psB, psC, psD]

            def ps8(tt):
                return pool[tt // 2][:, (tt % 2) * 512:(tt % 2 + 1) * 512]

            wdnb = [p.sb(f"wdnb{i}", [128, 512], BF16) for i in range(2)]
            gfc = [p.sb(f"gfc{i}", [128, TLOC], BF16) for i in range(2)]
            mstg = [p.sb(f"mstg{i}", [128, 512], F32) for i in range(2)]
            xnb = [p.sb(f"xnb{i}", [128, 512], F32) for i in range(2)]
            x1b = [p.sb(f"x1b{i}", [128, 512], F32) for i in range(2)]
            s1b = [p.sb(f"s1b{i}", [128, 512], F32) for i in range(2)]

            for oc in range(4):
                for c2 in range(64):
                    def g_wd(g, oc=oc, c2=c2):
                        k = oc * 64 + c2
                        if k == 0:
                            p.wait(g, "gust")
                        if k >= 2:
                            p.wge(g, "dmm", k - 1)
                        i = g.dma_start(
                            out=wdnb[k % 2][:],
                            in_=WdnT[c2 * 128:(c2 + 1) * 128,
                                     oc * 512:(oc + 1) * 512])
                        p.inc(i, "wdld", 16)
                    block.gpsimd(g_wd)

                    def s_gf(sync, oc=oc, c2=c2):
                        k = oc * 64 + c2
                        if k == 0:
                            p.wait(sync, "gust")
                        if k >= 2:
                            p.wge(sync, "dmm", k - 1)
                        i = sync.dma_start(out=gfc[k % 2][:],
                                           in_=guF.ap()[c2 * 128:(c2 + 1) * 128, :])
                        p.inc(i, "gfld", 16)
                    block.sync(s_gf)

                    def t_d(t, oc=oc, c2=c2):
                        k = oc * 64 + c2
                        p.wge(t, "wdld", (k + 1) * 16)
                        p.wge(t, "gfld", (k + 1) * 16)
                        if c2 == 0 and oc >= 1:
                            p.wait(t, "mev")
                        if c2 == 0 and oc == 0:
                            p.wait(t, "gue")
                        last = None
                        for tt in range(8):
                            last = t.matmul(
                                out=ps8(tt),
                                lhsT=gfc[k % 2][:, tt * 128:(tt + 1) * 128],
                                rhs=wdnb[k % 2][:],
                                start=(c2 == 0), stop=(c2 == 63),
                                skip_group_check=True)
                        p.inc(last, "dmm")
                    block.tensor(t_d)

                for tt in range(8):
                    k = oc * 8 + tt

                    def ldres(sync, oc=oc, tt=tt, k=k):
                        if k >= 2:
                            p.wge(sync, "upd", k - 1)
                        i = sync.dma_start(
                            out=xnb[k % 2][:],
                            in_=xnt1[tt * 128:(tt + 1) * 128,
                                     oc * 512:(oc + 1) * 512])
                        p.inc(i, "resld", 16)
                        i = sync.dma_start(
                            out=x1b[k % 2][:],
                            in_=x1t[tt * 128:(tt + 1) * 128,
                                    oc * 512:(oc + 1) * 512])
                        p.inc(i, "resld", 16)
                        i = sync.dma_start(
                            out=s1b[k % 2][:],
                            in_=s1[tt * 128:(tt + 1) * 128,
                                   oc * 512:(oc + 1) * 512])
                        p.inc(i, "resld", 16)
                    block.sync(ldres)

                    def s_m(s, oc=oc, tt=tt, k=k):
                        p.wait(s, "dmm")
                        if k >= 2:
                            p.wge(s, "x2st", (k - 1) * 16)
                        i = s.copy(out=mstg[k % 2][:], in_=ps8(tt))
                        p.inc(i, "mev")
                    block.scalar(s_m)

                    def v_r(v, oc=oc, tt=tt, k=k):
                        p.wait(v, "mev")
                        p.wge(v, "resld", (k + 1) * 48)
                        i = v.tensor_tensor(out=mstg[k % 2][:], in0=mstg[k % 2][:],
                                            in1=xnb[k % 2][:], op=ALU.add)
                        p.inc(i, "radd")
                    block.vector(v_r)

                    def s_sil(s, oc=oc, tt=tt, k=k):
                        p.wait(s, "radd")
                        i = s.activation(out=mstg[k % 2][:], in_=mstg[k % 2][:],
                                         func=AF.Silu)
                        p.inc(i, "gsil")
                    block.scalar(s_sil)

                    def v_u(v, oc=oc, tt=tt, k=k):
                        p.wait(v, "gsil")
                        v.tensor_tensor(out=mstg[k % 2][:], in0=mstg[k % 2][:],
                                        in1=s1b[k % 2][:], op=ALU.mult)
                        v.drain()
                        i = v.tensor_tensor(out=mstg[k % 2][:], in0=mstg[k % 2][:],
                                            in1=x1b[k % 2][:], op=ALU.add)
                        p.inc(i, "upd")
                    block.vector(v_u)

                    def st_x(sync, oc=oc, tt=tt, k=k):
                        p.wait(sync, "upd")
                        i = sync.dma_start(
                            out=o_x2[tt * 128:(tt + 1) * 128,
                                     oc * 512:(oc + 1) * 512],
                            in_=mstg[k % 2][:])
                        p.inc(i, "x2st", 16)
                    block.sync(st_x)

            def fin(sync):
                p.wait(sync, "x2st")
            block.sync(fin)




EXT_IN = {
    "x0f": ([E, TLOC], F32), "x0t": ([TLOC, E], F32),
    "Wattn": ([E, 6144], F32), "Wproj": ([E, E], F32),
    "w1T": ([E, 8192], F32), "w2T": ([8192, E], F32),
    "b1row": ([1, 8192], F32), "lngrow": ([1, 8192], F32),
    "lnbrow": ([1, 8192], F32), "b2row": ([1, E], F32),
    "l0lambda_q1": ([128, 16], F32), "l0lambda_k1": ([128, 16], F32),
    "l0lambda_q2": ([128, 16], F32), "l0lambda_k2": ([128, 16], F32),
    "wqkvT": ([E, 6144], F32), "woT": ([E, E], F32),
    "l1lambda_q1": ([1, 128], F32), "l1lambda_q2": ([1, 128], F32),
    "l1lambda_k1": ([1, 128], F32), "l1lambda_k2": ([1, 128], F32),
    "gamrow": ([1, TLOC], F32), "betrow": ([1, TLOC], F32),
    "WqT": ([E, E], F32), "WkT": ([E, E], F32), "WvT": ([E, E], F32),
    "WoT2": ([E, E], F32), "Amat": ([E, 16], F32), "Bmat": ([16, 8192], F32),
    "WinT": ([E, 8192], F32), "WupT": ([E, 8192], F32), "WdnT": ([8192, E], F32),
    "ln1w": ([128, 16], F32), "ln2w": ([128, 16], F32),
    "identf": ([128, 128], F32), "identb": ([128, 128], BF16),
    "E16": ([16, TLOC], F32), "cosfm": ([128, TLOC], F32),
    "sinfm": ([128, TLOC], F32), "maskA": ([512, 128], F32),
    "maskF": ([512, 128], F32),
}

HANDOFF = {
    "xnt": ([TLOC, E], F32), "qp": ([E, TLOC], F32), "kp": ([E, TLOC], F32),
    "vt": ([TLOC, E], F32), "kfull": ([4 * E, TLOC], F32),
    "vfull": ([4096, E], F32), "s0": ([TLOC, E], F32),
    "hpre": ([TLOC, 8192], F32), "xq2": ([TLOC, 8192], I8),
    "inv2": ([TLOC, 1], F32), "x1t": ([TLOC, E], F32),
    "xnt1": ([TLOC, E], F32), "qr": ([E, TLOC], F32), "kr": ([E, TLOC], F32),
    "vt1": ([TLOC, E], F32), "kfull1": ([4 * E, TLOC], F32),
    "vfull1": ([4096, E], F32), "y": ([E, TLOC], F32), "part": ([1, 32], F32),
    "part_all": ([4, 32], F32), "s1": ([TLOC, E], F32),
    "qf2": ([E, TLOC], F32), "kf2": ([E, TLOC], F32), "vt2": ([TLOC, E], F32),
    "kfull2": ([4 * E, TLOC], F32), "vfull2": ([4096, E], F32),
    "guF": ([8192, TLOC], BF16), "x2t": ([TLOC, E], F32),
}

STAGES = [
    (emit_l0pre, ["x0f", "Wattn", "l0lambda_q1", "l0lambda_k1", "l0lambda_q2",
                  "l0lambda_k2", "identf"], ["xnt", "qp", "kp", "vt"]),
    (emit_l0attn, ["qp", "kfull", "vfull", "Wproj", "identb"], ["s0"]),
    (emit_l0ffn_a, ["xnt", "w1T", "b1row", "identb"], ["hpre"]),
    (emit_l0ffn_b, ["hpre", "lngrow", "lnbrow"], ["xq2", "inv2"]),
    (emit_l0ffn_c, ["xq2", "inv2", "w2T", "b2row", "x0t", "s0", "identb"],
     ["x1t"]),
    (emit_l1pre, ["x1t", "wqkvT", "cosfm", "sinfm", "identb"],
     ["xnt1", "qr", "kr", "vt1"]),
    (emit_l1attn_a, ["qr", "kfull1", "vfull1", "l1lambda_q1", "l1lambda_k1",
                     "l1lambda_q2", "l1lambda_k2", "maskA", "identb", "identf"],
     ["y", "part"]),
    (emit_l1attn_b, ["y", "part_all", "gamrow", "betrow", "woT", "E16",
                     "identf"], ["s1"]),
    (emit_l1fpre, ["xnt1", "ln1w", "WqT", "WkT", "WvT", "identb"],
     ["qf2", "kf2", "vt2"]),
    (emit_l1ffn, ["qf2", "kfull2", "vfull2", "maskF", "WoT2", "Amat", "Bmat",
                  "WinT", "WupT", "WdnT", "ln2w", "xnt1", "x1t", "s1",
                  "identb"], ["x2t", "guF"]),
]


def build_stage(idx):
    emit, ins, outs = STAGES[idx]
    p = Prog()
    nc = p.nc
    io = {}
    for n in ins:
        shape, dt = EXT_IN.get(n, HANDOFF.get(n))
        io[n] = p.din(n, shape, dt)
    for n in outs:
        shape, dt = HANDOFF[n]
        if n == "guF":
            io[n] = p.dram(n, shape, dt)
        else:
            io[n] = p.dout(n, shape, dt)
    with nc.Block() as block:
        emit(p, block, io)
    return p


def build_fused(nstages=len(STAGES) if True else 10):
    p = Prog()
    nc = p.nc
    io = {}
    handles = {}
    for n, (shape, dt) in EXT_IN.items():
        io[n] = p.din(n, shape, dt)
    for n, (shape, dt) in HANDOFF.items():
        if n == "x2t":
            io[n] = p.dout(n, shape, dt)
        elif n == "guF":
            io[n] = p.dram(n, shape, dt)
            handles[n] = io[n]
        else:
            h = nc.dram_tensor(n, list(shape), dt)
            handles[n] = h
            io[n] = h.ap()

    gathers = {1: [("kp", "kfull"), ("vt", "vfull")],
               6: [("kr", "kfull1"), ("vt1", "vfull1")],
               7: [("part", "part_all")],
               9: [("kf2", "kfull2"), ("vt2", "vfull2")]}

    with nc.Block() as block:
        for si, (emit, ins, outs) in enumerate(STAGES[:nstages]):
            if si in gathers:
                def g_cc(g, pairs=gathers[si]):
                    for a, b in pairs:
                        i = g.collective_compute(
                            "AllGather", ALU.bypass,
                            replica_groups=[[0, 1, 2, 3], [4, 5, 6, 7]],
                            ins=[handles[a].ap().opt()],
                            outs=[handles[b].ap().opt()])
                        p.inc(i, "cc", 1)
                    p.wait(g, "cc")
                block.gpsimd(g_cc)
                p.barrier(block)
            p.new_stage()
            p.push_scope()
            emit(p, block, io)
            p.pop_scope()
            p.barrier(block)
    return p



def _in_maps(pr, x):
    maps = []
    for c in range(NCORES):
        m = {}
        for n in EXT_IN:
            v = pr["core"][c].get(n)
            if v is None:
                v = pr["shared"][n]
            m[n] = v
        maps.append(m)
    return maps


_FUSED_CACHE = {}


def kernel(**inputs):
    pr = _prep(inputs)
    x = np.asarray(inputs["L0_attn_x"])
    for c in range(NCORES):
        b, r = c // 4, c % 4
        pr["core"][c]["x0t"] = np.ascontiguousarray(x[b, r::4, :])
    maps = _in_maps(pr, x)
    p = build_fused()
    res = run_bass_kernel_spmd(p.nc, maps, core_ids=list(range(NCORES))).results
    out = np.empty((2, 4096, E), dtype=res[0]["x2t"].dtype)
    for c in range(NCORES):
        out[c // 4, c % 4::4, :] = res[c]["x2t"]
    return out


# version 05 (final)
# speedup vs baseline: 1.0598x; 1.0598x over previous
import contextlib
import numpy as np
import ml_dtypes
import concourse.bass as bass
import concourse.mybir as mybir
from concourse.bass_utils import run_bass_kernel_spmd

F32 = mybir.dt.float32
BF16 = mybir.dt.bfloat16
I8 = mybir.dt.int8
AF = mybir.ActivationFunctionType
ALU = mybir.AluOpType
AX = mybir.AxisListType.X

NCORES = 8
TLOC = 1024
E = 2048
NCH = 16
BF = ml_dtypes.bfloat16



def _prep(I):
    C = lambda a: np.ascontiguousarray(a)
    pr = {"shared": {}, "core": [dict() for _ in range(NCORES)]}
    sh = pr["shared"]
    x = np.asarray(I["L0_attn_x"])
    for c in range(NCORES):
        b, r = c // 4, c % 4
        pr["core"][c]["x0f"] = C(x[b, r::4, :].T)

    sh["Wattn"] = C(np.asarray(I["L0_attn_W_attn"]))
    sh["Wproj"] = C(np.asarray(I["L0_attn_W_proj"]))
    sh["w1T"] = C(np.asarray(I["L0_ffn_w1"]).T)
    sh["w2T"] = C(np.asarray(I["L0_ffn_w2"]).T)
    sh["b1row"] = C(np.asarray(I["L0_ffn_b1"])[None, :])
    sh["lngrow"] = C(np.asarray(I["L0_ffn_ln_g"])[None, :])
    sh["lnbrow"] = C(np.asarray(I["L0_ffn_ln_b"])[None, :])
    sh["b2row"] = C(np.asarray(I["L0_ffn_b2"])[None, :])
    for nm in ("lambda_q1", "lambda_k1", "lambda_q2", "lambda_k2"):
        sh["l0" + nm] = C(np.asarray(I["L0_attn_" + nm]).reshape(16, 128).T)

    sh["wqkvT"] = C(np.asarray(I["L1_attn_w_qkv"]).T)
    sh["woT"] = C(np.asarray(I["L1_attn_wo"]).T)
    for nm in ("lambda_q1", "lambda_q2", "lambda_k1", "lambda_k2"):
        sh["l1" + nm] = C(np.asarray(I["L1_attn_" + nm])[None, :])
    gam = np.asarray(I["L1_attn_gamma"]); bet = np.asarray(I["L1_attn_beta"])
    for c in range(NCORES):
        r = c % 4
        pr["core"][c]["gamrow"] = C(gam[(r // 2)::2][None, :1024])
        pr["core"][c]["betrow"] = C(bet[(r // 2)::2][None, :1024])

    sh["WqT"] = C(np.asarray(I["L1_ffn_Wq"]).T)
    sh["WkT"] = C(np.asarray(I["L1_ffn_Wk"]).T)
    sh["WvT"] = C(np.asarray(I["L1_ffn_Wv"]).T)
    sh["WoT2"] = C(np.asarray(I["L1_ffn_Wo"]).T)
    sh["Amat"] = C(np.asarray(I["L1_ffn_A"]))
    sh["Bmat"] = C(np.asarray(I["L1_ffn_B"]))
    sh["WinT"] = C(np.asarray(I["L1_ffn_W_in_base"]).T)
    sh["WupT"] = C(np.asarray(I["L1_ffn_W_up_base"]).T)
    sh["WdnT"] = C(np.asarray(I["L1_ffn_W_down"]).T)
    sh["ln1w"] = C(np.asarray(I["L1_ffn_ln1_w"]).reshape(16, 128).T)
    sh["ln2w"] = C(np.asarray(I["L1_ffn_ln2_w"]).reshape(16, 128).T)

    sh["identf"] = np.eye(128, dtype=np.float32)
    sh["identb"] = np.eye(128, dtype=BF)
    e16 = np.zeros((16, 1024), np.float32)
    for g in range(16):
        e16[g, g * 64:(g + 1) * 64] = 1.0
    sh["E16"] = e16

    hd = 128
    inv = 1.0 / (10000.0 ** (np.arange(0, hd, 2, dtype=np.float64) / hd))
    invfull = np.concatenate([inv, inv])
    for c in range(NCORES):
        r = c % 4
        pos = 4.0 * np.arange(1024, dtype=np.float64) + r
        ang = invfull[:, None] * pos[None, :]
        pr["core"][c]["cosfm"] = np.cos(ang).astype(np.float32)
        pr["core"][c]["sinfm"] = np.sin(ang).astype(np.float32)

    for c in range(NCORES):
        r = c % 4
        m_f = np.zeros((512, 128), np.float32)
        m_a = np.zeros((512, 128), np.float32)
        ki = np.arange(128)[None, :]
        qi = np.arange(128)[:, None]
        for rp in range(4):
            gt = (ki > qi) | ((ki == qi) & (rp > r))
            m_f[rp * 128:(rp + 1) * 128] = np.where(gt, -1e9, 0.0)
            m_a[rp * 128:(rp + 1) * 128] = np.where(~gt, -1.0, 0.0)
        pr["core"][c]["maskF"] = m_f
        pr["core"][c]["maskA"] = m_a
    return pr



class Prog:
    def __init__(self):
        self.nc = bass.Bass("TRN2", target_bir_lowering=False)
        self.ctx = contextlib.ExitStack()
        self._sems = {}
        self.cnt = {}
        self.name_map = {}
        self.base = {}
        self.next_slot = 0
        self.scopes = [self.ctx]

    NPOOL = 100

    def new_stage(self):
        self.name_map = {}
        self.base = {}

    def push_scope(self):
        s = contextlib.ExitStack()
        self.scopes.append(s)
        return s

    def pop_scope(self):
        self.scopes.pop().close()

    def _slot(self, name):
        if name not in self.name_map:
            sl = self.next_slot % self.NPOOL
            self.name_map[name] = sl
            self.next_slot += 1
            if sl not in self._sems:
                self._sems[sl] = self.ctx.enter_context(
                    self.nc.semaphore(f"S{sl}"))
                self.cnt[sl] = 0
            self.base[name] = self.cnt[sl]
        return self.name_map[name]

    def sem(self, name):
        return self._sems[self._slot(name)]

    def barrier(self, block):
        snap = [(self._sems[sl], self.cnt[sl]) for sl in sorted(self._sems)]

        def mk(eng):
            for s, v in snap:
                if v > 0:
                    eng.wait_ge(s, v)
        block.tensor(mk)
        block.vector(mk)
        block.scalar(mk)
        block.gpsimd(mk)
        block.sync(mk)

    def inc(self, inst, name, n=1):
        s = self.sem(name)
        sl = self.name_map[name]
        inst.then_inc(s, n)
        self.cnt[sl] += n
        return self.cnt[sl] - self.base[name]

    def wait(self, eng, name, val=None):
        s = self.sem(name)
        sl = self.name_map[name]
        eng.wait_ge(s, self.cnt[sl] if val is None
                    else self.base[name] + val)

    def wge(self, eng, name, val):
        s = self.sem(name)
        eng.wait_ge(s, self.base[name] + val)

    def c(self, name):
        self.sem(name)
        return self.cnt[self.name_map[name]] - self.base[name]

    _uid = [0]

    def un(self, name):
        Prog._uid[0] += 1
        return f"{name}_u{Prog._uid[0]}"

    def sb(self, name, shape, dt):
        Prog._uid[0] += 1
        return self.scopes[-1].enter_context(
            self.nc.sbuf_tensor(f"{name}_u{Prog._uid[0]}", list(shape), dt))

    def din(self, name, shape, dt=F32):
        return self.nc.dram_tensor(name, list(shape), dt, kind="ExternalInput").ap()

    def dout(self, name, shape, dt=F32):
        return self.nc.dram_tensor(name, list(shape), dt, kind="ExternalOutput").ap()

    def dram(self, name, shape, dt=F32):
        return self.nc.dram_tensor(name, list(shape), dt)


def emit_weight_stream(p, block, name, src_ap, blocks, dst_tiles, tile_free_sem=None,
                       tile_depth=None, pre_wait=None):
    ld_s = name + "_ld"

    def ld(g):
        if pre_wait is not None:
            pre_wait(g)
        for k, (a, b, c, d) in enumerate(blocks):
            if tile_free_sem is not None and k >= tile_depth:
                p.wge(g, tile_free_sem, k - tile_depth + 1)
            i = g.dma_start(out=dst_tiles(k), in_=src_ap[a:a + b, c:c + d])
            p.inc(i, ld_s, 16)
    block.gpsimd(ld)
    return ld_s



def emit_l0pre(p, block, io):
    nc = p.nc
    x0f = io["x0f"]
    Wattn = io["Wattn"]
    lam_in = [io["l0lambda_" + nm] for nm in ("q1", "k1", "q2", "k2")]
    identf = io["identf"]
    o_xnt = io["xnt"]
    o_qp = io["qp"]
    o_kp = io["kp"]
    o_vt = io["vt"]

    xw = p.sb("xw", [128, NCH * TLOC], F32)
    xbf = p.sb("xbf", [128, NCH * TLOC], BF16)
    sqs = [p.sb(f"sqs{i}", [128, TLOC], BF16) for i in range(2)]
    sid = p.sb("sid", [128, 128], F32)
    ones_cb = p.sb("ones_cb", [128, 1], BF16)
    ones_rf = p.sb("ones_rf", [1, 128], F32)
    onesf_c = p.sb("onesf_c", [128, 1], F32)
    rows = p.sb("rows", [1, 2 * TLOC], F32)
    mB = p.sb("mB", [128, TLOC], F32)
    rB = p.sb("rB", [128, TLOC], F32)
    lv = [p.sb(f"lv{i}", [128, 16], F32) for i in range(4)]
    ltmp = p.sb("ltmp", [128, 16], F32)
    lred = p.sb("lred", [128, 2], F32)
    lsc = p.sb("lsc", [1, 4], F32)
    neglam_c = p.sb("neglam_c", [128, 1], F32)
    xnt_sb = [p.sb(f"xnt{i}", [128, E], F32) for i in range(2)]
    wtil = [p.sb(f"wqk{i}", [128, 256], BF16) for i in range(2)]
    stage = [p.sb(f"qkstg{i}", [128, TLOC], F32) for i in range(2)]
    wvs = [p.sb(f"wvs{i}", [128, NCH * 512], BF16) for i in range(2)]
    vstage = [p.sb(f"vstg{i}", [128, 512], F32) for i in range(2)]

    SM = slice(0, TLOC)
    SA = slice(TLOC, 2 * TLOC)

    with \
         nc.psum_tensor(p.un("psA"), [128, TLOC], F32) as psA, \
         nc.psum_tensor(p.un("psB"), [128, TLOC], F32) as psB, \
         nc.psum_tensor(p.un("psC"), [128, TLOC], F32) as psC, \
         nc.psum_tensor(p.un("psD"), [128, TLOC], F32) as psD:

        sum_ps = psA[0:1, :]
        sq_ps = psB[0:1, :]
        dot_ps = psC[0:1, 0:2]
        nl_ps = psC[:, 512:513]
        bc_ps = psC
        tp_ps = [psD[:, 0:128], psD[:, 512:640]]
        qk_ps = [psA, psB]
        vv_ps = [psD[:, 0:512], psD[:, 512:1024]]

        def ld0(sync):
            i = sync.dma_start(out=xw[:].rearrange("p (j t) -> p j t", j=NCH),
                               in_=x0f.rearrange("(j p) t -> p j t", p=128))
            p.inc(i, "ld", 16)
            p.inc(sync.dma_start(out=sid[:], in_=identf), "ld", 16)
            for i_, t_ in enumerate(lam_in):
                p.inc(sync.dma_start(out=lv[i_][:], in_=t_), "ld", 16)
        block.sync(ld0)
        ld_done = p.c("ld")

        def v0(v):
            v.memset(ones_cb[:], 1.0)
            v.memset(ones_rf[:], 1.0)
            v.memset(onesf_c[:], 1.0)
            p.inc(v.drain(), "cst")
            p.wait(v, "ld", ld_done)
            p.inc(v.tensor_copy(out=xbf[:], in_=xw[:]), "xbf")
        block.vector(v0)

        def vlam(v):
            p.wait(v, "ld", ld_done)
            v.tensor_tensor(out=ltmp[:], in0=lv[0][:], in1=lv[1][:], op=ALU.mult)
            v.drain()
            v.reduce_sum(out=lred[:, 0:1], in_=ltmp[:], axis=AX)
            v.drain()
            v.tensor_tensor(out=ltmp[:], in0=lv[2][:], in1=lv[3][:], op=ALU.mult)
            v.drain()
            p.inc(v.reduce_sum(out=lred[:, 1:2], in_=ltmp[:], axis=AX), "lred")
        block.vector(vlam)

        def tlam(t):
            p.wait(t, "cst")
            p.wait(t, "lred")
            t.matmul(out=dot_ps[0:1, 0:1], lhsT=lred[:, 0:1], rhs=onesf_c[:],
                     start=True, stop=True)
            p.inc(t.matmul(out=dot_ps[0:1, 1:2], lhsT=lred[:, 1:2], rhs=onesf_c[:],
                           start=True, stop=True), "ldot")
        block.tensor(tlam)

        def slam(s):
            p.wait(s, "ldot")
            p.inc(s.activation(out=lsc[0:1, 0:2], in_=dot_ps[0:1, 0:2], func=AF.Exp),
                  "lexp")
        block.scalar(slam)

        def vlam2(v):
            p.wait(v, "lexp")
            v.tensor_tensor(out=lsc[0:1, 2:3], in0=lsc[0:1, 0:1], in1=lsc[0:1, 1:2],
                            op=ALU.subtract)
            v.drain()
            v.tensor_scalar(out=lsc[0:1, 2:3], in0=lsc[0:1, 2:3], scalar1=0.2,
                            scalar2=None, op0=ALU.add)
            v.drain()
            p.inc(v.tensor_scalar(out=lsc[0:1, 3:4], in0=lsc[0:1, 2:3], scalar1=-1.0,
                                  scalar2=None, op0=ALU.mult), "lam")
        block.vector(vlam2)

        def tnl(t):
            p.wait(t, "lam")
            p.inc(t.matmul(out=nl_ps, lhsT=ones_rf[:], rhs=lsc[0:1, 3:4],
                           start=True, stop=True), "nldot")
        block.tensor(tnl)

        def snl(s):
            p.wait(s, "nldot")
            p.inc(s.copy(out=neglam_c[:], in_=nl_ps), "nlc")
        block.scalar(snl)

        def ssq(s):
            p.wait(s, "xbf")
            for j in range(NCH):
                if j >= 2:
                    p.wge(s, "squse", j - 1)
                i = s.activation(out=sqs[j % 2][:],
                                 in_=xbf[:, j * TLOC:(j + 1) * TLOC], func=AF.Square)
                p.inc(i, "x2")
        block.scalar(ssq)

        def tstats(t):
            p.wait(t, "cst")
            p.wait(t, "xbf")
            for tc in range(2):
                i = None
                for j in range(NCH):
                    i = t.matmul(
                        out=sum_ps[0:1, tc * 512:(tc + 1) * 512], lhsT=ones_cb[:],
                        rhs=xbf[:, j * TLOC + tc * 512:j * TLOC + (tc + 1) * 512],
                        start=(j == 0), stop=(j == NCH - 1))
                p.inc(i, "stat")
            for j in range(NCH):
                p.wge(t, "x2", j + 1)
                i = None
                for tc in range(2):
                    i = t.matmul(
                        out=sq_ps[0:1, tc * 512:(tc + 1) * 512], lhsT=ones_cb[:],
                        rhs=sqs[j % 2][:, tc * 512:(tc + 1) * 512],
                        start=(j == 0), stop=(j == NCH - 1), skip_group_check=True)
                p.inc(i, "squse")
            p.inc(t.matmul(out=sq_ps[0:1, 0:1], lhsT=ones_cb[0:1, :],
                           rhs=sqs[(NCH - 1) % 2][0:1, 0:1], start=False, stop=True,
                           skip_group_check=True), "stat") if False else None
        block.tensor(tstats)

        def srows(s):
            p.wait(s, "stat")
            p.wge(s, "squse", NCH)
            s.activation(out=rows[0:1, SM], in_=sum_ps, func=AF.Identity,
                         scale=1.0 / E)
            p.inc(s.activation(out=rows[0:1, SA], in_=sq_ps, func=AF.Identity,
                               scale=1.0 / E), "mrow")
        block.scalar(srows)

        def vvar(v):
            p.wait(v, "mrow")
            v.tensor_tensor(out=rows[0:1, SA], in0=rows[0:1, SA], in1=rows[0:1, SM],
                            op=ALU.subtract) if False else None
            v.tensor_tensor(out=ltmp[0:1, 0:16], in0=ltmp[0:1, 0:16],
                            in1=ltmp[0:1, 0:16], op=ALU.mult) if False else None
            v.tensor_tensor(out=mB[0:1, :], in0=rows[0:1, SM], in1=rows[0:1, SM],
                            op=ALU.mult)
            v.drain()
            v.tensor_tensor(out=rows[0:1, SA], in0=rows[0:1, SA],
                            in1=mB[0:1, :], op=ALU.subtract)
            v.drain()
            p.inc(v.tensor_scalar(out=rows[0:1, SA], in0=rows[0:1, SA],
                                  scalar1=1e-5, scalar2=None, op0=ALU.add), "var")
        block.vector(vvar)

        def sstd(s):
            p.wait(s, "var")
            p.inc(s.activation(out=rows[0:1, SA], in_=rows[0:1, SA], func=AF.Sqrt),
                  "std")
        block.scalar(sstd)

        def vrstd(v):
            p.wait(v, "std")
            p.inc(v.reciprocal(out=rows[0:1, SA], in_=rows[0:1, SA]), "rstd")
        block.vector(vrstd)

        def tbcm(t):
            p.wait(t, "rstd")
            p.wait(t, "nlc")
            i = None
            for tc in range(2):
                i = t.matmul(out=bc_ps[:, tc * 512:(tc + 1) * 512], lhsT=ones_rf[:],
                             rhs=rows[0:1, tc * 512:(tc + 1) * 512],
                             start=True, stop=True)
            p.inc(i, "bcm")
        block.tensor(tbcm)

        def sevm(s):
            p.wait(s, "bcm")
            p.inc(s.copy(out=mB[:], in_=bc_ps[:]), "mB")
        block.scalar(sevm)

        def tbcr(t):
            p.wait(t, "mB")
            i = None
            for tc in range(2):
                i = t.matmul(out=bc_ps[:, tc * 512:(tc + 1) * 512], lhsT=ones_rf[:],
                             rhs=rows[0:1, TLOC + tc * 512:TLOC + (tc + 1) * 512],
                             start=True, stop=True)
            p.inc(i, "bcr")
        block.tensor(tbcr)

        def sevr(s):
            p.wait(s, "bcr")
            p.inc(s.copy(out=rB[:], in_=bc_ps[:]), "rB")
        block.scalar(sevr)

        def vxn(v):
            p.wait(v, "rB")
            for j in range(NCH):
                sl = slice(j * TLOC, (j + 1) * TLOC)
                v.tensor_tensor(out=xw[:, sl], in0=xw[:, sl], in1=mB[:],
                                op=ALU.subtract)
                v.drain()
                v.tensor_tensor(out=xw[:, sl], in0=xw[:, sl], in1=rB[:], op=ALU.mult)
                v.drain()
            p.inc(v.tensor_copy(out=xbf[:], in_=xw[:]), "xnbf")
        block.vector(vxn)
        xn_done = p.c("xnbf")

        def ttp(t):
            p.wait(t, "xnbf", xn_done)
            for tt in range(8):
                for j in range(NCH):
                    k = tt * NCH + j
                    if k >= 2:
                        p.wge(t, "xte", k - 1)
                    i = t.transpose(
                        out=tp_ps[k % 2],
                        in_=xw[:, j * TLOC + tt * 128:j * TLOC + (tt + 1) * 128],
                        identity=sid[:])
                    p.inc(i, "xtp")
        block.tensor(ttp)

        def stp(s):
            for tt in range(8):
                if tt >= 2:
                    p.wge(s, "xts", (tt - 1) * 16)
                for j in range(NCH):
                    k = tt * NCH + j
                    p.wge(s, "xtp", k + 1)
                    i = s.copy(out=xnt_sb[tt % 2][:, j * 128:(j + 1) * 128],
                               in_=tp_ps[k % 2])
                    p.inc(i, "xte")
        block.scalar(stp)

        def sttp(sync):
            for tt in range(8):
                p.wge(sync, "xte", (tt + 1) * NCH)
                i = sync.dma_start(out=o_xnt[tt * 128:(tt + 1) * 128, :],
                                   in_=xnt_sb[tt % 2][:])
                p.inc(i, "xts", 16)
        block.sync(sttp)

        wblocks = [(j * 128, 128, ocg * 256, 256)
                   for ocg in range(16) for j in range(NCH)]
        ld_qk = emit_weight_stream(
            p, block, "wqk", Wattn, wblocks, lambda k: wtil[k % 2][:],
            tile_free_sem="wqkuse", tile_depth=2)

        def tqk(t):
            p.wait(t, "mrow")
            p.wait(t, "xnbf", xn_done)
            for ocg in range(16):
                if ocg >= 1:
                    p.wge(t, "qkevh", ocg * 2)
                for j in range(NCH):
                    k = ocg * NCH + j
                    p.wge(t, ld_qk, (k + 1) * 16)
                    last = None
                    for oo in range(2):
                        for tc in range(2):
                            last = t.matmul(
                                out=qk_ps[oo][:, tc * 512:(tc + 1) * 512],
                                lhsT=wtil[k % 2][:, oo * 128:(oo + 1) * 128],
                                rhs=xbf[:, j * TLOC + tc * 512:j * TLOC + (tc + 1) * 512],
                                start=(j == 0), stop=(j == NCH - 1),
                                skip_group_check=True)
                    p.inc(last, "wqkuse")
        block.tensor(tqk)

        def sqk(s):
            p.wait(s, "nlc")
            for ocg in range(16):
                p.wge(s, "wqkuse", (ocg + 1) * NCH)
                for oo in range(2):
                    oc = ocg * 2 + oo
                    if oc >= 2:
                        p.wge(s, "qkst", (oc - 1) * 32)
                    buf = stage[oc % 2]
                    if oc < 8 or (16 <= oc < 24):
                        s.activation(out=buf[0:64, :], in_=qk_ps[oo][0:64, :],
                                     func=AF.Identity, scale=0.125)
                        i = s.activation(out=buf[64:128, :],
                                         in_=qk_ps[oo][64:128, :],
                                         func=AF.Identity, scale=0.125)
                    elif oc < 16:
                        s.copy(out=buf[0:64, :], in_=qk_ps[oo][0:64, :])
                        i = s.copy(out=buf[64:128, :], in_=qk_ps[oo][64:128, :])
                    else:
                        s.activation(out=buf[0:64, :], in_=qk_ps[oo][0:64, :],
                                     func=AF.Identity, scale=neglam_c[0:64, :])
                        i = s.activation(out=buf[64:128, :],
                                         in_=qk_ps[oo][64:128, :],
                                         func=AF.Identity,
                                         scale=neglam_c[64:128, :])
                    p.inc(i, "qkevh")
        block.scalar(sqk)

        def stqk(sync):
            for oc in range(32):
                p.wge(sync, "qkevh", oc + 1)
                buf = stage[oc % 2]
                if oc < 8:
                    dst, h0, off = o_qp, 2 * oc, 0
                elif oc < 16:
                    dst, h0, off = o_kp, 2 * (oc - 8), 0
                elif oc < 24:
                    dst, h0, off = o_qp, 2 * (oc - 16), 64
                else:
                    dst, h0, off = o_kp, 2 * (oc - 24), 64
                i = sync.dma_start(out=dst[h0 * 128 + off:h0 * 128 + off + 64, :],
                                   in_=buf[0:64, :])
                p.inc(i, "qkst", 16)
                i = sync.dma_start(
                    out=dst[(h0 + 1) * 128 + off:(h0 + 1) * 128 + off + 64, :],
                    in_=buf[64:128, :])
                p.inc(i, "qkst", 16)
        block.sync(stqk)

        vblocks = [(j * 128, 128, 4096 + vg * 512, 512)
                   for vg in range(4) for j in range(NCH)]
        ld_wv = emit_weight_stream(
            p, block, "wv", Wattn, vblocks,
            lambda k: wvs[(k // NCH) % 2][:, (k % NCH) * 512:((k % NCH) + 1) * 512],
            tile_free_sem="wvuse", tile_depth=2 * NCH)

        def tv(t):
            p.wait(t, "xnbf", xn_done)
            p.wge(t, "xte", 8 * NCH)
            for vg in range(4):
                p.wge(t, ld_wv, (vg + 1) * NCH * 16)
                for tt in range(8):
                    k = vg * 8 + tt
                    if k >= 2:
                        p.wge(t, "vev", k - 1)
                    last = None
                    for j in range(NCH):
                        last = t.matmul(
                            out=vv_ps[k % 2],
                            lhsT=xbf[:, j * TLOC + tt * 128:j * TLOC + (tt + 1) * 128],
                            rhs=wvs[vg % 2][:, j * 512:(j + 1) * 512],
                            start=(j == 0), stop=(j == NCH - 1))
                    if tt == 7:
                        for _ in range(NCH):
                            pass
                        p.inc(last, "wvuse", NCH)
                    p.inc(last, "vev0") if False else None
                    p.inc(last, "vmm") if tt != 7 else None
        block.tensor(tv)

        def sv(s):
            for vg in range(4):
                for tt in range(8):
                    k = vg * 8 + tt
                    if tt != 7:
                        p.wge(s, "vmm", vg * 7 + tt + 1)
                    else:
                        p.wge(s, "wvuse", (vg + 1) * NCH)
                    if k >= 2:
                        p.wge(s, "vts", (k - 1) * 16)
                    i = s.copy(out=vstage[k % 2][:], in_=vv_ps[k % 2])
                    p.inc(i, "vev")
        block.scalar(sv)

        def stv(sync):
            for vg in range(4):
                for tt in range(8):
                    k = vg * 8 + tt
                    p.wge(sync, "vev", k + 1)
                    i = sync.dma_start(
                        out=o_vt[tt * 128:(tt + 1) * 128, vg * 512:(vg + 1) * 512],
                        in_=vstage[k % 2][:])
                    p.inc(i, "vts", 16)
        block.sync(stv)

        def fin(sync):
            p.wait(sync, "xts")
            p.wait(sync, "qkst")
            p.wait(sync, "vts")
        block.sync(fin)





def emit_l0attn(p, block, io):
    nc = p.nc
    qp = io["qp"]
    kfull = io["kfull"]
    vfull = io["vfull"]
    Wproj = io["Wproj"]
    identb = io["identb"]
    o_s0 = io["s0"]

    NH = 16
    KT = 4096

    kb = [p.sb(f"kb{i}", [128, KT], BF16) for i in range(2)]
    vb = [p.sb(f"vb{i}", [128, KT], BF16) for i in range(2)]
    qb = [p.sb(f"qb{i}", [128, TLOC], BF16) for i in range(2)]
    ptile = [p.sb(f"ptile{i}", [128, KT], BF16) for i in range(4)]
    pts = [p.sb(f"pts{i}", [128, 512], BF16) for i in range(2)]
    fm = p.sb("fm", [128, 16], F32)
    ao = p.sb("ao", [128, NH * TLOC], BF16)
    sidb = p.sb("sidb", [128, 128], BF16)
    wpjs = [p.sb(f"wpjs{i}", [128, 16 * 512], BF16) for i in range(2)]
    stage = [p.sb(f"pstg{i}", [128, 512], F32) for i in range(2)]

    NM = slice(0, 4)
    SS = slice(4, 8)
    FF = slice(8, 12)
    CMN = slice(12, 13)
    CZ = slice(13, 14)
    CR = slice(14, 15)

    with \
         nc.psum_tensor(p.un("psA"), [128, TLOC], F32) as psA, \
         nc.psum_tensor(p.un("psB"), [128, TLOC], F32) as psB, \
         nc.psum_tensor(p.un("psC"), [128, TLOC], F32) as psC, \
         nc.psum_tensor(p.un("ptb0"), [128, 512], BF16) as ptb0, \
         nc.psum_tensor(p.un("ptb1"), [128, 512], BF16) as ptb1:

        sc_ps = [psA, psB]
        pv_ps = psC[:, 0:512]
        ptb = [ptb0, ptb1]

        def ld0(sync):
            p.inc(sync.dma_start(out=sidb[:], in_=identb), "ld", 16)
        block.sync(ld0)

        for h in range(NH):
            hb = h % 2

            def g_ld(g, h=h, hb=hb):
                if h >= 2:
                    p.wge(g, "hdone", h - 1)
                for rp in range(4):
                    i = g.dma_start(
                        out=kb[hb][:, rp * 1024:(rp + 1) * 1024],
                        in_=kfull[rp * 2048 + h * 128:rp * 2048 + (h + 1) * 128, :])
                    p.inc(i, "kld", 16)
                i = g.dma_start(
                    out=vb[hb][:].rearrange("pp (kt d) -> pp kt d", d=128),
                    in_=vfull.rearrange("(kt pp) (hh d) -> hh pp kt d",
                                        pp=128, d=128)[h])
                p.inc(i, "vld", 16)
                i = g.dma_start(out=qb[hb][:], in_=qp[h * 128:(h + 1) * 128, :])
                p.inc(i, "qld", 16)
            block.gpsimd(g_ld)

            for g in range(2):
                for l in range(4):
                    ql = g * 4 + l
                    for kc in range(4):
                        def t_sc(t, h=h, hb=hb, ql=ql, kc=kc):
                            if ql == 0 and kc == 0:
                                p.wge(t, "kld", (h + 1) * 64)
                                p.wge(t, "qld", (h + 1) * 16)
                                if h >= 1:
                                    p.wait(t, "aoev")
                            if kc == 0:
                                p.wait(t, "tpmm")
                            p.wait(t, "expd", max(0, p.c("expd") - 1))
                            last = None
                            for n2 in range(2):
                                last = t.matmul(
                                    out=sc_ps[kc % 2][:, n2 * 512:(n2 + 1) * 512],
                                    lhsT=qb[hb][:, ql * 128:(ql + 1) * 128],
                                    rhs=kb[hb][:, kc * 1024 + n2 * 512:kc * 1024 + (n2 + 1) * 512],
                                    start=True, stop=(n2 == 1),
                                    skip_group_check=True)
                            p.inc(last, "scmm")
                        block.tensor(t_sc)

                        def v_mx(v, h=h, ql=ql, kc=kc):
                            p.wait(v, "scmm")
                            if kc == 0:
                                p.wait(v, "resc")
                            i = v.reduce_max(out=fm[:, kc:kc + 1], in_=sc_ps[kc % 2][:],
                                             axis=AX, negate=True)
                            p.inc(i, "mx")
                        block.vector(v_mx)

                        def s_exp(s, h=h, l=l, kc=kc):
                            p.wait(s, "mx")
                            i = s.activation(
                                out=ptile[l][:, kc * 1024:(kc + 1) * 1024],
                                in_=sc_ps[kc % 2][:], func=AF.Exp,
                                bias=fm[:, kc:kc + 1],
                                accum_out=fm[:, 4 + kc:5 + kc])
                            p.inc(i, "expd")
                        block.scalar(s_exp)

                    def v_row(v, h=h, l=l):
                        p.wait(v, "expd")
                        i = v.tensor_reduce(out=fm[:, CMN], in_=fm[:, NM], axis=AX,
                                            op=ALU.min)
                        p.inc(i, "mn")
                    block.vector(v_row)

                    def s_f(s, h=h, l=l):
                        p.wait(s, "mn")
                        i = s.activation(out=fm[:, FF], in_=fm[:, NM], func=AF.Exp,
                                         scale=-1.0, bias=fm[:, CMN])
                        p.inc(i, "fe")
                    block.scalar(s_f)

                    def v_z(v, h=h, l=l):
                        p.wait(v, "fe")
                        v.tensor_tensor(out=fm[:, SS], in0=fm[:, SS], in1=fm[:, FF],
                                        op=ALU.mult)
                        v.drain()
                        v.reduce_sum(out=fm[:, CZ], in_=fm[:, SS], axis=AX)
                        v.drain()
                        v.reciprocal(out=fm[:, CR], in_=fm[:, CZ])
                        v.drain()
                        v.tensor_scalar(out=fm[:, FF], in0=fm[:, FF],
                                        scalar1=fm[:, CR], scalar2=0.8,
                                        op0=ALU.mult, op1=ALU.mult)
                        v.drain()
                        for kc in range(4):
                            i = v.tensor_scalar(
                                out=ptile[l][:, kc * 1024:(kc + 1) * 1024],
                                in0=ptile[l][:, kc * 1024:(kc + 1) * 1024],
                                scalar1=fm[:, 8 + kc:9 + kc], scalar2=None,
                                op0=ALU.mult)
                        p.inc(i, "resc")
                    block.vector(v_z)

                evval = {}
                for kt in range(33):
                    if kt < 32:
                        def t_tp(t, kt=kt, g=g, h=h):
                            if kt == 0:
                                p.wait(t, "resc")
                                if not (h == 0 and g == 0):
                                    p.wait(t, "aoev")
                            if kt >= 2:
                                sem = "ptevV" if kt % 2 == 0 else "ptevS"
                                p.wge(t, sem, evval[kt - 2])
                            last = None
                            for l in range(4):
                                last = t.transpose(
                                    out=ptb[kt % 2][:, l * 128:(l + 1) * 128],
                                    in_=ptile[l][:, kt * 128:(kt + 1) * 128],
                                    identity=sidb[:])
                            p.inc(last, "tpmm")
                        block.tensor(t_tp)

                        if kt % 2 == 0:
                            def v_ptev(v, kt=kt):
                                p.wait(v, "tpmm")
                                evval[kt] = p.inc(v.tensor_copy(
                                    out=pts[kt % 2][:], in_=ptb[kt % 2][:]), "ptevV")
                            block.vector(v_ptev)
                        else:
                            def s_ptev(s, kt=kt):
                                p.wait(s, "tpmm")
                                evval[kt] = p.inc(s.copy(
                                    out=pts[kt % 2][:], in_=ptb[kt % 2][:]), "ptevS")
                            block.scalar(s_ptev)

                    if kt >= 1:
                        def t_pv(t, kt=kt, h=h, hb=hb):
                            kk = kt - 1
                            sem = "ptevV" if kk % 2 == 0 else "ptevS"
                            p.wge(t, sem, evval[kk])
                            i = t.matmul(out=pv_ps,
                                         lhsT=vb[hb][:, kk * 128:(kk + 1) * 128],
                                         rhs=pts[kk % 2][:],
                                         start=(kk == 0), stop=(kk == 31),
                                         skip_group_check=True)
                            if kk == 31:
                                p.inc(i, "pvdn")
                        block.tensor(t_pv)

                def v_ao(v, h=h, g=g):
                    p.wait(v, "pvdn")
                    i = v.tensor_copy(
                        out=ao[:, h * TLOC + g * 512:h * TLOC + (g + 1) * 512],
                        in_=pv_ps)
                    p.inc(i, "aoev")
                block.vector(v_ao)

            def g_hdone(g_, h=h):
                p.wge(g_, "aoev", (h + 1) * 2)
                p.inc(g_.engine_nop(), "hdone")
            block.gpsimd(g_hdone)

        for ocg in range(4):
            def ldpj(g_, ocg=ocg):
                if ocg >= 2:
                    p.wge(g_, "pjmm", (ocg - 1) * 8)
                for j in range(NH):
                    i = g_.dma_start(
                        out=wpjs[ocg % 2][:, j * 512:(j + 1) * 512],
                        in_=Wproj[j * 128:(j + 1) * 128, ocg * 512:(ocg + 1) * 512])
                    p.inc(i, "pjld", 16)
            block.gpsimd(ldpj)

            def tpj(t, ocg=ocg):
                if ocg == 0:
                    p.wait(t, "aoev")
                p.wge(t, "pjld", (ocg + 1) * NH * 16)
                for tt in range(8):
                    k = ocg * 8 + tt
                    if k >= 2:
                        p.wge(t, "pjev", k - 1)
                    last = None
                    for j in range(NH):
                        last = t.matmul(
                            out=(psA if k % 2 == 0 else psB)[:, 0:512],
                            lhsT=ao[:, j * TLOC + tt * 128:j * TLOC + (tt + 1) * 128],
                            rhs=wpjs[ocg % 2][:, j * 512:(j + 1) * 512],
                            start=(j == 0), stop=(j == NH - 1),
                            skip_group_check=True)
                    p.inc(last, "pjmm")
            block.tensor(tpj)

            def spj(s, ocg=ocg):
                for tt in range(8):
                    k = ocg * 8 + tt
                    p.wge(s, "pjmm", k + 1)
                    if k >= 2:
                        p.wge(s, "pjst", (k - 1) * 16)
                    i = s.copy(out=stage[k % 2][:, 0:512],
                               in_=(psA if k % 2 == 0 else psB)[:, 0:512])
                    p.inc(i, "pjev")
            block.scalar(spj)

            def stpj(sync, ocg=ocg):
                for tt in range(8):
                    k = ocg * 8 + tt
                    p.wge(sync, "pjev", k + 1)
                    i = sync.dma_start(
                        out=o_s0[tt * 128:(tt + 1) * 128, ocg * 512:(ocg + 1) * 512],
                        in_=stage[k % 2][:, 0:512])
                    p.inc(i, "pjst", 16)
            block.sync(stpj)

        def fin(sync):
            p.wait(sync, "pjst")
        block.sync(fin)




def emit_meanabs(p, block, tag, W, R, Ccols, strip, wacc, srow_out, onesf_c):
    cb = min(strip[0].shape[1], Ccols)
    blks = [(r, c) for r in range(R // 128) for c in range(Ccols // cb)]

    def ld(sync):
        for k, (r, c) in enumerate(blks):
            if k >= 2:
                p.wge(sync, tag + "red", k - 1)
            i = sync.dma_start(out=strip[k % 2][:, 0:cb],
                               in_=W[r * 128:(r + 1) * 128, c * cb:(c + 1) * cb])
            p.inc(i, tag + "ld", 16)
    block.sync(ld)

    def vred(v):
        for k in range(len(blks)):
            p.wge(v, tag + "ld", (k + 1) * 16)
            i = v.reduce_sum(out=wacc[:, k:k + 1], in_=strip[k % 2][:, 0:cb], axis=AX,
                             apply_absolute_value=True)
            p.inc(i, tag + "red")
        v.drain()
        p.inc(v.reduce_sum(out=wacc[:, len(blks):len(blks) + 1],
                           in_=wacc[:, 0:len(blks)], axis=AX), tag + "racc")
    block.vector(vred)

    def tdot(t):
        p.wait(t, tag + "racc")
        p.inc(t.matmul(out=p.meanps[0:1, 0:1], lhsT=wacc[:, len(blks):len(blks) + 1],
                       rhs=onesf_c[:], start=True, stop=True,
                       skip_group_check=True), tag + "dot")
    block.tensor(tdot)

    def sev(s):
        p.wait(s, tag + "dot")
        p.inc(s.activation(out=srow_out, in_=p.meanps[0:1, 0:1], func=AF.Identity,
                           scale=1.0 / (R * Ccols)), tag + "mean")
    block.scalar(sev)

    def vclip(v):
        p.wait(v, tag + "mean")
        p.inc(v.tensor_scalar(out=srow_out, in0=srow_out, scalar1=1e-5, scalar2=None,
                              op0=ALU.max), tag + "f")
    block.vector(vclip)
    return tag + "f"



def emit_l0ffn_a(p, block, io):
    nc = p.nc
    xnt = io["xnt"]
    w1T = io["w1T"]
    b1row = io["b1row"]
    identb = io["identb"]
    o_h = io["hpre"]

    strip = [p.sb(f"wstrip{i}", [128, 4096], F32) for i in range(2)]
    wacc = p.sb("wacc", [128, 40], F32)
    srow = p.sb("srow", [1, 8], F32)
    onesf_c = p.sb("onesf_c", [128, 1], F32)
    ones_rf = p.sb("ones_rf", [1, 128], F32)
    ntauc = p.sb("ntauc", [128, 1], F32)
    ptauc = p.sb("ptauc", [128, 1], F32)
    f1c = p.sb("f1c", [128, 1], F32)
    xtile = [p.sb(f"xtile{i}", [128, E], F32) for i in range(2)]
    qcol = p.sb("qcol", [128, 6], F32)
    xr = p.sb("xr", [128, E], F32)
    xq8 = p.sb("xq8", [128, E], I8)
    xqb = p.sb("xqb", [128, E], BF16)
    xqf = p.sb("xqf", [128, NCH * TLOC], BF16)
    sidb = p.sb("sidb", [128, 128], BF16)
    sgn = [p.sb(f"sgn{i}", [128, 1024], BF16) for i in range(2)]
    wst = p.sb("wst", [128, NCH * 1024], BF16)
    b1B = p.sb("b1B", [128, 8192], F32)
    b1s = p.sb("b1s", [1, 512], F32)
    ystg = [p.sb(f"ystg{i}", [128, 512], F32) for i in range(2)]

    with \
         nc.psum_tensor(p.un("psA"), [128, TLOC], F32) as psA, \
         nc.psum_tensor(p.un("psB"), [128, TLOC], F32) as psB, \
         nc.psum_tensor(p.un("psC"), [128, TLOC], F32) as psC, \
         nc.psum_tensor(p.un("ptb0"), [128, 128], BF16) as ptb0, \
         nc.psum_tensor(p.un("ptb1"), [128, 128], BF16) as ptb1:

        p.meanps = psC[0:1, 512:1024]
        ptb = [ptb0, ptb1]

        def ld0(sync):
            p.inc(sync.dma_start(out=sidb[:], in_=identb), "ld", 16)
        block.sync(ld0)

        def v0(v):
            v.memset(onesf_c[:], 1.0)
            v.memset(ones_rf[:], 1.0)
            p.inc(v.drain(), "cst")
        block.vector(v0)

        f1sem = emit_meanabs(p, block, "w1m", w1T, E, 8192, strip, wacc,
                             srow[0:1, 0:1], onesf_c)

        def vtau(v):
            p.wait(v, f1sem)
            v.tensor_scalar(out=srow[0:1, 1:2], in0=srow[0:1, 0:1], scalar1=-0.5,
                            scalar2=None, op0=ALU.mult)
            v.drain()
            p.inc(v.tensor_scalar(out=srow[0:1, 2:3], in0=srow[0:1, 0:1], scalar1=0.5,
                                  scalar2=None, op0=ALU.mult), "tau")
        block.vector(vtau)

        def tbc(t):
            p.wait(t, "tau")
            p.wait(t, "cst")
            t.matmul(out=psC[:, 0:1], lhsT=ones_rf[:], rhs=srow[0:1, 1:2],
                     start=True, stop=True, skip_group_check=True)
            t.matmul(out=psC[:, 1:2], lhsT=ones_rf[:], rhs=srow[0:1, 2:3],
                     start=True, stop=True, skip_group_check=True)
            p.inc(t.matmul(out=psC[:, 2:3], lhsT=ones_rf[:], rhs=srow[0:1, 0:1],
                           start=True, stop=True, skip_group_check=True), "tbc")
        block.tensor(tbc)

        def sbc(s):
            p.wait(s, "tbc")
            s.copy(out=ntauc[:], in_=psC[:, 0:1])
            s.copy(out=ptauc[:], in_=psC[:, 1:2])
            p.inc(s.copy(out=f1c[:], in_=psC[:, 2:3]), "bcc")
        block.scalar(sbc)

        for tt in range(8):
            def ldx(sync, tt=tt):
                if tt >= 2:
                    p.wge(sync, "xqd", tt - 1)
                i = sync.dma_start(out=xtile[tt % 2][:],
                                   in_=xnt[tt * 128:(tt + 1) * 128, :])
                p.inc(i, "xld", 16)
            block.sync(ldx)

            def vq(v, tt=tt):
                p.wge(v, "xld", (tt + 1) * 16)
                xt = xtile[tt % 2]
                v.tensor_tensor(out=xr[:], in0=xt[:], in1=xt[:], op=ALU.mult)
                v.drain()
                v.reduce_sum(out=qcol[:, 0:1], in_=xr[:], axis=AX)
                v.drain()
                v.tensor_scalar(out=qcol[:, 0:1], in0=qcol[:, 0:1],
                                scalar1=1.0 / E, scalar2=1e-5, op0=ALU.mult,
                                op1=ALU.add)
                p.inc(v.drain(), "ms")
            block.vector(vq)

            def sq(s, tt=tt):
                p.wait(s, "ms")
                p.inc(s.activation(out=qcol[:, 1:2], in_=qcol[:, 0:1], func=AF.Sqrt),
                      "msq")
            block.scalar(sq)

            def vq2(v, tt=tt):
                p.wait(v, "msq")
                v.reciprocal(out=qcol[:, 1:2], in_=qcol[:, 1:2])
                v.drain()
                v.tensor_scalar(out=xr[:], in0=xtile[tt % 2][:], scalar1=qcol[:, 1:2],
                                scalar2=None, op0=ALU.mult)
                v.drain()
                v.reduce_max(out=qcol[:, 2:3], in_=xr[:], axis=AX,
                             apply_absolute_value=True)
                v.drain()
                v.tensor_scalar(out=qcol[:, 2:3], in0=qcol[:, 2:3], scalar1=1e-5,
                                scalar2=None, op0=ALU.max)
                v.drain()
                v.reciprocal(out=qcol[:, 3:4], in_=qcol[:, 2:3])
                v.drain()
                v.tensor_scalar(out=qcol[:, 3:4], in0=qcol[:, 3:4], scalar1=127.0,
                                scalar2=None, op0=ALU.mult)
                v.tensor_scalar(out=qcol[:, 4:5], in0=qcol[:, 2:3],
                                scalar1=1.0 / 127.0, scalar2=None, op0=ALU.mult)
                v.drain()
                v.tensor_scalar(out=xr[:], in0=xr[:], scalar1=qcol[:, 3:4],
                                scalar2=None, op0=ALU.mult)
                v.drain()
                v.tensor_copy(out=xq8[:], in_=xr[:])
                v.drain()
                p.inc(v.tensor_scalar(out=xqb[:], in0=xq8[:], scalar1=qcol[:, 4:5],
                                      scalar2=None, op0=ALU.mult), "xqb")
            block.vector(vq2)

            for j in range(NCH):
                def ttp(t, tt=tt, j=j):
                    k = tt * NCH + j
                    p.wait(t, "xqb")
                    p.wait(t, "xqe", max(0, p.c("xqe") - 1))
                    i = t.transpose(out=ptb[k % 2][:],
                                    in_=xqb[:, j * 128:(j + 1) * 128],
                                    identity=sidb[:])
                    p.inc(i, "xqt")
                block.tensor(ttp)

                def vev(v, tt=tt, j=j):
                    p.wait(v, "xqt")
                    i = v.tensor_copy(
                        out=xqf[:, j * TLOC + tt * 128:j * TLOC + (tt + 1) * 128],
                        in_=ptb[(tt * NCH + j) % 2][:])
                    p.inc(i, "xqe")
                block.vector(vev)

            def vdone(v, tt=tt):
                p.wait(v, "xqe")
                p.inc(v.engine_nop(), "xqd")
            block.vector(vdone)

        for oc in range(16):
            def ldb(sync, oc=oc):
                if oc >= 1:
                    p.wge(sync, "b1bc", oc)
                i = sync.dma_start(out=b1s[:], in_=b1row[0:1, oc * 512:(oc + 1) * 512])
                p.inc(i, "b1ld", 16)
            block.sync(ldb)

            def tb1(t, oc=oc):
                p.wge(t, "b1ld", (oc + 1) * 16)
                p.wait(t, "bcc")
                if oc >= 1:
                    p.wge(t, "b1ev", oc)
                i = t.matmul(out=psC[:, 0:512], lhsT=ones_rf[:], rhs=b1s[:],
                             start=True, stop=True, skip_group_check=True)
                p.inc(i, "b1bc")
            block.tensor(tb1)

            def sb1(s, oc=oc):
                p.wge(s, "b1bc", oc + 1)
                i = s.copy(out=b1B[:, oc * 512:(oc + 1) * 512], in_=psC[:, 0:512])
                p.inc(i, "b1ev")
            block.scalar(sb1)

        for og in range(8):
            for j in range(NCH):
                def ldw(sync, og=og, j=j):
                    k = og * NCH + j
                    if k < 2:
                        p.wait(sync, "w1mracc")
                    else:
                        p.wge(sync, "sgn", k - 1)
                    i = sync.dma_start(
                        out=strip[k % 2][:, 0:1024],
                        in_=w1T[j * 128:(j + 1) * 128, og * 1024:(og + 1) * 1024])
                    p.inc(i, "wqld", 16)
                block.sync(ldw)

                def ssgn(s, og=og, j=j):
                    k = og * NCH + j
                    p.wge(s, "wqld", (k + 1) * 16)
                    p.wait(s, "bcc")
                    p.wait(s, "wsg")
                    s.activation(out=sgn[0][:], in_=strip[k % 2][:, 0:1024],
                                 func=AF.Sign, bias=ntauc[:])
                    i = s.activation(out=sgn[1][:], in_=strip[k % 2][:, 0:1024],
                                     func=AF.Sign, bias=ptauc[:])
                    p.inc(i, "sgn")
                block.scalar(ssgn)

                def vws(v, og=og, j=j):
                    p.wait(v, "sgn")
                    if og >= 1:
                        p.wge(v, "ymm", og * 16)
                    wsl = wst[:, j * 1024:(j + 1) * 1024]
                    v.tensor_tensor(out=wsl, in0=sgn[0][:], in1=sgn[1][:], op=ALU.add)
                    v.drain()
                    i = v.tensor_scalar(out=wsl, in0=wsl, scalar1=0.5, scalar2=None,
                                        op0=ALU.mult)
                    p.inc(i, "wsg")
                block.vector(vws)

            for tt in range(8):
                for oc in range(2):
                    def tmm(t, og=og, tt=tt, oc=oc):
                        k = og * 16 + tt * 2 + oc
                        p.wait(t, "wsg")
                        p.wait(t, "xqe")
                        p.wait(t, "b1ev")
                        p.wait(t, "yf", max(0, p.c("yf") - 1))
                        last = None
                        for j in range(NCH):
                            last = t.matmul(
                                out=(psA if k % 2 == 0 else psB)[:, 0:512],
                                lhsT=xqf[:, j * TLOC + tt * 128:j * TLOC + (tt + 1) * 128],
                                rhs=wst[:, j * 1024 + oc * 512:j * 1024 + (oc + 1) * 512],
                                start=(j == 0), stop=(j == NCH - 1),
                                skip_group_check=True)
                        p.inc(last, "ymm")
                    block.tensor(tmm)

                    def sev(s, og=og, tt=tt, oc=oc):
                        k = og * 16 + tt * 2 + oc
                        p.wait(s, "ymm")
                        if k >= 2:
                            p.wge(s, "hst", (k - 1) * 16)
                        i = s.activation(out=ystg[k % 2][:],
                                         in_=(psA if k % 2 == 0 else psB)[:, 0:512],
                                         func=AF.Identity, scale=f1c[:])
                        p.inc(i, "yf")
                    block.scalar(sev)

                    def vb1(v, og=og, tt=tt, oc=oc):
                        k = og * 16 + tt * 2 + oc
                        p.wait(v, "yf")
                        i = v.tensor_tensor(
                            out=ystg[k % 2][:], in0=ystg[k % 2][:],
                            in1=b1B[:, og * 1024 + oc * 512:og * 1024 + (oc + 1) * 512],
                            op=ALU.add)
                        p.inc(i, "yb")
                    block.vector(vb1)

                    def ssl(s, og=og, tt=tt, oc=oc):
                        k = og * 16 + tt * 2 + oc
                        p.wait(s, "yb")
                        i = s.activation(out=ystg[k % 2][:], in_=ystg[k % 2][:],
                                         func=AF.Silu)
                        p.inc(i, "yev")
                    block.scalar(ssl)

                    def sth(sync, og=og, tt=tt, oc=oc):
                        p.wait(sync, "yev")
                        i = sync.dma_start(
                            out=o_h[tt * 128:(tt + 1) * 128,
                                    og * 1024 + oc * 512:og * 1024 + (oc + 1) * 512],
                            in_=ystg[(og * 16 + tt * 2 + oc) % 2][:])
                        p.inc(i, "hst", 16)
                    block.sync(sth)

        def fin(sync):
            p.wait(sync, "hst")
        block.sync(fin)





def emit_l0ffn_b(p, block, io):
    nc = p.nc
    hpre = io["hpre"]
    lngrow = io["lngrow"]
    lnbrow = io["lnbrow"]
    o_xq2 = io["xq2"]
    o_inv = io["inv2"]

    ht = p.sb("ht", [128, 8192], F32)
    hsq = p.sb("hsq", [128, 8192], F32)
    gB = p.sb("gB", [128, 8192], F32)
    bB = p.sb("bB", [128, 8192], F32)
    rowslc = p.sb("rowslc", [1, 512], F32)
    ones_rf = p.sb("ones_rf", [1, 128], F32)
    qcol = p.sb("qcol", [128, 8], F32)
    xq8 = p.sb("xq8", [128, 8192], I8)
    invst = p.sb("invst", [128, 8], F32)

    with \
         nc.psum_tensor(p.un("psA"), [128, TLOC], F32) as psA:

        def v0(v):
            v.memset(ones_rf[:], 1.0)
            p.inc(v.drain(), "cst")
        block.vector(v0)

        for i2, (row_in, dstB) in enumerate(((lngrow, gB), (lnbrow, bB))):
            for oc in range(16):
                def ldb(sync, row_in=row_in, i2=i2, oc=oc):
                    k = i2 * 16 + oc
                    if k >= 1:
                        p.wge(sync, "gbc", k)
                    i = sync.dma_start(out=rowslc[:],
                                       in_=row_in[0:1, oc * 512:(oc + 1) * 512])
                    p.inc(i, "gld", 16)
                block.sync(ldb)

                def tb(t, i2=i2, oc=oc):
                    k = i2 * 16 + oc
                    p.wge(t, "gld", (k + 1) * 16)
                    p.wait(t, "cst")
                    if k >= 1:
                        p.wge(t, "gev", k)
                    i = t.matmul(out=psA[:, 0:512], lhsT=ones_rf[:], rhs=rowslc[:],
                                 start=True, stop=True, skip_group_check=True)
                    p.inc(i, "gbc")
                block.tensor(tb)

                def sb_(s, dstB=dstB, i2=i2, oc=oc):
                    k = i2 * 16 + oc
                    p.wge(s, "gbc", k + 1)
                    i = s.copy(out=dstB[:, oc * 512:(oc + 1) * 512], in_=psA[:, 0:512])
                    p.inc(i, "gev")
                block.scalar(sb_)

        for tt in range(8):
            def ldh(sync, tt=tt):
                if tt >= 1:
                    p.wge(sync, "q2", tt)
                i = sync.dma_start(out=ht[:], in_=hpre[tt * 128:(tt + 1) * 128, :])
                p.inc(i, "hld", 16)
            block.sync(ldh)

            def sgel(s, tt=tt):
                p.wge(s, "hld", (tt + 1) * 16)
                p.wait(s, "gev")
                i = s.activation(out=ht[:], in_=ht[:], func=AF.Gelu)
                p.inc(i, "gelu")
            block.scalar(sgel)

            def vln(v, tt=tt):
                p.wait(v, "gelu")
                v.reduce_sum(out=qcol[:, 0:1], in_=ht[:], axis=AX)
                v.drain()
                v.tensor_scalar(out=qcol[:, 0:1], in0=qcol[:, 0:1], scalar1=1.0 / 8192,
                                scalar2=None, op0=ALU.mult)
                v.drain()
                v.tensor_scalar(out=ht[:], in0=ht[:], scalar1=qcol[:, 0:1],
                                scalar2=None, op0=ALU.subtract)
                v.drain()
                v.tensor_tensor(out=hsq[:], in0=ht[:], in1=ht[:], op=ALU.mult)
                v.drain()
                v.reduce_sum(out=qcol[:, 1:2], in_=hsq[:], axis=AX)
                v.drain()
                v.tensor_scalar(out=qcol[:, 1:2], in0=qcol[:, 1:2], scalar1=1.0 / 8192,
                                scalar2=1e-5, op0=ALU.mult, op1=ALU.add)
                p.inc(v.drain(), "var2")
            block.vector(vln)

            def sstd(s, tt=tt):
                p.wait(s, "var2")
                p.inc(s.activation(out=qcol[:, 2:3], in_=qcol[:, 1:2], func=AF.Sqrt),
                      "std2")
            block.scalar(sstd)

            def vq(v, tt=tt):
                p.wait(v, "std2")
                v.reciprocal(out=qcol[:, 2:3], in_=qcol[:, 2:3])
                v.drain()
                v.tensor_scalar(out=ht[:], in0=ht[:], scalar1=qcol[:, 2:3],
                                scalar2=None, op0=ALU.mult)
                v.drain()
                v.tensor_tensor(out=ht[:], in0=ht[:], in1=gB[:], op=ALU.mult)
                v.drain()
                v.tensor_tensor(out=ht[:], in0=ht[:], in1=bB[:], op=ALU.add)
                v.drain()
                v.reduce_max(out=qcol[:, 3:4], in_=ht[:], axis=AX,
                             apply_absolute_value=True)
                v.drain()
                v.tensor_scalar(out=qcol[:, 3:4], in0=qcol[:, 3:4], scalar1=1e-5,
                                scalar2=None, op0=ALU.max)
                v.drain()
                v.reciprocal(out=qcol[:, 4:5], in_=qcol[:, 3:4])
                v.drain()
                v.tensor_scalar(out=qcol[:, 4:5], in0=qcol[:, 4:5], scalar1=127.0,
                                scalar2=None, op0=ALU.mult)
                v.tensor_scalar(out=invst[:, 0:1], in0=qcol[:, 3:4],
                                scalar1=1.0 / 127.0, scalar2=None, op0=ALU.mult)
                v.drain()
                v.tensor_scalar(out=ht[:], in0=ht[:], scalar1=qcol[:, 4:5],
                                scalar2=None, op0=ALU.mult)
                v.drain()
                p.wge(v, "q2st", tt * 32)
                p.inc(v.tensor_copy(out=xq8[:], in_=ht[:]), "q2")
            block.vector(vq)

            def stq(sync, tt=tt):
                p.wait(sync, "q2")
                p.inc(sync.dma_start(out=o_xq2[tt * 128:(tt + 1) * 128, :],
                                     in_=xq8[:]), "q2st", 16)
                p.inc(sync.dma_start(out=o_inv[tt * 128:(tt + 1) * 128, :],
                                     in_=invst[:, 0:1]), "q2st", 16)
            block.sync(stq)

        def fin(sync):
            p.wait(sync, "q2st")
        block.sync(fin)




def emit_l0ffn_c(p, block, io):
    nc = p.nc
    xq2 = io["xq2"]
    inv2 = io["inv2"]
    w2T = io["w2T"]
    b2row = io["b2row"]
    x0t = io["x0t"]
    s0 = io["s0"]
    identb = io["identb"]
    o_x1 = io["x1t"]

    xq2f = p.sb("xq2f", [128, 64 * TLOC], BF16)
    strip = [p.sb(f"w2s{i}", [128, 2048], F32) for i in range(2)]
    wacc = p.sb("wacc", [128, 80], F32)
    srow = p.sb("srow", [1, 8], F32)
    onesf_c = p.sb("onesf_c", [128, 1], F32)
    ones_rf = p.sb("ones_rf", [1, 128], F32)
    ntauc = p.sb("ntauc", [128, 1], F32)
    ptauc = p.sb("ptauc", [128, 1], F32)
    f2c = p.sb("f2c", [128, 1], F32)
    sgn = [p.sb(f"sgn{i}", [128, 512], BF16) for i in range(2)]
    wsb = [p.sb(f"wsb{i}", [128, 512], BF16) for i in range(2)]
    xq8t = p.sb("xq8t", [128, 8192], I8)
    invc = p.sb("invc", [128, 8], F32)
    xqb2 = p.sb("xqb2", [128, 8192], BF16)
    rowslc = p.sb("rowslc", [1, 512], F32)
    b2B = p.sb("b2B", [128, E], F32)
    ystg = [p.sb(f"ystg{i}", [128, 512], F32) for i in range(2)]
    xblk = [p.sb(f"xblk{i}", [128, 512], F32) for i in range(2)]
    sblk = [p.sb(f"sblk{i}", [128, 512], F32) for i in range(2)]
    sidb = p.sb("sidb", [128, 128], BF16)

    if True:
        with nc.psum_tensor(p.un("mps"), [128, 512], F32) as mps, \
             nc.psum_tensor(p.un("ptb0"), [128, 128], BF16) as ptb0, \
             nc.psum_tensor(p.un("ptb1"), [128, 128], BF16) as ptb1:
            ptb = [ptb0, ptb1]
            p.meanps = mps[0:1, :]

            def ld0(sync):
                p.inc(sync.dma_start(out=sidb[:], in_=identb), "ld", 16)
            block.sync(ld0)

            def v0(v):
                v.memset(onesf_c[:], 1.0)
                v.memset(ones_rf[:], 1.0)
                p.inc(v.drain(), "cst")
            block.vector(v0)

            f2sem = emit_meanabs(p, block, "w2m", w2T, 8192, E, strip, wacc,
                                 srow[0:1, 0:1], onesf_c)

            def vtau(v):
                p.wait(v, f2sem)
                v.tensor_scalar(out=srow[0:1, 1:2], in0=srow[0:1, 0:1], scalar1=-0.5,
                                scalar2=None, op0=ALU.mult)
                v.drain()
                p.inc(v.tensor_scalar(out=srow[0:1, 2:3], in0=srow[0:1, 0:1],
                                      scalar1=0.5, scalar2=None, op0=ALU.mult), "tau")
            block.vector(vtau)

            def tbc(t):
                p.wait(t, "tau")
                p.wait(t, "cst")
                t.matmul(out=mps[:, 0:1], lhsT=ones_rf[:], rhs=srow[0:1, 1:2],
                         start=True, stop=True, skip_group_check=True)
                t.matmul(out=mps[:, 1:2], lhsT=ones_rf[:], rhs=srow[0:1, 2:3],
                         start=True, stop=True, skip_group_check=True)
                p.inc(t.matmul(out=mps[:, 2:3], lhsT=ones_rf[:], rhs=srow[0:1, 0:1],
                               start=True, stop=True, skip_group_check=True), "tbc")
            block.tensor(tbc)

            def sbc(s):
                p.wait(s, "tbc")
                s.copy(out=ntauc[:], in_=mps[:, 0:1])
                s.copy(out=ptauc[:], in_=mps[:, 1:2])
                p.inc(s.copy(out=f2c[:], in_=mps[:, 2:3]), "bcc")
            block.scalar(sbc)

            for oc in range(4):
                def ldb(sync, oc=oc):
                    if oc >= 1:
                        p.wge(sync, "b2bc", oc)
                    i = sync.dma_start(out=rowslc[:],
                                       in_=b2row[0:1, oc * 512:(oc + 1) * 512])
                    p.inc(i, "b2ld", 16)
                block.sync(ldb)

                def tb2(t, oc=oc):
                    p.wge(t, "b2ld", (oc + 1) * 16)
                    p.wait(t, "bcc")
                    if oc >= 1:
                        p.wge(t, "b2ev", oc)
                    i = t.matmul(out=mps[:], lhsT=ones_rf[:], rhs=rowslc[:],
                                 start=True, stop=True, skip_group_check=True)
                    p.inc(i, "b2bc")
                block.tensor(tb2)

                def sb2(s, oc=oc):
                    p.wge(s, "b2bc", oc + 1)
                    i = s.copy(out=b2B[:, oc * 512:(oc + 1) * 512], in_=mps[:])
                    p.inc(i, "b2ev")
                block.scalar(sb2)

            for tt in range(8):
                def ldq(sync, tt=tt):
                    if tt >= 1:
                        p.wge(sync, "xqrec", tt)
                    i = sync.dma_start(out=xq8t[:],
                                       in_=xq2[tt * 128:(tt + 1) * 128, :])
                    p.inc(i, "qld", 16)
                    i = sync.dma_start(out=invc[:, 0:1],
                                       in_=inv2[tt * 128:(tt + 1) * 128, :])
                    p.inc(i, "qld", 16)
                block.sync(ldq)

                def vrec(v, tt=tt):
                    p.wge(v, "qld", (tt + 1) * 32)
                    if tt >= 1:
                        p.wge(v, "xq2e", tt * 64)
                    i = v.tensor_scalar(out=xqb2[:], in0=xq8t[:], scalar1=invc[:, 0:1],
                                        scalar2=None, op0=ALU.mult)
                    p.inc(i, "xqrec")
                block.vector(vrec)

                for c2 in range(64):
                    def ttp(t, tt=tt, c2=c2):
                        k = tt * 64 + c2
                        p.wge(t, "xqrec", tt + 1)
                        p.wge(t, "xq2e", max(0, k - 1))
                        i = t.transpose(out=ptb[k % 2][:],
                                        in_=xqb2[:, c2 * 128:(c2 + 1) * 128],
                                        identity=sidb[:])
                        p.inc(i, "xqt")
                    block.tensor(ttp)

                    def vev(v, tt=tt, c2=c2):
                        k = tt * 64 + c2
                        p.wge(v, "xqt", k + 1)
                        i = v.tensor_copy(
                            out=xq2f[:, c2 * TLOC + tt * 128:c2 * TLOC + (tt + 1) * 128],
                            in_=ptb[k % 2][:])
                        p.inc(i, "xq2e")
                    block.vector(vev)

        with nc.psum_tensor(p.un("psA"), [128, TLOC], F32) as psA, \
             nc.psum_tensor(p.un("psB"), [128, TLOC], F32) as psB, \
             nc.psum_tensor(p.un("psC"), [128, TLOC], F32) as psC, \
             nc.psum_tensor(p.un("psD"), [128, TLOC], F32) as psD:
            pool = [psA, psB, psC, psD]

            def ps8(tt):
                return pool[tt // 2][:, (tt % 2) * 512:(tt % 2 + 1) * 512]

            for oc in range(4):
                for c2 in range(64):
                    def ldw2(sync, oc=oc, c2=c2):
                        k = oc * 64 + c2
                        if k < 2:
                            p.wait(sync, "w2mracc")
                        else:
                            p.wge(sync, "sgn2", k - 1)
                        i = sync.dma_start(
                            out=strip[k % 2][:, 0:512],
                            in_=w2T[c2 * 128:(c2 + 1) * 128, oc * 512:(oc + 1) * 512])
                        p.inc(i, "w2ld", 16)
                    block.sync(ldw2)

                    def ssgn(s, oc=oc, c2=c2):
                        k = oc * 64 + c2
                        p.wge(s, "w2ld", (k + 1) * 16)
                        p.wait(s, "bcc")
                        p.wait(s, "wsg2")
                        s.activation(out=sgn[0][:], in_=strip[k % 2][:, 0:512],
                                     func=AF.Sign, bias=ntauc[:])
                        i = s.activation(out=sgn[1][:], in_=strip[k % 2][:, 0:512],
                                         func=AF.Sign, bias=ptauc[:])
                        p.inc(i, "sgn2")
                    block.scalar(ssgn)

                    def vws(v, oc=oc, c2=c2):
                        k = oc * 64 + c2
                        p.wait(v, "sgn2")
                        if k >= 2:
                            p.wge(v, "w2use", k - 1)
                        wsl = wsb[k % 2][:]
                        v.tensor_tensor(out=wsl, in0=sgn[0][:], in1=sgn[1][:],
                                        op=ALU.add)
                        v.drain()
                        i = v.tensor_scalar(out=wsl, in0=wsl, scalar1=0.5,
                                            scalar2=None, op0=ALU.mult)
                        p.inc(i, "wsg2")
                    block.vector(vws)

                    def tmm(t, oc=oc, c2=c2):
                        k = oc * 64 + c2
                        p.wait(t, "wsg2")
                        if c2 == 0:
                            p.wait(t, "xq2e")
                            if oc >= 1:
                                p.wait(t, "gev")
                        last = None
                        for tt in range(8):
                            last = t.matmul(
                                out=ps8(tt),
                                lhsT=xq2f[:, c2 * TLOC + tt * 128:c2 * TLOC + (tt + 1) * 128],
                                rhs=wsb[k % 2][:],
                                start=(c2 == 0), stop=(c2 == 63),
                                skip_group_check=True)
                        p.inc(last, "w2use")
                    block.tensor(tmm)

                for tt in range(8):
                    k = oc * 8 + tt

                    def sev(s, oc=oc, tt=tt, k=k):
                        p.wait(s, "w2use")
                        if k >= 2:
                            p.wge(s, "x1st", (k - 1) * 16)
                        i = s.activation(out=ystg[k % 2][:], in_=ps8(tt),
                                         func=AF.Identity, scale=f2c[:])
                        p.inc(i, "gev")
                    block.scalar(sev)

                    def ldx(sync, oc=oc, tt=tt, k=k):
                        if k >= 2:
                            p.wge(sync, "upd", k - 1)
                        i = sync.dma_start(
                            out=xblk[k % 2][:],
                            in_=x0t[tt * 128:(tt + 1) * 128, oc * 512:(oc + 1) * 512])
                        p.inc(i, "xld", 16)
                        i = sync.dma_start(
                            out=sblk[k % 2][:],
                            in_=s0[tt * 128:(tt + 1) * 128, oc * 512:(oc + 1) * 512])
                        p.inc(i, "xld", 16)
                    block.sync(ldx)

                    def vb2(v, oc=oc, tt=tt, k=k):
                        p.wait(v, "gev")
                        i = v.tensor_tensor(out=ystg[k % 2][:], in0=ystg[k % 2][:],
                                            in1=b2B[:, oc * 512:(oc + 1) * 512],
                                            op=ALU.add)
                        p.inc(i, "gb2")
                    block.vector(vb2)

                    def ssil(s, oc=oc, tt=tt, k=k):
                        p.wait(s, "gb2")
                        i = s.activation(out=ystg[k % 2][:], in_=ystg[k % 2][:],
                                         func=AF.Silu)
                        p.inc(i, "gsil")
                    block.scalar(ssil)

                    def vupd(v, oc=oc, tt=tt, k=k):
                        p.wait(v, "gsil")
                        p.wge(v, "xld", (k + 1) * 32)
                        v.tensor_tensor(out=ystg[k % 2][:], in0=ystg[k % 2][:],
                                        in1=sblk[k % 2][:], op=ALU.mult)
                        v.drain()
                        i = v.tensor_tensor(out=ystg[k % 2][:], in0=ystg[k % 2][:],
                                            in1=xblk[k % 2][:], op=ALU.add)
                        p.inc(i, "upd")
                    block.vector(vupd)

                    def stx(sync, oc=oc, tt=tt, k=k):
                        p.wait(sync, "upd")
                        i = sync.dma_start(
                            out=o_x1[tt * 128:(tt + 1) * 128, oc * 512:(oc + 1) * 512],
                            in_=ystg[k % 2][:])
                        p.inc(i, "x1st", 16)
                    block.sync(stx)

            def fin(sync):
                p.wait(sync, "x1st")
            block.sync(fin)




def emit_l1pre(p, block, io):
    nc = p.nc
    x1t = io["x1t"]
    wqkvT = io["wqkvT"]
    cosfm = io["cosfm"]
    sinfm = io["sinfm"]
    identb = io["identb"]
    o_xnt = io["xnt1"]
    o_qr = io["qr"]
    o_kr = io["kr"]
    o_vt = io["vt1"]

    xtile = [p.sb(f"xtile{i}", [128, E], F32) for i in range(2)]
    xbt = p.sb("xbt", [128, E], BF16)
    qcol = p.sb("qcol", [128, 4], F32)
    xnf_bf = p.sb("xnf_bf", [128, NCH * TLOC], BF16)
    cosb = p.sb("cosb", [128, TLOC], F32)
    sinb = p.sb("sinb", [128, TLOC], F32)
    sidb = p.sb("sidb", [128, 128], BF16)
    wtil = [p.sb(f"wqk{i}", [128, 256], BF16) for i in range(2)]
    qtile = p.sb("qtile", [128, TLOC], F32)
    rot = p.sb("rot", [128, TLOC], F32)
    ra = p.sb("ra", [128, TLOC], F32)
    lnsq = p.sb("lnsq", [128, E], F32)
    stage = [p.sb(f"rstg{i}", [128, TLOC], F32) for i in range(2)]
    wvs = [p.sb(f"wvs{i}", [128, NCH * 512], BF16) for i in range(2)]
    vstage = [p.sb(f"vstg{i}", [128, 512], F32) for i in range(2)]

    SCALER = float(128.0 ** (-0.5))

    with \
         nc.psum_tensor(p.un("psA"), [128, TLOC], F32) as psA, \
         nc.psum_tensor(p.un("psB"), [128, TLOC], F32) as psB, \
         nc.psum_tensor(p.un("psD"), [128, TLOC], F32) as psD, \
         nc.psum_tensor(p.un("ptb0"), [128, 128], BF16) as ptb0, \
         nc.psum_tensor(p.un("ptb1"), [128, 128], BF16) as ptb1:

        qk_ps = [psA, psB]
        vv_ps = [psD[:, 0:512], psD[:, 512:1024]]
        ptb = [ptb0, ptb1]

        def ld0(sync):
            p.inc(sync.dma_start(out=sidb[:], in_=identb), "ld", 16)
            p.inc(sync.dma_start(out=cosb[:], in_=cosfm), "ld", 16)
            p.inc(sync.dma_start(out=sinb[:], in_=sinfm), "ld", 16)
        block.sync(ld0)
        ld_done = p.c("ld")

        for tt in range(8):
            def ldx(sync, tt=tt):
                if tt >= 2:
                    p.wge(sync, "xnst", (tt - 1) * 16)
                i = sync.dma_start(out=xtile[tt % 2][:],
                                   in_=x1t[tt * 128:(tt + 1) * 128, :])
                p.inc(i, "xld", 16)
            block.sync(ldx)

            def vln(v, tt=tt):
                p.wge(v, "xld", (tt + 1) * 16)
                xt = xtile[tt % 2]
                v.reduce_sum(out=qcol[:, 0:1], in_=xt[:], axis=AX)
                v.drain()
                v.tensor_scalar(out=qcol[:, 0:1], in0=qcol[:, 0:1], scalar1=1.0 / E,
                                scalar2=None, op0=ALU.mult)
                v.drain()
                v.tensor_scalar(out=xt[:], in0=xt[:], scalar1=qcol[:, 0:1],
                                scalar2=None, op0=ALU.subtract)
                v.drain()
                v.tensor_tensor(out=lnsq[:], in0=xt[:], in1=xt[:], op=ALU.mult)
                v.drain()
                v.reduce_sum(out=qcol[:, 1:2], in_=lnsq[:], axis=AX)
                v.drain()
                v.tensor_scalar(out=qcol[:, 1:2], in0=qcol[:, 1:2], scalar1=1.0 / E,
                                scalar2=1e-5, op0=ALU.mult, op1=ALU.add)
                p.inc(v.drain(), "vr2")
            block.vector(vln)

            def sstd(s, tt=tt):
                p.wait(s, "vr2")
                p.inc(s.activation(out=qcol[:, 2:3], in_=qcol[:, 1:2], func=AF.Sqrt),
                      "std")
            block.scalar(sstd)

            def vxn(v, tt=tt):
                p.wait(v, "std")
                v.reciprocal(out=qcol[:, 2:3], in_=qcol[:, 2:3])
                v.drain()
                v.tensor_scalar(out=xtile[tt % 2][:], in0=xtile[tt % 2][:],
                                scalar1=qcol[:, 2:3], scalar2=None, op0=ALU.mult)
                v.drain()
                p.inc(v.tensor_copy(out=xbt[:], in_=xtile[tt % 2][:]), "xnb")
            block.vector(vxn)

            def stxn(sync, tt=tt):
                p.wait(sync, "xnb")
                i = sync.dma_start(out=o_xnt[tt * 128:(tt + 1) * 128, :],
                                   in_=xtile[tt % 2][:])
                p.inc(i, "xnst", 16)
            block.sync(stxn)

            for j in range(NCH):
                def ttp(t, tt=tt, j=j):
                    k = tt * NCH + j
                    p.wait(t, "xnb")
                    p.wge(t, "xfe", max(0, p.c("xfe") - 1))
                    i = t.transpose(out=ptb[k % 2][:],
                                    in_=xbt[:, j * 128:(j + 1) * 128],
                                    identity=sidb[:])
                    p.inc(i, "xft")
                block.tensor(ttp)

                def vev(v, tt=tt, j=j):
                    k = tt * NCH + j
                    p.wge(v, "xft", k + 1)
                    i = v.tensor_copy(
                        out=xnf_bf[:, j * TLOC + tt * 128:j * TLOC + (tt + 1) * 128],
                        in_=ptb[k % 2][:])
                    p.inc(i, "xfe")
                block.vector(vev)

            def vnb(v, tt=tt):
                p.wait(v, "xfe")
                p.inc(v.engine_nop(), "xnd")
            block.vector(vnb)

        wblocks = [(j * 128, 128, ocg * 256, 256)
                   for ocg in range(16) for j in range(NCH)]
        ld_qk = emit_weight_stream(
            p, block, "wqk", wqkvT, wblocks, lambda k: wtil[k % 2][:],
            tile_free_sem="wqkuse", tile_depth=2)

        def tqk(t):
            p.wait(t, "xfe")
            for ocg in range(16):
                if ocg >= 1:
                    p.wge(t, "qkev", ocg * 2)
                for j in range(NCH):
                    k = ocg * NCH + j
                    p.wge(t, ld_qk, (k + 1) * 16)
                    last = None
                    for oo in range(2):
                        for tc in range(2):
                            last = t.matmul(
                                out=qk_ps[oo][:, tc * 512:(tc + 1) * 512],
                                lhsT=wtil[k % 2][:, oo * 128:(oo + 1) * 128],
                                rhs=xnf_bf[:, j * TLOC + tc * 512:j * TLOC + (tc + 1) * 512],
                                start=(j == 0), stop=(j == NCH - 1),
                                skip_group_check=True)
                    p.inc(last, "wqkuse")
        block.tensor(tqk)

        for ocg in range(16):
            for oo in range(2):
                oc = ocg * 2 + oo

                def sev(s, ocg=ocg, oo=oo, oc=oc):
                    p.wge(s, "wqkuse", (ocg + 1) * NCH)
                    p.wait(s, "rpd", max(0, p.c("rpd")))
                    if oc < 16:
                        i = s.activation(out=qtile[:], in_=qk_ps[oo][:],
                                         func=AF.Identity, scale=SCALER)
                    else:
                        i = s.copy(out=qtile[:], in_=qk_ps[oo][:])
                    p.inc(i, "qkev")
                block.scalar(sev)

                def srot(s, oc=oc):
                    p.wait(s, "qkev")
                    s.activation(out=rot[0:64, :], in_=qtile[64:128, :],
                                 func=AF.Identity, scale=-1.0)
                    i = s.copy(out=rot[64:128, :], in_=qtile[0:64, :])
                    p.inc(i, "rotd")
                block.scalar(srot)

                def vrope(v, oc=oc):
                    p.wait(v, "rotd")
                    p.wge(v, "rpst", max(0, (oc - 1) * 16))
                    v.tensor_tensor(out=ra[:], in0=qtile[:], in1=cosb[:], op=ALU.mult)
                    v.drain()
                    v.tensor_tensor(out=stage[oc % 2][:], in0=rot[:], in1=sinb[:],
                                    op=ALU.mult)
                    v.drain()
                    i = v.tensor_tensor(out=stage[oc % 2][:], in0=stage[oc % 2][:],
                                        in1=ra[:], op=ALU.add)
                    p.inc(i, "rpd")
                block.vector(vrope)

                def strp(sync, oc=oc):
                    p.wait(sync, "rpd")
                    dst = o_qr if oc < 16 else o_kr
                    h = oc if oc < 16 else oc - 16
                    i = sync.dma_start(out=dst[h * 128:(h + 1) * 128, :],
                                       in_=stage[oc % 2][:])
                    p.inc(i, "rpst", 16)
                block.sync(strp)

        vblocks = [(j * 128, 128, 4096 + vg * 512, 512)
                   for vg in range(4) for j in range(NCH)]
        ld_wv = emit_weight_stream(
            p, block, "wv", wqkvT, vblocks,
            lambda k: wvs[(k // NCH) % 2][:, (k % NCH) * 512:((k % NCH) + 1) * 512],
            tile_free_sem="wvuse", tile_depth=2 * NCH)

        def tv(t):
            p.wait(t, "wqkuse")
            for vg in range(4):
                p.wge(t, ld_wv, (vg + 1) * NCH * 16)
                for tt in range(8):
                    k = vg * 8 + tt
                    if k >= 2:
                        p.wge(t, "vev", k - 1)
                    last = None
                    for j in range(NCH):
                        last = t.matmul(
                            out=vv_ps[k % 2],
                            lhsT=xnf_bf[:, j * TLOC + tt * 128:j * TLOC + (tt + 1) * 128],
                            rhs=wvs[vg % 2][:, j * 512:(j + 1) * 512],
                            start=(j == 0), stop=(j == NCH - 1))
                    if tt == 7:
                        p.inc(last, "wvuse", NCH)
                    else:
                        p.inc(last, "vmm")
        block.tensor(tv)

        def sv(s):
            for vg in range(4):
                for tt in range(8):
                    k = vg * 8 + tt
                    if tt != 7:
                        p.wge(s, "vmm", vg * 7 + tt + 1)
                    else:
                        p.wge(s, "wvuse", (vg + 1) * NCH)
                    if k >= 2:
                        p.wge(s, "vts", (k - 1) * 16)
                    i = s.copy(out=vstage[k % 2][:], in_=vv_ps[k % 2])
                    p.inc(i, "vev")
        block.scalar(sv)

        def stv(sync):
            for vg in range(4):
                for tt in range(8):
                    k = vg * 8 + tt
                    p.wge(sync, "vev", k + 1)
                    i = sync.dma_start(
                        out=o_vt[tt * 128:(tt + 1) * 128, vg * 512:(vg + 1) * 512],
                        in_=vstage[k % 2][:])
                    p.inc(i, "vts", 16)
        block.sync(stv)

        def fin(sync):
            p.wait(sync, "xnst")
            p.wait(sync, "rpst")
            p.wait(sync, "vts")
        block.sync(fin)





def emit_l1attn_a(p, block, io):
    nc = p.nc
    qr = io["qr"]
    kfull = io["kfull1"]
    vfull = io["vfull1"]
    lamin = [io["l1lambda_" + nm] for nm in ("q1", "k1", "q2", "k2")]
    maskA = io["maskA"]
    identb = io["identb"]
    o_y = io["y"]
    o_part = io["part"]

    NH = 16
    KT = 4096

    kb = [p.sb(f"kb{i}", [128, KT], BF16) for i in range(2)]
    vb = [p.sb(f"vb{i}", [128, KT], BF16) for i in range(2)]
    qb = [p.sb(f"qb{i}", [128, TLOC], BF16) for i in range(2)]
    lrow = [p.sb(f"lrow{i}", [1, 128], F32) for i in range(4)]
    lsc = p.sb("lsc", [1, 8], F32)
    nlB = p.sb("nlB", [128, 1], F32)
    maskb = p.sb("maskb", [128, 512], F32)
    ptile = [p.sb(f"ptile{i}", [128, KT], BF16) for i in range(4)]
    p2t = p.sb("p2t", [128, KT], BF16)
    pts = [p.sb(f"pts{i}", [128, 512], BF16) for i in range(2)]
    fmb = [p.sb(f"fmb{i}", [128, 52], F32) for i in range(2)]
    ao = p.sb("ao", [128, NH * TLOC], BF16)
    ystage = [p.sb(f"ystage{i}", [128, 512], F32) for i in range(2)]
    sidb = p.sb("sidb", [128, 128], BF16)
    ones_rf = p.sb("ones_rf", [1, 128], F32)
    ones_cb = p.sb("ones_cb", [128, 1], BF16)
    prow = p.sb("prow", [1, TLOC], F32)
    part = p.sb("part_sb", [1, 32], F32)
    sq_s = p.sb("sq_s", [128, TLOC], BF16)

    def ranges_for(ql):
        rr = []
        if ql > 0:
            rr.append((0, ql * 128, 1))
        rr.append((ql * 128, (ql + 1) * 128, 0))
        if ql < 7:
            rr.append(((ql + 1) * 128, 1024, 0))
        return rr

    with \
         nc.psum_tensor(p.un("psA"), [128, TLOC], F32) as psA, \
         nc.psum_tensor(p.un("psB"), [128, TLOC], F32) as psB, \
         nc.psum_tensor(p.un("psC"), [128, TLOC], F32) as psC, \
         nc.psum_tensor(p.un("ptb0"), [128, 512], BF16) as ptb0, \
         nc.psum_tensor(p.un("ptb1"), [128, 512], BF16) as ptb1:

        sc_ps = [psA, psB]
        pv_ps = psC[:, 0:512]
        aux_ps = psC[:, 512:1024]
        ptb = [ptb0, ptb1]

        def ld0(sync):
            p.inc(sync.dma_start(out=sidb[:], in_=identb), "ld", 16)
            p.inc(sync.dma_start(
                out=maskb[:].rearrange("pp (r c) -> pp r c", c=128),
                in_=maskA.rearrange("(r pp) c -> pp r c", pp=128)), "ld", 16)
            for i_, t_ in enumerate(lamin):
                p.inc(sync.dma_start(out=lrow[i_][:], in_=t_), "ld", 16)
        block.sync(ld0)
        ld_done = p.c("ld")

        def v0(v):
            v.memset(ones_rf[:], 1.0)
            v.memset(ones_cb[:], 1.0)
            p.inc(v.drain(), "cst")
        block.vector(v0)

        def vlam(v):
            p.wait(v, "ld", ld_done)
            v.tensor_tensor(out=lrow[0][:], in0=lrow[0][:], in1=lrow[1][:],
                            op=ALU.mult)
            v.drain()
            v.reduce_sum(out=lsc[0:1, 0:1], in_=lrow[0][:], axis=AX)
            v.drain()
            v.tensor_tensor(out=lrow[2][:], in0=lrow[2][:], in1=lrow[3][:],
                            op=ALU.mult)
            v.drain()
            v.reduce_sum(out=lsc[0:1, 1:2], in_=lrow[2][:], axis=AX)
            p.inc(v.drain(), "ldot")
        block.vector(vlam)

        def slam(s):
            p.wait(s, "ldot")
            p.inc(s.activation(out=lsc[0:1, 2:4], in_=lsc[0:1, 0:2], func=AF.Exp),
                  "lexp")
        block.scalar(slam)

        def vlam2(v):
            p.wait(v, "lexp")
            v.tensor_tensor(out=lsc[0:1, 4:5], in0=lsc[0:1, 2:3], in1=lsc[0:1, 3:4],
                            op=ALU.subtract)
            v.drain()
            v.tensor_scalar(out=lsc[0:1, 4:5], in0=lsc[0:1, 4:5], scalar1=0.8,
                            scalar2=None, op0=ALU.add)
            v.drain()
            p.inc(v.tensor_scalar(out=lsc[0:1, 5:6], in0=lsc[0:1, 4:5], scalar1=-1.0,
                                  scalar2=None, op0=ALU.mult), "lam")
        block.vector(vlam2)

        def tnl(t):
            p.wait(t, "lam")
            p.wait(t, "cst")
            p.inc(t.matmul(out=aux_ps[:, 0:1], lhsT=ones_rf[:], rhs=lsc[0:1, 5:6],
                           start=True, stop=True, skip_group_check=True), "nldot")
        block.tensor(tnl)

        def snl(s):
            p.wait(s, "nldot")
            p.inc(s.copy(out=nlB[:], in_=aux_ps[:, 0:1]), "nlc")
        block.scalar(snl)

        for h in range(NH):
            hb = h % 2

            def g_ld(g, h=h, hb=hb):
                if h >= 2:
                    p.wge(g, "hdone", h - 1)
                for rp in range(4):
                    i = g.dma_start(
                        out=kb[hb][:, rp * 1024:(rp + 1) * 1024],
                        in_=kfull[rp * 2048 + h * 128:rp * 2048 + (h + 1) * 128, :])
                    p.inc(i, "kld", 16)
                i = g.dma_start(
                    out=vb[hb][:].rearrange("pp (kt d) -> pp kt d", d=128),
                    in_=vfull.rearrange("(kt pp) (hh d) -> hh pp kt d",
                                        pp=128, d=128)[h])
                p.inc(i, "vld", 16)
                i = g.dma_start(out=qb[hb][:], in_=qr[h * 128:(h + 1) * 128, :])
                p.inc(i, "qld", 16)
            block.gpsimd(g_ld)

            for g in range(2):
                for l in range(4):
                    ql = g * 4 + l
                    rrs = ranges_for(ql)
                    for si in range(2):
                        pdst = ptile[l] if si == 0 else p2t
                        prange = slice(0, 64) if si == 0 else slice(64, 128)
                        fm = fmb[si]

                        def v_fz(v, si=si, fm=fm):
                            if si == 0:
                                p.wait(v, "resc")
                            v.memset(fm[:, 0:24], 1e30)
                            v.memset(fm[:, 24:36], 0.0)
                            p.inc(v.drain(), "fz")
                        block.vector(v_fz)

                        for rp in range(4):
                            def t_sc(t, h=h, hb=hb, ql=ql, si=si, rp=rp,
                                     prange=prange, g=g, l=l):
                                if ql == 0 and si == 0 and rp == 0:
                                    p.wge(t, "kld", (h + 1) * 64)
                                    p.wge(t, "qld", (h + 1) * 16)
                                    p.wait(t, "nlc")
                                    if h >= 1:
                                        p.wait(t, "aoev")
                                if si == 0 and rp == 0:
                                    p.wait(t, "tpmm")
                                if si == 1 and rp == 0:
                                    p.wait(t, "resc", max(0, p.c("resc") - 0))
                                p.wait(t, "expd", max(0, p.c("expd") - 1))
                                last = None
                                for n2 in range(2):
                                    last = t.matmul(
                                        out=sc_ps[rp % 2][:, n2 * 512:(n2 + 1) * 512],
                                        lhsT=qb[hb][prange, ql * 128:(ql + 1) * 128],
                                        rhs=kb[hb][prange,
                                                   rp * 1024 + n2 * 512:rp * 1024 + (n2 + 1) * 512],
                                        start=True, stop=(n2 == 1),
                                        skip_group_check=True)
                                p.inc(last, "scmm")
                            block.tensor(t_sc)

                            def v_dg(v, ql=ql, si=si, rp=rp):
                                p.wait(v, "scmm")
                                i = v.tensor_tensor(
                                    out=sc_ps[rp % 2][:, ql * 128:(ql + 1) * 128],
                                    in0=sc_ps[rp % 2][:, ql * 128:(ql + 1) * 128],
                                    in1=maskb[:, rp * 128:(rp + 1) * 128],
                                    op=ALU.add)
                                p.inc(i, "dmask")
                            block.vector(v_dg)

                            for ri, (lo, hi, shift) in enumerate(rrs):
                                ci = rp * 3 + ri

                                def v_mx(v, si=si, rp=rp, ri=ri, lo=lo, hi=hi,
                                         ci=ci, fm=fm):
                                    p.wait(v, "dmask")
                                    i = v.reduce_max(out=fm[:, ci:ci + 1],
                                                     in_=sc_ps[rp % 2][:, lo:hi],
                                                     axis=AX, negate=True)
                                    p.inc(i, "mx")
                                block.vector(v_mx)

                                def s_exp(s, si=si, rp=rp, ri=ri, lo=lo, hi=hi,
                                          ci=ci, fm=fm, pdst=pdst):
                                    p.wait(s, "mx")
                                    i = s.activation(
                                        out=pdst[:, rp * 1024 + lo:rp * 1024 + hi],
                                        in_=sc_ps[rp % 2][:, lo:hi], func=AF.Exp,
                                        bias=fm[:, ci:ci + 1],
                                        accum_out=fm[:, 24 + ci:25 + ci])
                                    p.inc(i, "expd")
                                block.scalar(s_exp)

                        def v_nmc(v, si=si, fm=fm, rrs=rrs, ql=ql):
                            p.wait(v, "expd")
                            for rp in range(4):
                                for ri, (lo, hi, shift) in enumerate(rrs):
                                    ci = rp * 3 + ri
                                    v.tensor_scalar(out=fm[:, 12 + ci:13 + ci],
                                                    in0=fm[:, ci:ci + 1],
                                                    scalar1=float(shift),
                                                    scalar2=None, op0=ALU.add)
                            v.drain()
                            i = v.tensor_reduce(out=fm[:, 48:49],
                                                in_=fm[:, 12:12 + 12], axis=AX,
                                                op=ALU.min)
                            p.inc(i, "mn")
                        block.vector(v_nmc)

                    for si in range(2):
                        fm = fmb[si]

                        def s_f(s, si=si, fm=fm):
                            p.wait(s, "mn")
                            i = s.activation(out=fm[:, 36:48], in_=fm[:, 12:24],
                                             func=AF.Exp, scale=-1.0,
                                             bias=fm[:, 48:49])
                            p.inc(i, "fe")
                        block.scalar(s_f)

                        def v_z(v, si=si, fm=fm):
                            p.wait(v, "fe")
                            v.tensor_tensor(out=fm[:, 24:36], in0=fm[:, 24:36],
                                            in1=fm[:, 36:48], op=ALU.mult)
                            v.drain()
                            v.reduce_sum(out=fm[:, 49:50], in_=fm[:, 24:36], axis=AX)
                            v.drain()
                            i = v.reciprocal(out=fm[:, 50:51], in_=fm[:, 49:50])
                            p.inc(i, "zr")
                        block.vector(v_z)

                    def v_resc(v, h=h, g=g, l=l, ql=ql, rrs=rrs):
                        p.wait(v, "zr")
                        v.tensor_scalar(out=fmb[0][:, 36:48], in0=fmb[0][:, 36:48],
                                        scalar1=fmb[0][:, 50:51], scalar2=None,
                                        op0=ALU.mult)
                        v.tensor_scalar(out=fmb[1][:, 36:48], in0=fmb[1][:, 36:48],
                                        scalar1=fmb[1][:, 50:51], scalar2=None,
                                        op0=ALU.mult)
                        v.drain()
                        v.tensor_scalar(out=fmb[1][:, 36:48], in0=fmb[1][:, 36:48],
                                        scalar1=nlB[:], scalar2=None, op0=ALU.mult)
                        v.drain()
                        i = None
                        for rp in range(4):
                            for ri, (lo, hi, shift) in enumerate(rrs):
                                ci = rp * 3 + ri
                                sl = slice(rp * 1024 + lo, rp * 1024 + hi)
                                v.tensor_scalar(out=ptile[l][:, sl],
                                                in0=ptile[l][:, sl],
                                                scalar1=fmb[0][:, 36 + ci:37 + ci],
                                                scalar2=None, op0=ALU.mult)
                                i = v.scalar_tensor_tensor(
                                    out=ptile[l][:, sl], in0=p2t[:, sl],
                                    scalar=fmb[1][:, 36 + ci:37 + ci],
                                    in1=ptile[l][:, sl], op0=ALU.mult, op1=ALU.add)
                        v.drain()
                        p.inc(i, "resc")
                    block.vector(v_resc)

                evval = {}
                for kt in range(33):
                    if kt < 32:
                        def t_tp(t, kt=kt, g=g, h=h):
                            if kt == 0:
                                p.wait(t, "resc")
                                if not (h == 0 and g == 0):
                                    p.wait(t, "aoev")
                            if kt >= 2:
                                p.wge(t, "ptevV" if kt % 2 == 0 else "ptevS", evval[kt - 2])
                            last = None
                            for l in range(4):
                                last = t.transpose(
                                    out=ptb[kt % 2][:, l * 128:(l + 1) * 128],
                                    in_=ptile[l][:, kt * 128:(kt + 1) * 128],
                                    identity=sidb[:])
                            p.inc(last, "tpmm")
                        block.tensor(t_tp)

                        if kt % 2 == 0:
                            def v_ptev(v, kt=kt):
                                p.wait(v, "tpmm")
                                evval[kt] = p.inc(v.tensor_copy(
                                    out=pts[kt % 2][:], in_=ptb[kt % 2][:]), "ptevV")
                            block.vector(v_ptev)
                        else:
                            def s_ptev(s, kt=kt):
                                p.wait(s, "tpmm")
                                evval[kt] = p.inc(s.copy(
                                    out=pts[kt % 2][:], in_=ptb[kt % 2][:]), "ptevS")
                            block.scalar(s_ptev)

                    if kt >= 1:
                        def t_pv(t, kt=kt, h=h, hb=hb):
                            kk = kt - 1
                            p.wge(t, "ptevV" if kk % 2 == 0 else "ptevS", evval[kk])
                            i = t.matmul(out=pv_ps,
                                         lhsT=vb[hb][:, kk * 128:(kk + 1) * 128],
                                         rhs=pts[kk % 2][:],
                                         start=(kk == 0), stop=(kk == 31),
                                         skip_group_check=True)
                            if kk == 31:
                                p.inc(i, "pvdn")
                        block.tensor(t_pv)

                def s_yev(s, h=h, g=g):
                    p.wait(s, "pvdn")
                    p.wge(s, "yst", max(0, (h * 2 + g - 1) * 16))
                    i = s.copy(out=ystage[g % 2][:], in_=pv_ps)
                    p.inc(i, "ysev")
                block.scalar(s_yev)

                def v_ao(v, h=h, g=g):
                    p.wait(v, "ysev")
                    i = v.tensor_copy(
                        out=ao[:, h * TLOC + g * 512:h * TLOC + (g + 1) * 512],
                        in_=pv_ps)
                    p.inc(i, "aoev")
                block.vector(v_ao)

                def st_y(sync, h=h, g=g):
                    p.wait(sync, "ysev")
                    i = sync.dma_start(
                        out=o_y[h * 128:(h + 1) * 128, g * 512:(g + 1) * 512],
                        in_=ystage[g % 2][:])
                    p.inc(i, "yst", 16)
                block.sync(st_y)

            def g_hdone(g_, h=h):
                p.wge(g_, "aoev", (h + 1) * 2)
                p.inc(g_.engine_nop(), "hdone")
            block.gpsimd(g_hdone)

        def t_gn1(t):
            p.wait(t, "cst")
            p.wait(t, "aoev")
            p.wait(t, "expd")
            for tc in range(2):
                i = None
                for j in range(NH):
                    i = t.matmul(out=psA[0:1, tc * 512:(tc + 1) * 512],
                                 lhsT=ones_cb[:],
                                 rhs=ao[:, j * TLOC + tc * 512:j * TLOC + (tc + 1) * 512],
                                 start=(j == 0), stop=(j == NH - 1),
                                 skip_group_check=True)
                p.inc(i, "gn1")
        block.tensor(t_gn1)

        def s_gn1(s):
            p.wait(s, "gn1")
            p.inc(s.copy(out=prow[:], in_=psA[0:1, :]), "gn1e")
        block.scalar(s_gn1)

        def v_seg1(v):
            p.wait(v, "gn1e")
            for gq in range(16):
                v.reduce_sum(out=part[0:1, gq:gq + 1],
                             in_=prow[0:1, gq * 64:(gq + 1) * 64], axis=AX)
            p.inc(v.drain(), "seg1")
        block.vector(v_seg1)

        for j in range(NH):
            def s_sqj(s, j=j):
                p.wait(s, "aoev")
                if j >= 1:
                    p.wait(s, "gn2")
                i = s.activation(out=sq_s[:],
                                 in_=ao[:, j * TLOC:(j + 1) * TLOC], func=AF.Square)
                p.inc(i, "aosqj")
            block.scalar(s_sqj)

            def t_gn2(t, j=j):
                p.wait(t, "aosqj")
                p.wait(t, "seg1")
                i = None
                for tc in range(2):
                    i = t.matmul(out=psB[0:1, tc * 512:(tc + 1) * 512],
                                 lhsT=ones_cb[:],
                                 rhs=sq_s[:, tc * 512:(tc + 1) * 512],
                                 start=(j == 0), stop=(j == NH - 1),
                                 skip_group_check=True)
                p.inc(i, "gn2")
            block.tensor(t_gn2)

        def s_gn2(s):
            p.wait(s, "gn2")
            p.inc(s.copy(out=prow[:], in_=psB[0:1, :]), "gn2e")
        block.scalar(s_gn2)

        def v_seg2(v):
            p.wait(v, "gn2e")
            for gq in range(16):
                v.reduce_sum(out=part[0:1, 16 + gq:16 + gq + 1],
                             in_=prow[0:1, gq * 64:(gq + 1) * 64], axis=AX)
            p.inc(v.drain(), "seg2")
        block.vector(v_seg2)

        def st_part(sync):
            p.wait(sync, "seg2")
            p.inc(sync.dma_start(out=o_part, in_=part[:]), "pst", 16)
        block.sync(st_part)

        def fin(sync):
            p.wait(sync, "yst")
            p.wait(sync, "pst")
        block.sync(fin)





def emit_l1attn_b(p, block, io):
    nc = p.nc
    y_in = io["y"]
    part_all = io["part_all"]
    gamrow = io["gamrow"]
    betrow = io["betrow"]
    woT = io["woT"]
    E16 = io["E16"]
    identf = io["identf"]
    o_s1 = io["s1"]

    NH = 16
    NTOK = 2 * E * TLOC * 4 // 16 // 1
    NCHK = 524288.0

    yf = p.sb("yf", [128, NH * TLOC], F32)
    ybf = p.sb("ybf", [128, NH * TLOC], BF16)
    pa = p.sb("pa", [4, 32], F32)
    e16 = p.sb("e16", [16, TLOC], F32)
    grow = p.sb("grow", [1, TLOC], F32)
    brow = p.sb("brow", [1, TLOC], F32)
    rows = p.sb("rows2", [1, 6 * 64], F32)
    mcol = p.sb("mcol", [16, 1], F32)
    rcol = p.sb("rcol2", [16, 1], F32)
    arow = p.sb("arow", [1, TLOC], F32)
    crow = p.sb("crow", [1, TLOC], F32)
    AB = p.sb("AB", [128, TLOC], F32)
    CB = p.sb("CB", [128, TLOC], F32)
    onesf4 = p.sb("onesf4", [4, 1], F32)
    ones_rf = p.sb("ones_rf", [1, 128], F32)
    oneb = p.sb("oneb", [1, 1], F32)
    sid = p.sb("sid", [128, 128], F32)
    wos = [p.sb(f"wos{i}", [128, NH * 512], BF16) for i in range(2)]
    stage = [p.sb(f"sstg{i}", [128, 512], F32) for i in range(2)]

    SM16 = slice(0, 16)
    SV16 = slice(64, 80)
    SR16 = slice(128, 144)
    SME = slice(192, 192 + 0)

    with \
         nc.psum_tensor(p.un("psA"), [128, TLOC], F32) as psA, \
         nc.psum_tensor(p.un("psB"), [128, TLOC], F32) as psB:

        def ld0(sync):
            i = sync.dma_start(out=yf[:].rearrange("pp (j t) -> pp j t", j=NH),
                               in_=y_in.rearrange("(j pp) t -> pp j t", pp=128))
            p.inc(i, "ld", 16)
            p.inc(sync.dma_start(out=pa[:], in_=part_all), "ld", 16)
            p.inc(sync.dma_start(out=e16[:], in_=E16), "ld", 16)
            p.inc(sync.dma_start(out=grow[:], in_=gamrow), "ld", 16)
            p.inc(sync.dma_start(out=brow[:], in_=betrow), "ld", 16)
            p.inc(sync.dma_start(out=sid[:], in_=identf), "ld", 16)
        block.sync(ld0)
        ld_done = p.c("ld")

        def v0(v):
            v.memset(onesf4[:], 1.0)
            v.memset(ones_rf[:], 1.0)
            v.memset(oneb[:], 1.0)
            p.inc(v.drain(), "cst")
        block.vector(v0)

        def t_pc(t):
            p.wait(t, "cst")
            p.wait(t, "ld", ld_done)
            p.inc(t.matmul(out=psA[0:1, 0:32], lhsT=onesf4[:], rhs=pa[:],
                           start=True, stop=True, skip_group_check=True), "pcm")
        block.tensor(t_pc)

        def s_pc(s):
            p.wait(s, "pcm")
            s.activation(out=rows[0:1, 0:16], in_=psA[0:1, 0:16], func=AF.Identity,
                         scale=1.0 / NCHK)
            p.inc(s.activation(out=rows[0:1, 64:80], in_=psA[0:1, 16:32],
                               func=AF.Identity, scale=1.0 / NCHK), "pce")
        block.scalar(s_pc)

        def v_var(v):
            p.wait(v, "pce")
            v.tensor_tensor(out=rows[0:1, 128:144], in0=rows[0:1, 0:16],
                            in1=rows[0:1, 0:16], op=ALU.mult)
            v.drain()
            v.tensor_tensor(out=rows[0:1, 64:80], in0=rows[0:1, 64:80],
                            in1=rows[0:1, 128:144], op=ALU.subtract)
            v.drain()
            p.inc(v.tensor_scalar(out=rows[0:1, 64:80], in0=rows[0:1, 64:80],
                                  scalar1=1e-5, scalar2=None, op0=ALU.add), "var")
        block.vector(v_var)

        def s_std(s):
            p.wait(s, "var")
            p.inc(s.activation(out=rows[0:1, 64:80], in_=rows[0:1, 64:80],
                               func=AF.Sqrt), "std")
        block.scalar(s_std)

        def v_rstd(v):
            p.wait(v, "std")
            p.inc(v.reciprocal(out=rows[0:1, 128:144], in_=rows[0:1, 64:80]), "rstd")
        block.vector(v_rstd)

        def t_mc(t):
            p.wait(t, "rstd")
            t.matmul(out=psA[0:16, 32:33], lhsT=rows[0:1, 0:16], rhs=oneb[:],
                     start=True, stop=True, skip_group_check=True)
            p.inc(t.matmul(out=psA[0:16, 33:34], lhsT=rows[0:1, 128:144], rhs=oneb[:],
                           start=True, stop=True, skip_group_check=True), "mcm")
        block.tensor(t_mc)

        def s_mc(s):
            p.wait(s, "mcm")
            s.copy(out=mcol[:], in_=psA[0:16, 32:33])
            p.inc(s.copy(out=rcol[:], in_=psA[0:16, 33:34]), "mce")
        block.scalar(s_mc)

        def t_me(t):
            p.wait(t, "mce")
            i = None
            for tc in range(2):
                t.matmul(out=psA[0:1, tc * 512:(tc + 1) * 512], lhsT=mcol[:],
                         rhs=e16[:, tc * 512:(tc + 1) * 512], start=True, stop=True,
                         skip_group_check=True)
                i = t.matmul(out=psB[0:1, tc * 512:(tc + 1) * 512], lhsT=rcol[:],
                             rhs=e16[:, tc * 512:(tc + 1) * 512], start=True,
                             stop=True, skip_group_check=True)
            p.inc(i, "mem")
        block.tensor(t_me)

        def s_me(s):
            p.wait(s, "mem")
            s.copy(out=arow[:], in_=psB[0:1, :])
            p.inc(s.copy(out=crow[:], in_=psA[0:1, :]), "mee")
        block.scalar(s_me)

        def v_ac(v):
            p.wait(v, "mee")
            v.tensor_tensor(out=arow[:], in0=arow[:], in1=grow[:], op=ALU.mult)
            v.drain()
            v.tensor_tensor(out=crow[:], in0=crow[:], in1=arow[:], op=ALU.mult)
            v.drain()
            v.tensor_tensor(out=crow[:], in0=brow[:], in1=crow[:], op=ALU.subtract)
            v.drain()
            v.tensor_scalar(out=arow[:], in0=arow[:], scalar1=0.2, scalar2=None,
                            op0=ALU.mult)
            v.tensor_scalar(out=crow[:], in0=crow[:], scalar1=0.2, scalar2=None,
                            op0=ALU.mult)
            p.inc(v.drain(), "ac")
        block.vector(v_ac)

        def t_acb(t):
            p.wait(t, "ac")
            i = None
            for tc in range(2):
                t.matmul(out=psA[:, tc * 512:(tc + 1) * 512], lhsT=ones_rf[:],
                         rhs=arow[0:1, tc * 512:(tc + 1) * 512], start=True,
                         stop=True, skip_group_check=True)
                i = t.matmul(out=psB[:, tc * 512:(tc + 1) * 512], lhsT=ones_rf[:],
                             rhs=crow[0:1, tc * 512:(tc + 1) * 512], start=True,
                             stop=True, skip_group_check=True)
            p.inc(i, "acb")
        block.tensor(t_acb)

        def s_acb(s):
            p.wait(s, "acb")
            s.copy(out=AB[:], in_=psA[:])
            p.inc(s.copy(out=CB[:], in_=psB[:]), "abe")
        block.scalar(s_acb)

        def v_yb(v):
            p.wait(v, "abe")
            for j in range(NH):
                sl = slice(j * TLOC, (j + 1) * TLOC)
                v.tensor_tensor(out=yf[:, sl], in0=yf[:, sl], in1=AB[:], op=ALU.mult)
                v.drain()
                v.tensor_tensor(out=ybf[:, sl], in0=yf[:, sl], in1=CB[:], op=ALU.add)
                v.drain()
            p.inc(v.engine_nop(), "ybd")
        block.vector(v_yb)

        for ocg in range(4):
            def ldwo(g_, ocg=ocg):
                if ocg >= 2:
                    p.wge(g_, "womm", (ocg - 1) * 8)
                for j in range(NH):
                    i = g_.dma_start(
                        out=wos[ocg % 2][:, j * 512:(j + 1) * 512],
                        in_=woT[j * 128:(j + 1) * 128, ocg * 512:(ocg + 1) * 512])
                    p.inc(i, "wold", 16)
            block.gpsimd(ldwo)

            def t_wo(t, ocg=ocg):
                if ocg == 0:
                    p.wait(t, "ybd")
                p.wge(t, "wold", (ocg + 1) * NH * 16)
                for tt in range(8):
                    k = ocg * 8 + tt
                    if k >= 2:
                        p.wge(t, "woev", k - 1)
                    last = None
                    for j in range(NH):
                        last = t.matmul(
                            out=(psA if k % 2 == 0 else psB)[:, 0:512],
                            lhsT=ybf[:, j * TLOC + tt * 128:j * TLOC + (tt + 1) * 128],
                            rhs=wos[ocg % 2][:, j * 512:(j + 1) * 512],
                            start=(j == 0), stop=(j == NH - 1),
                            skip_group_check=True)
                    p.inc(last, "womm")
            block.tensor(t_wo)

            def s_wo(s, ocg=ocg):
                for tt in range(8):
                    k = ocg * 8 + tt
                    p.wge(s, "womm", k + 1)
                    if k >= 2:
                        p.wge(s, "s1st", (k - 1) * 16)
                    i = s.copy(out=stage[k % 2][:, 0:512],
                               in_=(psA if k % 2 == 0 else psB)[:, 0:512])
                    p.inc(i, "woev")
            block.scalar(s_wo)

            def st_wo(sync, ocg=ocg):
                for tt in range(8):
                    k = ocg * 8 + tt
                    p.wge(sync, "woev", k + 1)
                    i = sync.dma_start(
                        out=o_s1[tt * 128:(tt + 1) * 128, ocg * 512:(ocg + 1) * 512],
                        in_=stage[k % 2][:, 0:512])
                    p.inc(i, "s1st", 16)
            block.sync(st_wo)

        def fin(sync):
            p.wait(sync, "s1st")
        block.sync(fin)





def emit_rms_fm(p, block, tag, xnt_ap, wcol, dst, ptb, sidb, xtile, lnsq, qcol, xbt,
                pre_wait=None, wcol_sem=None, wcol_val=0):
    for tt in range(8):
        def ldx(sync, tt=tt):
            if tt == 0 and pre_wait is not None:
                pre_wait(sync)
            if tt >= 2:
                p.wge(sync, tag + "xd", tt - 1)
            i = sync.dma_start(out=xtile[tt % 2][:],
                               in_=xnt_ap[tt * 128:(tt + 1) * 128, :])
            p.inc(i, tag + "xld", 16)
        block.sync(ldx)

        def vrms(v, tt=tt):
            p.wge(v, tag + "xld", (tt + 1) * 16)
            xt = xtile[tt % 2]
            v.tensor_tensor(out=lnsq[:], in0=xt[:], in1=xt[:], op=ALU.mult)
            v.drain()
            v.reduce_sum(out=qcol[:, 0:1], in_=lnsq[:], axis=AX)
            v.drain()
            v.tensor_scalar(out=qcol[:, 0:1], in0=qcol[:, 0:1], scalar1=1.0 / E,
                            scalar2=1e-6, op0=ALU.mult, op1=ALU.add)
            p.inc(v.drain(), tag + "ms")
        block.vector(vrms)

        def sstd(s, tt=tt):
            p.wait(s, tag + "ms")
            p.inc(s.activation(out=qcol[:, 1:2], in_=qcol[:, 0:1], func=AF.Sqrt),
                  tag + "sd")
        block.scalar(sstd)

        def vxx(v, tt=tt):
            p.wait(v, tag + "sd")
            v.reciprocal(out=qcol[:, 1:2], in_=qcol[:, 1:2])
            v.drain()
            v.tensor_scalar(out=xtile[tt % 2][:], in0=xtile[tt % 2][:],
                            scalar1=qcol[:, 1:2], scalar2=None, op0=ALU.mult)
            v.drain()
            p.inc(v.tensor_copy(out=xbt[:], in_=xtile[tt % 2][:]), tag + "xb")
        block.vector(vxx)

        for j in range(NCH):
            def ttp(t, tt=tt, j=j):
                k = tt * NCH + j
                p.wait(t, tag + "xb")
                p.wge(t, tag + "fe", max(0, p.c(tag + "fe") - 1))
                i = t.transpose(out=ptb[k % 2][:],
                                in_=xbt[:, j * 128:(j + 1) * 128],
                                identity=sidb[:])
                p.inc(i, tag + "tp")
            block.tensor(ttp)

            def vev(v, tt=tt, j=j):
                k = tt * NCH + j
                if k == 0 and wcol_sem is not None:
                    p.wge(v, wcol_sem, wcol_val)
                p.wge(v, tag + "tp", k + 1)
                v.tensor_copy(out=dst[:, j * TLOC + tt * 128:j * TLOC + (tt + 1) * 128],
                              in_=ptb[k % 2][:])
                v.drain()
                i = v.tensor_scalar(
                    out=dst[:, j * TLOC + tt * 128:j * TLOC + (tt + 1) * 128],
                    in0=dst[:, j * TLOC + tt * 128:j * TLOC + (tt + 1) * 128],
                    scalar1=wcol[:, j:j + 1], scalar2=None, op0=ALU.mult)
                p.inc(i, tag + "fe")
            block.vector(vev)

        def vdn(v, tt=tt):
            p.wait(v, tag + "fe")
            p.inc(v.engine_nop(), tag + "xd")
        block.vector(vdn)




def emit_l1fpre(p, block, io):
    nc = p.nc
    xnt1 = io["xnt1"]
    ln1w = io["ln1w"]
    WqT = io["WqT"]
    WkT = io["WkT"]
    WvT = io["WvT"]
    identb = io["identb"]
    o_q = io["qf2"]
    o_k = io["kf2"]
    o_v = io["vt2"]

    SCALER = float(128.0 ** (-0.5))

    xtile = [p.sb(f"xtile{i}", [128, E], F32) for i in range(2)]
    lnsq = p.sb("lnsq", [128, E], F32)
    xbt = p.sb("xbt", [128, E], BF16)
    qcol = p.sb("qcol", [128, 4], F32)
    wcol = p.sb("wcol", [128, 16], F32)
    xxf = p.sb("xxf", [128, NCH * TLOC], BF16)
    sidb = p.sb("sidb", [128, 128], BF16)
    wtil2 = [[p.sb(f"wt{w}_{i}", [128, 256], BF16) for i in range(2)]
             for w in range(2)]
    stage = [p.sb(f"qstg{i}", [128, TLOC], F32) for i in range(2)]
    wvs = [p.sb(f"wvs{i}", [128, NCH * 512], BF16) for i in range(2)]
    vstage = [p.sb(f"vstg{i}", [128, 512], F32) for i in range(2)]

    with \
         nc.psum_tensor(p.un("psA"), [128, TLOC], F32) as psA, \
         nc.psum_tensor(p.un("psB"), [128, TLOC], F32) as psB, \
         nc.psum_tensor(p.un("psD"), [128, TLOC], F32) as psD, \
         nc.psum_tensor(p.un("ptb0"), [128, 128], BF16) as ptb0, \
         nc.psum_tensor(p.un("ptb1"), [128, 128], BF16) as ptb1:

        qk_ps = [psA, psB]
        vv_ps = [psD[:, 0:512], psD[:, 512:1024]]
        ptb = [ptb0, ptb1]

        def ld0(sync):
            p.inc(sync.dma_start(out=sidb[:], in_=identb), "ld", 16)
            p.inc(sync.dma_start(out=wcol[:], in_=ln1w), "ld", 16)
        block.sync(ld0)
        ld_done = p.c("ld")

        def vw(v):
            p.wait(v, "ld", ld_done)
            p.inc(v.drain(), "wc")
        block.vector(vw)

        emit_rms_fm(p, block, "r1", xnt1, wcol, xxf, ptb, sidb, xtile, lnsq, qcol,
                    xbt, wcol_sem="ld", wcol_val=ld_done)

        for wi, (Wap, dsto, scale) in enumerate(((WqT, o_q, SCALER), (WkT, o_k, None))):
            wblocks = [(j * 128, 128, ocg * 256, 256)
                       for ocg in range(8) for j in range(NCH)]
            wtil = wtil2[wi]
            ld_s = emit_weight_stream(
                p, block, f"w{wi}", Wap, wblocks,
                lambda k, wtil=wtil: wtil[k % 2][:],
                tile_free_sem=f"w{wi}use", tile_depth=2)

            def tqk(t, wi=wi, ld_s=ld_s, wtil=wtil2[wi]):
                if wi == 0:
                    p.wait(t, "r1fe")
                for ocg in range(8):
                    if ocg >= 1 or wi == 1:
                        p.wge(t, "qkev", wi * 16 + ocg * 2)
                    for j in range(NCH):
                        k = ocg * NCH + j
                        p.wge(t, ld_s, (k + 1) * 16)
                        last = None
                        for oo in range(2):
                            for tc in range(2):
                                last = t.matmul(
                                    out=qk_ps[oo][:, tc * 512:(tc + 1) * 512],
                                    lhsT=wtil[k % 2][:, oo * 128:(oo + 1) * 128],
                                    rhs=xxf[:, j * TLOC + tc * 512:j * TLOC + (tc + 1) * 512],
                                    start=(j == 0), stop=(j == NCH - 1),
                                    skip_group_check=True)
                        p.inc(last, f"w{wi}use")
            block.tensor(tqk)

            def sev(s, wi=wi, dsto=dsto, scale=scale):
                for ocg in range(8):
                    p.wge(s, f"w{wi}use", (ocg + 1) * NCH)
                    for oo in range(2):
                        oc = ocg * 2 + oo
                        kk = wi * 16 + oc
                        if kk >= 2:
                            p.wge(s, "qkst", (kk - 1) * 16)
                        if scale is not None:
                            i = s.activation(out=stage[oc % 2][:], in_=qk_ps[oo][:],
                                             func=AF.Identity, scale=scale)
                        else:
                            i = s.copy(out=stage[oc % 2][:], in_=qk_ps[oo][:])
                        p.inc(i, "qkev")
                block.scalar if False else None
            block.scalar(sev)

            def stq(sync, wi=wi, dsto=dsto):
                for oc in range(16):
                    kk = wi * 16 + oc
                    p.wge(sync, "qkev", kk + 1)
                    i = sync.dma_start(out=dsto[oc * 128:(oc + 1) * 128, :],
                                       in_=stage[oc % 2][:])
                    p.inc(i, "qkst", 16)
            block.sync(stq)

        vblocks = [(j * 128, 128, vg * 512, 512)
                   for vg in range(4) for j in range(NCH)]
        ld_wv = emit_weight_stream(
            p, block, "wv2", WvT, vblocks,
            lambda k: wvs[(k // NCH) % 2][:, (k % NCH) * 512:((k % NCH) + 1) * 512],
            tile_free_sem="wv2use", tile_depth=2 * NCH)

        def tv(t):
            p.wait(t, "w1use")
            for vg in range(4):
                p.wge(t, ld_wv, (vg + 1) * NCH * 16)
                for tt in range(8):
                    k = vg * 8 + tt
                    if k >= 2:
                        p.wge(t, "vev", k - 1)
                    last = None
                    for j in range(NCH):
                        last = t.matmul(
                            out=vv_ps[k % 2],
                            lhsT=xxf[:, j * TLOC + tt * 128:j * TLOC + (tt + 1) * 128],
                            rhs=wvs[vg % 2][:, j * 512:(j + 1) * 512],
                            start=(j == 0), stop=(j == NCH - 1))
                    if tt == 7:
                        p.inc(last, "wv2use", NCH)
                    else:
                        p.inc(last, "vmm")
        block.tensor(tv)

        def sv(s):
            for vg in range(4):
                for tt in range(8):
                    k = vg * 8 + tt
                    if tt != 7:
                        p.wge(s, "vmm", vg * 7 + tt + 1)
                    else:
                        p.wge(s, "wv2use", (vg + 1) * NCH)
                    if k >= 2:
                        p.wge(s, "vts", (k - 1) * 16)
                    i = s.copy(out=vstage[k % 2][:], in_=vv_ps[k % 2])
                    p.inc(i, "vev")
        block.scalar(sv)

        def stv(sync):
            for vg in range(4):
                for tt in range(8):
                    k = vg * 8 + tt
                    p.wge(sync, "vev", k + 1)
                    i = sync.dma_start(
                        out=o_v[tt * 128:(tt + 1) * 128, vg * 512:(vg + 1) * 512],
                        in_=vstage[k % 2][:])
                    p.inc(i, "vts", 16)
        block.sync(stv)

        def fin(sync):
            p.wait(sync, "qkst")
            p.wait(sync, "vts")
        block.sync(fin)





def emit_l1ffn(p, block, io):
    nc = p.nc
    qf2 = io["qf2"]
    kfull = io["kfull2"]
    vfull = io["vfull2"]
    maskF = io["maskF"]
    WoT2 = io["WoT2"]
    Amat = io["Amat"]
    Bmat = io["Bmat"]
    WinT = io["WinT"]
    WupT = io["WupT"]
    WdnT = io["WdnT"]
    ln2w = io["ln2w"]
    xnt1 = io["xnt1"]
    x1t = io["x1t"]
    s1 = io["s1"]
    identb = io["identb"]
    o_x2 = io["x2t"]
    guF = io["guF"]

    NH = 16

    sidb = p.sb("sidb", [128, 128], BF16)
    ao2 = p.sb("ao2", [128, NH * TLOC], BF16)

    attn_ctx = contextlib.ExitStack()

    def asb(name, shape, dt):
        return attn_ctx.enter_context(nc.sbuf_tensor(name, list(shape), dt))

    kb = [asb(f"kb{i}", [128, 4096], BF16) for i in range(2)]
    vb = [asb(f"vb{i}", [128, 4096], BF16) for i in range(2)]
    qb = [asb(f"qb{i}", [128, TLOC], BF16) for i in range(2)]
    ptile = [asb(f"ptile{i}", [128, 4096], BF16) for i in range(2)]
    pts = [asb(f"pts{i}", [128, 128], BF16) for i in range(2)]
    fm = asb("fm", [128, 28], F32)
    maskb = asb("maskb", [128, 512], F32)

    with \
         nc.psum_tensor(p.un("psA"), [128, TLOC], F32) as psA, \
         nc.psum_tensor(p.un("psB"), [128, TLOC], F32) as psB, \
         nc.psum_tensor(p.un("psC"), [128, TLOC], F32) as psC:

        ptb_ctx = contextlib.ExitStack()
        ptb = [ptb_ctx.enter_context(nc.psum_tensor(f"ptb{i}", [128, 128], BF16))
               for i in range(2)]

        sc_ps = [psA, psB]
        pv_ps = psC[:, 0:128]

        def ld0(sync):
            p.inc(sync.dma_start(out=sidb[:], in_=identb), "ld", 16)
            p.inc(sync.dma_start(
                out=maskb[:].rearrange("pp (r c) -> pp r c", c=128),
                in_=maskF.rearrange("(r pp) c -> pp r c", pp=128)), "ld", 16)
        block.sync(ld0)
        ld_done = p.c("ld")

        def v0(v):
            p.wait(v, "ld", ld_done)
            p.inc(v.drain(), "cst")
        block.vector(v0)

        for h in range(NH):
            hb = h % 2

            def g_ld(g, h=h, hb=hb):
                if h >= 2:
                    p.wge(g, "hdone", h - 1)
                for rp in range(4):
                    i = g.dma_start(
                        out=kb[hb][:, rp * 1024:(rp + 1) * 1024],
                        in_=kfull[rp * 2048 + h * 128:rp * 2048 + (h + 1) * 128, :])
                    p.inc(i, "kld", 16)
                i = g.dma_start(
                    out=vb[hb][:].rearrange("pp (kt d) -> pp kt d", d=128),
                    in_=vfull.rearrange("(kt pp) (hh d) -> hh pp kt d",
                                        pp=128, d=128)[h])
                p.inc(i, "vld", 16)
                i = g.dma_start(out=qb[hb][:], in_=qf2[h * 128:(h + 1) * 128, :])
                p.inc(i, "qld", 16)
            block.gpsimd(g_ld)

            for ql in range(8):
                lb = ql % 2
                rrs = ([(0, ql * 128, 0)] if ql > 0 else []) + \
                      [(ql * 128, (ql + 1) * 128, 1)]

                def v_fz(v, h=h, ql=ql):
                    p.wait(v, "resc")
                    v.memset(fm[:, 0:8], 1e30)
                    v.memset(fm[:, 8:16], 0.0)
                    p.inc(v.drain(), "fz")
                block.vector(v_fz)

                for rp in range(4):
                    def t_sc(t, h=h, hb=hb, ql=ql, rp=rp):
                        if ql == 0 and rp == 0:
                            p.wge(t, "kld", (h + 1) * 64)
                            p.wge(t, "qld", (h + 1) * 16)
                            p.wait(t, "cst")
                            if h >= 1:
                                p.wait(t, "aoev")
                        if rp == 0:
                            p.wait(t, "tpmm")
                        p.wait(t, "expd", max(0, p.c("expd") - 1))
                        hi = (ql + 1) * 128
                        last = None
                        for n2 in range((hi + 511) // 512):
                            w = min(512, hi - n2 * 512)
                            last = t.matmul(
                                out=sc_ps[rp % 2][:, n2 * 512:n2 * 512 + w],
                                lhsT=qb[hb][:, ql * 128:(ql + 1) * 128],
                                rhs=kb[hb][:, rp * 1024 + n2 * 512:rp * 1024 + n2 * 512 + w],
                                start=True, stop=True, skip_group_check=True)
                        p.inc(last, "scmm")
                    block.tensor(t_sc)

                    def v_dg(v, ql=ql, rp=rp):
                        p.wait(v, "scmm")
                        p.wait(v, "fz")
                        i = v.tensor_tensor(
                            out=sc_ps[rp % 2][:, ql * 128:(ql + 1) * 128],
                            in0=sc_ps[rp % 2][:, ql * 128:(ql + 1) * 128],
                            in1=maskb[:, rp * 128:(rp + 1) * 128], op=ALU.add)
                        p.inc(i, "dmask")
                    block.vector(v_dg)

                    for ri, (lo, hi, isd) in enumerate(rrs):
                        ci = rp * 2 + ri

                        def v_mx(v, rp=rp, ri=ri, lo=lo, hi=hi, ci=ci):
                            p.wait(v, "dmask")
                            i = v.reduce_max(out=fm[:, ci:ci + 1],
                                             in_=sc_ps[rp % 2][:, lo:hi],
                                             axis=AX, negate=True)
                            p.inc(i, "mx")
                        block.vector(v_mx)

                        def s_exp(s, ql=ql, rp=rp, ri=ri, lo=lo, hi=hi, ci=ci):
                            p.wait(s, "mx")
                            i = s.activation(
                                out=ptile[ql % 2][:, rp * 1024 + lo:rp * 1024 + hi],
                                in_=sc_ps[rp % 2][:, lo:hi], func=AF.Exp,
                                bias=fm[:, ci:ci + 1],
                                accum_out=fm[:, 8 + ci:9 + ci])
                            p.inc(i, "expd")
                        block.scalar(s_exp)

                def v_row(v, h=h, ql=ql):
                    p.wait(v, "expd")
                    i = v.tensor_reduce(out=fm[:, 24:25], in_=fm[:, 0:8], axis=AX,
                                        op=ALU.min)
                    p.inc(i, "mn")
                block.vector(v_row)

                def s_f(s, h=h, ql=ql):
                    p.wait(s, "mn")
                    i = s.activation(out=fm[:, 16:24], in_=fm[:, 0:8], func=AF.Exp,
                                     scale=-1.0, bias=fm[:, 24:25])
                    p.inc(i, "fe")
                block.scalar(s_f)

                def v_z(v, h=h, ql=ql, rrs=rrs):
                    p.wait(v, "fe")
                    v.tensor_tensor(out=fm[:, 8:16], in0=fm[:, 8:16], in1=fm[:, 16:24],
                                    op=ALU.mult)
                    v.drain()
                    v.reduce_sum(out=fm[:, 25:26], in_=fm[:, 8:16], axis=AX)
                    v.drain()
                    v.reciprocal(out=fm[:, 26:27], in_=fm[:, 25:26])
                    v.drain()
                    v.tensor_scalar(out=fm[:, 16:24], in0=fm[:, 16:24],
                                    scalar1=fm[:, 26:27], scalar2=None, op0=ALU.mult)
                    v.drain()
                    i = None
                    for rp in range(4):
                        for ri, (lo, hi, isd) in enumerate(rrs):
                            ci = rp * 2 + ri
                            sl = slice(rp * 1024 + lo, rp * 1024 + hi)
                            i = v.tensor_scalar(out=ptile[ql % 2][:, sl],
                                                in0=ptile[ql % 2][:, sl],
                                                scalar1=fm[:, 16 + ci:17 + ci],
                                                scalar2=None, op0=ALU.mult)
                    v.drain()
                    p.inc(i, "resc")
                block.vector(v_z)

                kts = [rp * 8 + kk for rp in range(4) for kk in range(ql + 1)]
                evval = {}
                for idx in range(len(kts) + 1):
                    if idx < len(kts):
                        kt = kts[idx]

                        def t_tp(t, idx=idx, kt=kt, h=h, ql=ql):
                            if idx == 0:
                                p.wait(t, "resc")
                                if not (h == 0 and ql == 0):
                                    p.wait(t, "aoev")
                            if idx >= 2:
                                p.wge(t, "ptevV" if idx % 2 == 0 else "ptevS", evval[idx - 2])
                            i = t.transpose(out=ptb[idx % 2][:],
                                            in_=ptile[ql % 2][:, kt * 128:(kt + 1) * 128],
                                            identity=sidb[:])
                            p.inc(i, "tpmm")
                        block.tensor(t_tp)

                        if idx % 2 == 0:
                            def v_pe(v, idx=idx):
                                p.wait(v, "tpmm")
                                evval[idx] = p.inc(v.tensor_copy(
                                    out=pts[idx % 2][:], in_=ptb[idx % 2][:]),
                                    "ptevV")
                            block.vector(v_pe)
                        else:
                            def s_pe(s, idx=idx):
                                p.wait(s, "tpmm")
                                evval[idx] = p.inc(s.copy(
                                    out=pts[idx % 2][:], in_=ptb[idx % 2][:]),
                                    "ptevS")
                            block.scalar(s_pe)

                    if idx >= 1:
                        kk = idx - 1
                        kt = kts[kk]

                        def t_pv(t, idx=idx, kk=kk, kt=kt, h=h, hb=hb, last=(idx == len(kts))):
                            p.wge(t, "ptevV" if kk % 2 == 0 else "ptevS", evval[kk])
                            i = t.matmul(out=pv_ps,
                                         lhsT=vb[hb][:, kt * 128:(kt + 1) * 128],
                                         rhs=pts[kk % 2][:],
                                         start=(kk == 0), stop=last,
                                         skip_group_check=True)
                            if last:
                                p.inc(i, "pvdn")
                        block.tensor(t_pv)

                def v_ao(v, h=h, ql=ql):
                    p.wait(v, "pvdn")
                    i = v.tensor_copy(
                        out=ao2[:, h * TLOC + ql * 128:h * TLOC + (ql + 1) * 128],
                        in_=pv_ps)
                    p.inc(i, "aoev")
                block.vector(v_ao)

            def g_hdone(g_, h=h):
                p.wge(g_, "aoev", (h + 1) * 8)
                p.inc(g_.engine_nop(), "hdone")
            block.gpsimd(g_hdone)

        attn_ctx.close()
        attnb = p.sb("attnb", [128, NH * TLOC], BF16)
        wot = [p.sb(f"wot{i}", [128, 256], BF16) for i in range(2)]
        wo_blocks = [(j * 128, 128, ocg * 256, 256)
                     for ocg in range(8) for j in range(NCH)]
        ld_wo = emit_weight_stream(
            p, block, "wo2", WoT2, wo_blocks, lambda k: wot[k % 2][:],
            tile_free_sem="wo2use", tile_depth=2)

        def t_wo(t):
            p.wait(t, "aoev")
            for ocg in range(8):
                if ocg >= 1:
                    p.wge(t, "woev", ocg * 2)
                for j in range(NCH):
                    k = ocg * NCH + j
                    p.wge(t, ld_wo, (k + 1) * 16)
                    last = None
                    for oo in range(2):
                        for tc in range(2):
                            last = t.matmul(
                                out=sc_ps[oo][:, tc * 512:(tc + 1) * 512],
                                lhsT=wot[k % 2][:, oo * 128:(oo + 1) * 128],
                                rhs=ao2[:, j * TLOC + tc * 512:j * TLOC + (tc + 1) * 512],
                                start=(j == 0), stop=(j == NCH - 1),
                                skip_group_check=True)
                    p.inc(last, "wo2use")
        block.tensor(t_wo)

        def s_wo(s):
            for ocg in range(8):
                p.wge(s, "wo2use", (ocg + 1) * NCH)
                for oo in range(2):
                    oc = ocg * 2 + oo
                    i = s.copy(out=attnb[:, oc * TLOC:(oc + 1) * TLOC],
                               in_=sc_ps[oo][:])
                    p.inc(i, "woev")
        block.scalar(s_wo)

        abf = p.sb("abf", [128, 16 * 16], BF16)
        bbf = [p.sb(f"bbf{i}", [16, 512], BF16) for i in range(2)]
        mod = p.sb("mod", [16, TLOC], F32)
        hA = p.sb("hA", [16, TLOC], F32)
        hmodb = p.sb("hmodb", [16, TLOC], BF16)

        def g_ab(g):
            i = g.dma_start(out=abf[:].rearrange("pp (j c) -> pp j c", c=16),
                            in_=Amat.rearrange("(j pp) c -> pp j c", pp=128))
            p.inc(i, "abld", 16)
            p.inc(g.engine_nop(), "abld", 16)
        block.gpsimd(g_ab)

        def t_mod(t):
            p.wait(t, "woev")
            p.wait(t, "abld", 32)
            for tc in range(2):
                i = None
                for j in range(NCH):
                    i = t.matmul(out=psC[0:16, tc * 512:(tc + 1) * 512],
                                 lhsT=abf[:, j * 16:(j + 1) * 16],
                                 rhs=attnb[:, j * TLOC + tc * 512:j * TLOC + (tc + 1) * 512],
                                 start=(j == 0), stop=(j == NCH - 1),
                                 skip_group_check=True)
                p.inc(i, "modm")
        block.tensor(t_mod)

        def s_mod(s):
            p.wait(s, "modm")
            p.inc(s.activation(out=mod[:], in_=psC[0:16, 0:TLOC], func=AF.Tanh),
                  "modt")
        block.scalar(s_mod)

        def v_mod(v):
            p.wait(v, "modt")
            p.inc(v.tensor_scalar(out=mod[:], in0=mod[:], scalar1=0.1, scalar2=None,
                                  op0=ALU.mult), "modev")
        block.vector(v_mod)

        h2bf = p.sb("h2bf", [128, NCH * TLOC], BF16)
        xtile = [p.sb(f"xtile{i}", [128, E], F32) for i in range(2)]
        lnsq = p.sb("lnsq", [128, E], F32)
        xbt = p.sb("xbt", [128, E], BF16)
        qcol = p.sb("qcol", [128, 4], F32)
        wcol = p.sb("wcol", [128, 16], F32)

        def ldw2(sync):
            p.inc(sync.dma_start(out=wcol[:], in_=ln2w), "w2ld", 16)
        block.sync(ldw2)

        emit_rms_fm(p, block, "r2", xnt1, wcol, h2bf, ptb, sidb, xtile, lnsq, qcol,
                    xbt, pre_wait=lambda e: p.wge(e, "woev", 16),
                    wcol_sem="w2ld", wcol_val=16)

        def t_hA(t):
            p.wait(t, "r2fe")
            p.wait(t, "modt")
            for tc in range(2):
                i = None
                for j in range(NCH):
                    i = t.matmul(out=psC[0:16, tc * 512:(tc + 1) * 512],
                                 lhsT=abf[:, j * 16:(j + 1) * 16],
                                 rhs=h2bf[:, j * TLOC + tc * 512:j * TLOC + (tc + 1) * 512],
                                 start=(j == 0), stop=(j == NCH - 1),
                                 skip_group_check=True)
                p.inc(i, "hAm")
        block.tensor(t_hA)

        def s_hA(s):
            p.wait(s, "hAm")
            p.inc(s.copy(out=hA[:], in_=psC[0:16, 0:TLOC]), "hAe")
        block.scalar(s_hA)

        def v_hm(v):
            p.wait(v, "hAe")
            p.wait(v, "modev")
            v.tensor_tensor(out=hA[:], in0=hA[:], in1=mod[:], op=ALU.mult)
            v.drain()
            p.inc(v.tensor_copy(out=hmodb[:], in_=hA[:]), "hme")
        block.vector(v_hm)

        wis = p.sb("wis", [128, NCH * 512], BF16)
        gstg = [p.sb(f"gstg{i}", [128, 512], F32) for i in range(8)]
        ustg = [p.sb(f"ustg{i}", [128, 512], F32) for i in range(2)]
        gub = p.sb("gub", [128, 512], BF16)
        guf_og = p.sb("guf_og", [128, 4 * TLOC], BF16)

        for og in range(16):
            def g_win(g, og=og):
                p.wait(g, "hme")
                if og >= 1:
                    p.wge(g, "gmm", og * 8)
                    p.wge(g, "umm", og * 8)
                for j in range(NCH):
                    i = g.dma_start(out=wis[:, j * 512:(j + 1) * 512],
                                    in_=WinT[j * 128:(j + 1) * 128,
                                             og * 512:(og + 1) * 512])
                    p.inc(i, "wgld", 16)
                i = g.dma_start(out=bbf[og % 2][:],
                                in_=Bmat[0:16, og * 512:(og + 1) * 512])
                p.inc(i, "bgld", 16)
            block.gpsimd(g_win)

            for tt in range(8):
                def t_g(t, og=og, tt=tt):
                    k = og * 8 + tt
                    p.wge(t, "wgld", (og * 2) * NCH * 16 + NCH * 16)
                    if tt == 0:
                        p.wait(t, "hAe")
                        p.wait(t, "hme")
                        if og == 0:
                            p.wait(t, "woev")
                    p.wait(t, "gev", max(0, p.c("gev") - 1))
                    last = None
                    for j in range(NCH):
                        last = t.matmul(
                            out=psA[:, (k % 2) * 512:(k % 2 + 1) * 512],
                            lhsT=h2bf[:, j * TLOC + tt * 128:j * TLOC + (tt + 1) * 128],
                            rhs=wis[:, j * 512:(j + 1) * 512],
                            start=(j == 0), stop=False, skip_group_check=True)
                    if tt == 0:
                        p.wge(t, "bgld", (og + 1) * 16)
                    last = t.matmul(
                        out=psA[:, (k % 2) * 512:(k % 2 + 1) * 512],
                        lhsT=hmodb[0:16, tt * 128:(tt + 1) * 128],
                        rhs=bbf[og % 2][0:16, :],
                        start=False, stop=True, skip_group_check=True)
                    p.inc(last, "gmm")
                block.tensor(t_g)

                def s_g(s, og=og, tt=tt):
                    k = og * 8 + tt
                    p.wait(s, "gmm")
                    if og >= 1:
                        p.wge(s, "gu", (og - 1) * 8 + tt + 1)
                    i = s.activation(out=gstg[tt][:],
                                     in_=psA[:, (k % 2) * 512:(k % 2 + 1) * 512],
                                     func=AF.Silu)
                    p.inc(i, "gev")
                block.scalar(s_g)

            def g_wup(g, og=og):
                p.wge(g, "gmm", (og + 1) * 8)
                for j in range(NCH):
                    i = g.dma_start(out=wis[:, j * 512:(j + 1) * 512],
                                    in_=WupT[j * 128:(j + 1) * 128,
                                             og * 512:(og + 1) * 512])
                    p.inc(i, "wgld", 16)
            block.gpsimd(g_wup)

            for tt in range(8):
                def t_u(t, og=og, tt=tt):
                    k = og * 8 + tt
                    p.wge(t, "wgld", (og * 2 + 2) * NCH * 16)
                    p.wait(t, "uev", max(0, p.c("uev") - 1))
                    last = None
                    for j in range(NCH):
                        last = t.matmul(
                            out=psB[:, (k % 2) * 512:(k % 2 + 1) * 512],
                            lhsT=h2bf[:, j * TLOC + tt * 128:j * TLOC + (tt + 1) * 128],
                            rhs=wis[:, j * 512:(j + 1) * 512],
                            start=(j == 0), stop=(j == NCH - 1),
                            skip_group_check=True)
                    p.inc(last, "umm")
                block.tensor(t_u)

                def s_u(s, og=og, tt=tt):
                    k = og * 8 + tt
                    p.wait(s, "umm")
                    if k >= 2:
                        p.wait(s, "gu", max(0, p.c("gu") - 1))
                    i = s.copy(out=ustg[k % 2][:],
                               in_=psB[:, (k % 2) * 512:(k % 2 + 1) * 512])
                    p.inc(i, "uev")
                block.scalar(s_u)

                def v_gu(v, og=og, tt=tt):
                    k = og * 8 + tt
                    p.wait(v, "gev", k + 1)
                    p.wait(v, "uev", k + 1)
                    if k >= 1:
                        p.wait(v, "gut")
                    i = v.tensor_tensor(out=gub[:], in0=gstg[tt][:],
                                        in1=ustg[k % 2][:], op=ALU.mult)
                    p.inc(i, "gu")
                block.vector(v_gu)

                for cc in range(4):
                    def t_gt(t, og=og, tt=tt, cc=cc):
                        k = (og * 8 + tt) * 4 + cc
                        p.wait(t, "gu")
                        if tt == 0 and cc == 0:
                            p.wge(t, "gust", og * 16)
                        p.wge(t, "gue", max(0, p.c("gue") - 1))
                        i = t.transpose(out=ptb[k % 2][:],
                                        in_=gub[:, cc * 128:(cc + 1) * 128],
                                        identity=sidb[:])
                        p.inc(i, "gut")
                    block.tensor(t_gt)

                    def v_gte(v, og=og, tt=tt, cc=cc):
                        p.wait(v, "gut")
                        i = v.tensor_copy(
                            out=guf_og[:, cc * TLOC + tt * 128:cc * TLOC + (tt + 1) * 128],
                            in_=ptb[((og * 8 + tt) * 4 + cc) % 2][:])
                        p.inc(i, "gue")
                    block.vector(v_gte)

            def st_gu(sync, og=og):
                p.wait(sync, "gue")
                i = sync.dma_start(
                    out=guF.ap().rearrange("(oo cc pp) t -> oo pp cc t",
                                           pp=128, cc=4)[og],
                    in_=guf_og[:].rearrange("pp (cc t) -> pp cc t", cc=4))
                p.inc(i, "gust", 16)
            block.sync(st_gu)
            def v_gow(v, og=og):
                p.wge(v, "gust", (og + 1) * 16)
                p.inc(v.engine_nop(), "gow")
            block.vector(v_gow)

        ptb_ctx.close()

        with nc.psum_tensor(p.un("psD"), [128, TLOC], F32) as psD:
            pool = [psA, psB, psC, psD]

            def ps8(tt):
                return pool[tt // 2][:, (tt % 2) * 512:(tt % 2 + 1) * 512]

            wdnb = [p.sb(f"wdnb{i}", [128, 512], BF16) for i in range(2)]
            gfc = [p.sb(f"gfc{i}", [128, TLOC], BF16) for i in range(2)]
            mstg = [p.sb(f"mstg{i}", [128, 512], F32) for i in range(2)]
            xnb = [p.sb(f"xnb{i}", [128, 512], F32) for i in range(2)]
            x1b = [p.sb(f"x1b{i}", [128, 512], F32) for i in range(2)]
            s1b = [p.sb(f"s1b{i}", [128, 512], F32) for i in range(2)]

            for oc in range(4):
                for c2 in range(64):
                    def g_wd(g, oc=oc, c2=c2):
                        k = oc * 64 + c2
                        if k == 0:
                            p.wait(g, "gust")
                        if k >= 2:
                            p.wge(g, "dmm", k - 1)
                        i = g.dma_start(
                            out=wdnb[k % 2][:],
                            in_=WdnT[c2 * 128:(c2 + 1) * 128,
                                     oc * 512:(oc + 1) * 512])
                        p.inc(i, "wdld", 16)
                    block.gpsimd(g_wd)

                    def s_gf(sync, oc=oc, c2=c2):
                        k = oc * 64 + c2
                        if k == 0:
                            p.wait(sync, "gust")
                        if k >= 2:
                            p.wge(sync, "dmm", k - 1)
                        i = sync.dma_start(out=gfc[k % 2][:],
                                           in_=guF.ap()[c2 * 128:(c2 + 1) * 128, :])
                        p.inc(i, "gfld", 16)
                    block.sync(s_gf)

                    def t_d(t, oc=oc, c2=c2):
                        k = oc * 64 + c2
                        p.wge(t, "wdld", (k + 1) * 16)
                        p.wge(t, "gfld", (k + 1) * 16)
                        if c2 == 0 and oc >= 1:
                            p.wait(t, "mev")
                        if c2 == 0 and oc == 0:
                            p.wait(t, "gue")
                        last = None
                        for tt in range(8):
                            last = t.matmul(
                                out=ps8(tt),
                                lhsT=gfc[k % 2][:, tt * 128:(tt + 1) * 128],
                                rhs=wdnb[k % 2][:],
                                start=(c2 == 0), stop=(c2 == 63),
                                skip_group_check=True)
                        p.inc(last, "dmm")
                    block.tensor(t_d)

                for tt in range(8):
                    k = oc * 8 + tt

                    def ldres(sync, oc=oc, tt=tt, k=k):
                        if k >= 2:
                            p.wge(sync, "upd", k - 1)
                        i = sync.dma_start(
                            out=xnb[k % 2][:],
                            in_=xnt1[tt * 128:(tt + 1) * 128,
                                     oc * 512:(oc + 1) * 512])
                        p.inc(i, "resld", 16)
                        i = sync.dma_start(
                            out=x1b[k % 2][:],
                            in_=x1t[tt * 128:(tt + 1) * 128,
                                    oc * 512:(oc + 1) * 512])
                        p.inc(i, "resld", 16)
                        i = sync.dma_start(
                            out=s1b[k % 2][:],
                            in_=s1[tt * 128:(tt + 1) * 128,
                                   oc * 512:(oc + 1) * 512])
                        p.inc(i, "resld", 16)
                    block.sync(ldres)

                    def s_m(s, oc=oc, tt=tt, k=k):
                        p.wait(s, "dmm")
                        if k >= 2:
                            p.wge(s, "x2st", (k - 1) * 16)
                        i = s.copy(out=mstg[k % 2][:], in_=ps8(tt))
                        p.inc(i, "mev")
                    block.scalar(s_m)

                    def v_r(v, oc=oc, tt=tt, k=k):
                        p.wait(v, "mev")
                        p.wge(v, "resld", (k + 1) * 48)
                        i = v.tensor_tensor(out=mstg[k % 2][:], in0=mstg[k % 2][:],
                                            in1=xnb[k % 2][:], op=ALU.add)
                        p.inc(i, "radd")
                    block.vector(v_r)

                    def s_sil(s, oc=oc, tt=tt, k=k):
                        p.wait(s, "radd")
                        i = s.activation(out=mstg[k % 2][:], in_=mstg[k % 2][:],
                                         func=AF.Silu)
                        p.inc(i, "gsil")
                    block.scalar(s_sil)

                    def v_u(v, oc=oc, tt=tt, k=k):
                        p.wait(v, "gsil")
                        v.tensor_tensor(out=mstg[k % 2][:], in0=mstg[k % 2][:],
                                        in1=s1b[k % 2][:], op=ALU.mult)
                        v.drain()
                        i = v.tensor_tensor(out=mstg[k % 2][:], in0=mstg[k % 2][:],
                                            in1=x1b[k % 2][:], op=ALU.add)
                        p.inc(i, "upd")
                    block.vector(v_u)

                    def st_x(sync, oc=oc, tt=tt, k=k):
                        p.wait(sync, "upd")
                        i = sync.dma_start(
                            out=o_x2[tt * 128:(tt + 1) * 128,
                                     oc * 512:(oc + 1) * 512],
                            in_=mstg[k % 2][:])
                        p.inc(i, "x2st", 16)
                    block.sync(st_x)

            def fin(sync):
                p.wait(sync, "x2st")
            block.sync(fin)




EXT_IN = {
    "x0f": ([E, TLOC], F32), "x0t": ([TLOC, E], F32),
    "Wattn": ([E, 6144], F32), "Wproj": ([E, E], F32),
    "w1T": ([E, 8192], F32), "w2T": ([8192, E], F32),
    "b1row": ([1, 8192], F32), "lngrow": ([1, 8192], F32),
    "lnbrow": ([1, 8192], F32), "b2row": ([1, E], F32),
    "l0lambda_q1": ([128, 16], F32), "l0lambda_k1": ([128, 16], F32),
    "l0lambda_q2": ([128, 16], F32), "l0lambda_k2": ([128, 16], F32),
    "wqkvT": ([E, 6144], F32), "woT": ([E, E], F32),
    "l1lambda_q1": ([1, 128], F32), "l1lambda_q2": ([1, 128], F32),
    "l1lambda_k1": ([1, 128], F32), "l1lambda_k2": ([1, 128], F32),
    "gamrow": ([1, TLOC], F32), "betrow": ([1, TLOC], F32),
    "WqT": ([E, E], F32), "WkT": ([E, E], F32), "WvT": ([E, E], F32),
    "WoT2": ([E, E], F32), "Amat": ([E, 16], F32), "Bmat": ([16, 8192], F32),
    "WinT": ([E, 8192], F32), "WupT": ([E, 8192], F32), "WdnT": ([8192, E], F32),
    "ln1w": ([128, 16], F32), "ln2w": ([128, 16], F32),
    "identf": ([128, 128], F32), "identb": ([128, 128], BF16),
    "E16": ([16, TLOC], F32), "cosfm": ([128, TLOC], F32),
    "sinfm": ([128, TLOC], F32), "maskA": ([512, 128], F32),
    "maskF": ([512, 128], F32),
}

HANDOFF = {
    "xnt": ([TLOC, E], F32), "qp": ([E, TLOC], F32), "kp": ([E, TLOC], F32),
    "vt": ([TLOC, E], F32), "kfull": ([4 * E, TLOC], F32),
    "vfull": ([4096, E], F32), "s0": ([TLOC, E], F32),
    "hpre": ([TLOC, 8192], F32), "xq2": ([TLOC, 8192], I8),
    "inv2": ([TLOC, 1], F32), "x1t": ([TLOC, E], F32),
    "xnt1": ([TLOC, E], F32), "qr": ([E, TLOC], F32), "kr": ([E, TLOC], F32),
    "vt1": ([TLOC, E], F32), "kfull1": ([4 * E, TLOC], F32),
    "vfull1": ([4096, E], F32), "y": ([E, TLOC], F32), "part": ([1, 32], F32),
    "part_all": ([4, 32], F32), "s1": ([TLOC, E], F32),
    "qf2": ([E, TLOC], F32), "kf2": ([E, TLOC], F32), "vt2": ([TLOC, E], F32),
    "kfull2": ([4 * E, TLOC], F32), "vfull2": ([4096, E], F32),
    "guF": ([8192, TLOC], BF16), "x2t": ([TLOC, E], F32),
}

STAGES = [
    (emit_l0pre, ["x0f", "Wattn", "l0lambda_q1", "l0lambda_k1", "l0lambda_q2",
                  "l0lambda_k2", "identf"], ["xnt", "qp", "kp", "vt"]),
    (emit_l0attn, ["qp", "kfull", "vfull", "Wproj", "identb"], ["s0"]),
    (emit_l0ffn_a, ["xnt", "w1T", "b1row", "identb"], ["hpre"]),
    (emit_l0ffn_b, ["hpre", "lngrow", "lnbrow"], ["xq2", "inv2"]),
    (emit_l0ffn_c, ["xq2", "inv2", "w2T", "b2row", "x0t", "s0", "identb"],
     ["x1t"]),
    (emit_l1pre, ["x1t", "wqkvT", "cosfm", "sinfm", "identb"],
     ["xnt1", "qr", "kr", "vt1"]),
    (emit_l1attn_a, ["qr", "kfull1", "vfull1", "l1lambda_q1", "l1lambda_k1",
                     "l1lambda_q2", "l1lambda_k2", "maskA", "identb", "identf"],
     ["y", "part"]),
    (emit_l1attn_b, ["y", "part_all", "gamrow", "betrow", "woT", "E16",
                     "identf"], ["s1"]),
    (emit_l1fpre, ["xnt1", "ln1w", "WqT", "WkT", "WvT", "identb"],
     ["qf2", "kf2", "vt2"]),
    (emit_l1ffn, ["qf2", "kfull2", "vfull2", "maskF", "WoT2", "Amat", "Bmat",
                  "WinT", "WupT", "WdnT", "ln2w", "xnt1", "x1t", "s1",
                  "identb"], ["x2t", "guF"]),
]


def build_stage(idx):
    emit, ins, outs = STAGES[idx]
    p = Prog()
    nc = p.nc
    io = {}
    for n in ins:
        shape, dt = EXT_IN.get(n, HANDOFF.get(n))
        io[n] = p.din(n, shape, dt)
    for n in outs:
        shape, dt = HANDOFF[n]
        if n == "guF":
            io[n] = p.dram(n, shape, dt)
        else:
            io[n] = p.dout(n, shape, dt)
    with nc.Block() as block:
        emit(p, block, io)
    return p


def build_fused(nstages=len(STAGES) if True else 10):
    p = Prog()
    nc = p.nc
    io = {}
    handles = {}
    for n, (shape, dt) in EXT_IN.items():
        io[n] = p.din(n, shape, dt)
    for n, (shape, dt) in HANDOFF.items():
        if n == "x2t":
            io[n] = p.dout(n, shape, dt)
        elif n == "guF":
            io[n] = p.dram(n, shape, dt)
            handles[n] = io[n]
        else:
            h = nc.dram_tensor(n, list(shape), dt)
            handles[n] = h
            io[n] = h.ap()

    gathers = {1: [("kp", "kfull"), ("vt", "vfull")],
               6: [("kr", "kfull1"), ("vt1", "vfull1")],
               7: [("part", "part_all")],
               9: [("kf2", "kfull2"), ("vt2", "vfull2")]}

    marks = []
    with nc.Block() as block:
        for si, (emit, ins, outs) in enumerate(STAGES[:nstages]):
            if si in gathers:
                def g_cc(g, pairs=gathers[si]):
                    for a, b in pairs:
                        i = g.collective_compute(
                            "AllGather", ALU.bypass,
                            replica_groups=[[0, 1, 2, 3], [4, 5, 6, 7]],
                            ins=[handles[a].ap().opt()],
                            outs=[handles[b].ap().opt()])
                        p.inc(i, "cc", 1)
                    p.wait(g, "cc")
                block.gpsimd(g_cc)
                p.barrier(block)
            marks.append(len(nc.main_func.blocks))
            p.new_stage()
            p.push_scope()
            emit(p, block, io)
            p.pop_scope()
            p.barrier(block)
            marks.append(len(nc.main_func.blocks))
    _reorder_blocks(nc, marks)
    return p


def _reorder_blocks(nc, marks):
    mf = nc.main_func
    blocks = list(mf.blocks)
    order = {"EngineType.SP": 0, "EngineType.PE": 1, "EngineType.Pool": 2,
             "EngineType.DVE": 3, "EngineType.Activation": 4}

    def beng(b):
        for i in b.instructions:
            return str(i.engine)
        return "?"

    out = list(blocks)
    for k in range(0, len(marks) - 1, 2):
        si = k // 2
        if REORDER_SET is not None and si not in REORDER_SET:
            continue
        b0, b1 = marks[k], marks[k + 1]
        if si in REORDER_LIMIT:
            b1 = min(b1, b0 + REORDER_LIMIT[si])
        b0 += REORDER_SKIP.get(si, 0)
        b0 = min(b0, b1)
        for w0 in range(b0, b1, REORDER_WIN):
            w1 = min(w0 + REORDER_WIN, b1)
            seg = blocks[w0:w1]
            if REORDER_ACT_ONLY:
                out[w0:w1] = sorted(
                    seg,
                    key=lambda b: 1 if beng(b) == "EngineType.Activation" else 0)
            else:
                out[w0:w1] = sorted(seg, key=lambda b: order.get(beng(b), 9))
    mf.blocks = out


REORDER_SET = {0, 1, 2, 3, 4, 5, 8}
REORDER_LIMIT = {}
REORDER_ACT_ONLY = False
REORDER_WIN = 10 ** 9
REORDER_SKIP = {}



def _in_maps(pr, x):
    maps = []
    for c in range(NCORES):
        m = {}
        for n in EXT_IN:
            v = pr["core"][c].get(n)
            if v is None:
                v = pr["shared"][n]
            m[n] = v
        maps.append(m)
    return maps


_FUSED_CACHE = {}


def kernel(**inputs):
    pr = _prep(inputs)
    x = np.asarray(inputs["L0_attn_x"])
    for c in range(NCORES):
        b, r = c // 4, c % 4
        pr["core"][c]["x0t"] = np.ascontiguousarray(x[b, r::4, :])
    maps = _in_maps(pr, x)
    p = build_fused()
    res = run_bass_kernel_spmd(p.nc, maps, core_ids=list(range(NCORES))).results
    out = np.empty((2, 4096, E), dtype=res[0]["x2t"].dtype)
    for c in range(NCORES):
        out[c // 4, c % 4::4, :] = res[c]["x2t"]
    return out
